# version 01
# baseline (speedup 1.0000x reference)
import numpy as np
import concourse.bass as bass
import concourse.mybir as mybir
from concourse.bass_utils import run_bass_kernel_spmd

F32 = mybir.dt.float32
BF16 = mybir.dt.bfloat16
AF = mybir.ActivationFunctionType
ALU = mybir.AluOpType
AX = mybir.AxisListType.X

D = 1024
KC = 8
T = 4096
TOWN = 2048
HN = 8
DK = 64
MF = 256
HID = 2816
JCN = 22
NEXP = 8
LNEPS = 1e-5
LN16 = float(np.log(16.0))
LNS128 = 0.5 * float(np.log(128.0))
EPSB = 1e-6
KB = 1024

DEBUG_STOP = None


def APbc(handle, off, p, n):
    return bass.AP(handle.ap().tensor, off, [[0, p], [1, n]])


def APflat(handle, off, n):
    return bass.AP(handle.ap().tensor, off, [[0, 1], [1, n]])


class Ctx:

    def __init__(self, nc, eng, name, sems):
        self.nc, self.e, self.name, self.sems = nc, eng, name, sems
        self.cnt = {k: 0 for k in sems}

    def c(self, s):
        return self.cnt[s]

    def op(self, eng, fn, wait=(), inc=()):
        for s, n in inc:
            self.cnt[s] += n
        if eng != self.name:
            return
        for s, v in wait:
            if v > 0:
                self.e.wait_ge(self.sems[s], v)
        if fn is None:
            return
        ins = fn(self.e)
        for s, n in inc:
            ins.then_inc(self.sems[s], n)

    def dma(self, q, out, in_, wait=(), sem=None, **kw):
        assert sem is not None
        self.op(q, lambda e: e.dma_start(out=out, in_=in_, **kw),
                wait=wait, inc=((sem, 16),))
        return self.cnt[sem]

    def pe(self, fn, wait=(), inc_n=0):
        self.op("tensor", fn, wait=wait, inc=((("pe", inc_n),) if inc_n else ()))
        return self.cnt["pe"]

    def dve(self, fn, wait=(), inc_n=0):
        w = tuple(wait) + (("dve", self.cnt["dve"]),)
        self.op("vector", fn, wait=w, inc=(("dve", 1),))
        return self.cnt["dve"]

    def act(self, fn, wait=(), inc_n=1):
        w = tuple(wait) + (("act", self.cnt["act"]),)
        self.op("scalar", fn, wait=w, inc=(("act", 1),))
        return self.cnt["act"]

    def pool(self, fn, wait=(), inc_n=1):
        w = tuple(wait) + (("pool", self.cnt["pool"]),)
        self.op("gpsimd", fn, wait=w, inc=(("pool", 1),))
        return self.cnt["pool"]


def build_program():
    nc = bass.Bass("TRN2", target_bir_lowering=False)

    def din(name, shape, dt=F32):
        return nc.dram_tensor(name, list(shape), dt, kind="ExternalInput")

    h = {}
    h["xT"] = din("xT", [KC, 128, TOWN])
    h["win"] = din("win", [KC, 128, 4104])
    h["bqkp"] = din("bqkp", [128, 3 * HN])
    h["bvrow"] = din("bvrow", [HN, 128])
    h["byrow"] = din("byrow", [1, HN])
    h["lnw"] = din("lnw", [128, KC]); h["lnb"] = din("lnb", [128, KC])
    h["olnw"] = din("olnw", [128, KC]); h["olnb"] = din("olnb", [128, KC])
    h["woutT"] = din("woutT", [KC, 128, D])
    h["smear"] = din("smear", [1, HN])
    h["lscale"] = din("lscale", [1, HN])
    h["wqT"] = din("wqT", [KC, 128, 512])
    h["wkT"] = din("wkT", [KC, 128, 512])
    h["wvT"] = din("wvT", [KC, 128, 512])
    h["woT"] = din("woT", [4, 128, D])
    h["bo"] = din("bo", [128, KC])
    h["projT"] = din("projT", [HN, DK, MF])
    h["gates"] = din("gates", [NEXP, KC, 128, HID])
    h["ups"] = din("ups", [NEXP, KC, 128, HID])
    h["downs"] = din("downs", [NEXP, JCN, 128, D])
    h["wr"] = din("wr", [KC, 128, 7])
    h["rbias"] = din("rbias", [1, 7])
    h["ident"] = din("ident", [128, 128])
    h["ones_col16"] = din("ones_col16", [128, 1], BF16)
    h["ones_row16"] = din("ones_row16", [1, 128], BF16)
    h["ones16m"] = din("ones16m", [128, 128], BF16)
    h["ones_col64_16"] = din("ones_col64_16", [64, 1], BF16)
    h["ones_row64"] = din("ones_row64", [1, 64])
    h["ones_col32"] = din("ones_col32", [128, 1])
    h["tri_loc"] = din("tri_loc", [128, 129])
    h["tri_off"] = din("tri_off", [32, 32])
    h["mask1m"] = din("mask1m", [128, 896])
    h["mask2m"] = din("mask2m", [128, 896])

    out_fm = nc.dram_tensor("out_fm", [KC, 128, TOWN], F32, kind="ExternalOutput")

    def dint(name, shape, dt):
        return nc.dram_tensor(name, list(shape), dt, kind="Internal")

    wst = {k: dint("wst_" + k, shp, BF16) for k, shp in [
        ("win", [KC, 128, 4104]), ("woutT", [KC, 128, D]),
        ("wqT", [KC, 128, 512]), ("wkT", [KC, 128, 512]),
        ("wvT", [KC, 128, 512]), ("woT", [4, 128, D]),
        ("projT", [HN, DK, MF]), ("gates", [NEXP, KC, 128, HID]),
        ("ups", [NEXP, KC, 128, HID]), ("downs", [NEXP, JCN, 128, D])]}
    M = dict(nc=nc, h=h, wst=wst, out_fm=out_fm)
    M["ag_in"] = dint("ag_in", [KC, 128, TOWN], BF16)
    M["ag_out"] = dint("ag_out", [2 * KC, 128, TOWN], BF16)
    M["rs_in"] = dint("rs_in", [2, KC, 128, TOWN], F32)
    M["rs_out"] = dint("rs_out", [KC, 128, TOWN], F32)
    M["qst"] = dint("qst", [HN, 128, T], BF16)
    M["kst"] = dint("kst", [HN, 128, T], BF16)
    M["vst"] = dint("vst", [HN, 128, T], BF16)
    M["spst"] = dint("spst", [HN, 128, T], BF16)
    M["gst"] = dint("gst", [HN, 128, T], BF16)
    M["mgst"] = dint("mgst", [4, 128, T], BF16)
    M["rowd"] = dint("rowd", [2, TOWN], F32)
    M["lnrows"] = dint("lnrows", [3, TOWN], F32)
    M["poshd"] = dint("poshd", [HN, T], BF16)
    M["posld"] = dint("posld", [HN, T], BF16)
    M["mrowd"] = dint("mrowd", [32, 128], BF16)
    M["brql"] = dint("brql", [1, T], BF16)
    M["linvd"] = dint("linvd", [HN, T], F32)
    M["wrowd"] = dint("wrowd", [7, TOWN], F32)
    M["prmd"] = dint("prmd", [1, 3 * HN], F32)
    M["groups"] = [[0, 1], [2, 3], [4, 5], [6, 7]]

    sb = {}

    def asb(name, shape, dt=F32):
        sb[name] = nc.alloc_sbuf_tensor("s_" + name, list(shape), dt)
        return sb[name]

    asb("x", [128, KC * TOWN])
    asb("ident", [128, 128])
    asb("ones_col16", [128, 1], BF16)
    asb("ones_row16", [1, 128], BF16)
    asb("ones16m", [128, 128], BF16)
    asb("ones_col64_16", [64, 1], BF16)
    asb("ones_row64", [1, 64])
    asb("ones_col32", [128, 1])
    asb("tri_loc", [128, 129])
    asb("tri_off", [32, 32])
    asb("lnw", [128, KC]); asb("lnb", [128, KC])
    asb("olnw", [128, KC]); asb("olnb", [128, KC])
    asb("bo", [128, KC])
    asb("bqkp", [128, 3 * HN])
    asb("pcol", [128, 3 * HN])
    asb("prows", [1, 64])
    asb("ywin", [128, KC * HN], BF16)
    asb("ybb", [128, HN])
    asb("vbb", [128, 128])
    asb("sigy", [128, 32 * HN])
    asb("pos_t", [128, 32 * HN])
    asb("negm32", [128, 32])
    asb("linv", [128, 32])
    asb("rowpf", [128, 2048])
    asb("lncol", [128, 128])
    asb("rowp16", [128, T], BF16)
    asb("rcols", [128, 16])
    asb("rbb", [128, 7])
    asb("colscr", [128, 24])
    asb("colscr3", [128, 8], BF16)
    asb("cvt32", [128, 1024])
    asb("cvt16", [128, 1024], BF16)
    asb("wt7", [7, 16 * 128])

    ARENA_B = 106 * KB
    arena = nc.alloc_sbuf_tensor("arena", [128, ARENA_B], mybir.dt.uint8)
    abase = nc.lookup_mloc(arena).addr
    ar = {}

    def av(name, off, shape, dt=F32):
        t = nc.alloc_sbuf_tensor_at("a_" + name, list(shape), dt, offset=abase + off)
        nb = int(np.prod(shape[1:])) * mybir.dt.size(dt)
        assert off + nb <= ARENA_B, (name, off, nb)
        ar[name] = t
        return t

    av("xnf", 0, [128, KC * T], BF16)
    av("xno", 0, [128, KC * TOWN], BF16)
    av("lncb", 64 * KB, [128, 2 * TOWN], BF16)
    av("lncq32", 64 * KB, [128, TOWN])
    av("lncq", 72 * KB, [128, 2 * TOWN], BF16)
    av("Rb", 80 * KB, [128, TOWN])
    av("Bb", 88 * KB, [128, TOWN])
    av("lnes", 96 * KB, [128, TOWN])
    av("lnes2", 64 * KB, [128, TOWN])
    av("lnrt", 104 * KB, [48, 128])
    av("posfm", 64 * KB, [8, T])
    av("poshi16", 80 * KB, [8, T], BF16)
    av("poslo16", 88 * KB, [8, T], BF16)
    av("hi32h", 96 * KB, [8, TOWN])
    av("winblk", 64 * KB, [128, 2 * KC * 128], BF16)
    av("stg16", 68 * KB, [128, T], BF16)
    av("ksc", 76 * KB, [128, T], BF16)
    av("qh", 0, [128, 2 * T], BF16)
    av("kh", 16 * KB, [128, 2 * T], BF16)
    av("vh", 32 * KB, [128, 2 * T], BF16)
    av("sph", 48 * KB, [128, 2 * T], BF16)
    av("pts", 64 * KB, [128, 2 * 512], BF16)
    av("pthrow", 66 * KB, [128, 2048], BF16)
    av("esmall", 70 * KB, [128, 512])
    av("gtile", 72 * KB, [128, 512], BF16)
    av("invlb", 73 * KB, [128, 512])
    av("nm32t", 75 * KB, [32, 128])
    av("nm16t", 75 * KB + 512, [32, 128], BF16)
    av("masks1", 76 * KB, [128, 896])
    av("masks2", 76 * KB + 3584, [128, 896])
    av("wout_s", 0, [128, KC * D], BF16)
    av("gld", 16 * KB, [128, HN * 512], BF16)
    av("aotile", 24 * KB, [128, KC * 512])
    av("hbuf", 32 * KB, [128, JCN * 512], BF16)
    av("dhalf", 54 * KB, [128, JCN * 512], BF16)
    av("uring", 76 * KB, [128, 4 * 1024], BF16)
    av("wbb", 84 * KB, [128, 1024])
    av("sg", 88 * KB, [128, 512])
    av("fes", 90 * KB, [128, 512])
    av("ltile", 92 * KB, [128, 8])
    av("ptile", 92 * KB + 64, [128, 8])
    av("wr32", 92 * KB + 128, [128, KC * 7])
    av("gring", 94 * KB, [128, 4 * 1024], BF16)
    av("wqkv3", 64 * KB, [128, 3 * 512], BF16)
    av("wproj", 67 * KB, [128, HN * MF], BF16)
    av("qkh", 71 * KB, [128, T], BF16)
    av("vb_h", 79 * KB, [128, 32 * 65], BF16)
    av("phk16", 83 * KB + 512, [128, MF], BF16)
    av("sqs", 84 * KB, [64, 128])
    av("bes2", 84 * KB + 512, [128, MF])
    av("kvh", 85 * KB + 512, [128, 2 * 65])
    av("kvh16", 86 * KB + 64, [128, 2 * 65], BF16)
    av("phq", 86 * KB + 512, [128, 2 * 512], BF16)
    av("invb", 88 * KB + 512, [64, 512])
    av("mgt", 90 * KB + 512, [128, 512], BF16)
    M["sb"], M["ar"] = sb, ar

    M["psA"] = nc.alloc_psum_tensor("psA", [128, 2048], F32)
    M["psB"] = nc.alloc_psum_tensor("psB", [128, 2048], F32)

    semnames = ["q_p0", "q_prm", "q_ywin", "q_xnf", "q_win0", "q_win1",
                "q_mask", "q_hl0", "q_hl1", "q_rp", "q_nm", "q_gld", "q_wout",
                "q_wo2", "q_wr", "q_wqkv", "q_wproj", "q_mg", "q_dh",
                "q_g0", "q_g1", "q_g2", "q_g3", "q_u0", "q_u1", "q_u2", "q_u3",
                "q_cvi", "q_cvoA", "q_cvoF",
                "q_pcol", "q_ybb", "q_vbb", "q_RB", "q_inv", "q_wbb", "q_rbb",
                "st_prm", "st_lnr", "st_pos", "st_stg", "st_nm", "st_g",
                "st_ao", "st_wrow", "st_mg", "st_out", "st_ag", "q_ao3",
                "pe", "dve", "act", "pool", "cc"]
    sems = {s: nc.alloc_semaphore(name=s) for s in semnames}
    M["sems"] = sems

    with nc.Block() as block:
        @block.sync
        def _(e):
            emit_program(Ctx(nc, e, "sync", sems), M)

        @block.tensor
        def _(e):
            emit_program(Ctx(nc, e, "tensor", sems), M)

        @block.vector
        def _(e):
            emit_program(Ctx(nc, e, "vector", sems), M)

        @block.scalar
        def _(e):
            emit_program(Ctx(nc, e, "scalar", sems), M)

        @block.gpsimd
        def _(e):
            emit_program(Ctx(nc, e, "gpsimd", sems), M)

    return nc


def emit_program(C, M):
    h, wst, sb, ar = M["h"], M["wst"], M["sb"], M["ar"]
    psA, psB = M["psA"], M["psB"]
    x = sb["x"]
    xnf = ar["xnf"]

    def phase_sync():
        w = tuple((s, C.c(s)) for s in ("pe", "dve", "act"))
        for en in ("sync", "tensor", "vector", "scalar", "gpsimd"):
            C.op(en, None, wait=w)

    for kc in range(KC):
        C.dma("sync", x[:, kc * TOWN:(kc + 1) * TOWN], h["xT"].ap()[kc], sem="q_p0")
    for name in ["ident", "ones_col16", "ones_row16", "ones16m", "ones_col64_16",
                 "ones_row64", "ones_col32", "tri_loc", "tri_off",
                 "lnw", "lnb", "olnw", "olnb", "bo", "bqkp"]:
        C.dma("sync", sb[name][:, :], h[name].ap(), sem="q_p0")
    x_ld = C.c("q_p0")

    conv_jobs = []

    def plan(dkey, outer, cols):
        for o in range(outer):
            c0 = 0
            while c0 < cols:
                cw = min(1024, cols - c0)
                conv_jobs.append((dkey, o, c0, cw))
                c0 += cw

    plan("win", KC, 4104)
    plan("woutT", KC, D)
    plan("wqT", KC, 512)
    plan("wkT", KC, 512)
    plan("wvT", KC, 512)
    plan("woT", 4, D)
    plan("projT", HN, MF)
    N_ATTN_JOBS = len(conv_jobs)
    plan("gates", NEXP * KC, HID)
    plan("ups", NEXP * KC, HID)
    plan("downs", NEXP * JCN, D)
    cs = dict(i=0)

    def conv_pump(n):
        for _ in range(n):
            i = cs["i"]
            if i >= len(conv_jobs):
                return
            cs["i"] = i + 1
            dkey, o, c0, cw = conv_jobs[i]
            sv, dv = h[dkey].ap(), wst[dkey].ap()
            if len(sv.shape) == 4:
                sap = sv[o // sv.shape[1], o % sv.shape[1]][:, c0:c0 + cw]
                dap = dv[o // dv.shape[1], o % dv.shape[1]][:, c0:c0 + cw]
            else:
                sap, dap = sv[o][:, c0:c0 + cw], dv[o][:, c0:c0 + cw]
            pdim = sap.shape[0]
            osem = "q_cvoA" if i < N_ATTN_JOBS else "q_cvoF"
            C.dma("scalar", sb["cvt32"][0:pdim, 0:cw], sap,
                  wait=(("pool", C.c("pool")),), sem="q_cvi")
            di = C.c("q_cvi")
            C.pool(lambda e, cw=cw, pdim=pdim: e.tensor_copy(
                sb["cvt16"][0:pdim, 0:cw], sb["cvt32"][0:pdim, 0:cw]),
                   wait=(("q_cvi", di), (osem, C.c(osem))))
            pc = C.c("pool")
            C.dma("scalar", dap, sb["cvt16"][0:pdim, 0:cw], wait=(("pool", pc),),
                  sem=osem)

    conv_pump(N_ATTN_JOBS)
    attn_w = ("q_cvoA", C.c("q_cvoA"))
    if DEBUG_STOP == 11:
        emit_output(C, M)
        return

    C.dma("sync", sb["prows"][0:1, 0:HN], h["lscale"].ap(), sem="q_prm")
    C.dma("sync", sb["prows"][0:1, 8:16], h["smear"].ap(), sem="q_prm")
    ldp = C.c("q_prm")
    C.dve(lambda e: e.tensor_scalar(sb["prows"][0:1, 48:56], sb["prows"][0:1, 0:8],
                                    -2.0, -LNS128, ALU.mult, ALU.add),
          wait=(("q_prm", ldp),), inc_n=1)
    C.act(lambda e: e.activation(sb["prows"][0:1, 16:24], sb["prows"][0:1, 48:56],
                                 AF.Exp, bias=0.0, scale=1.0),
          wait=(("dve", C.c("dve")),))
    C.act(lambda e: e.activation(sb["prows"][0:1, 24:32], sb["prows"][0:1, 8:16],
                                 AF.Sigmoid, bias=0.0, scale=1.0))
    pa = C.c("act")
    C.dve(lambda e: e.tensor_scalar(sb["prows"][0:1, 32:40], sb["prows"][0:1, 24:32],
                                    -1.0, 1.0, ALU.mult, ALU.add),
          wait=(("act", pa),), inc_n=1)
    pd = C.c("dve")
    C.dma("gpsimd", M["prmd"].ap()[0:1, :], sb["prows"][0:1, 16:40],
          wait=(("dve", pd),), sem="st_prm")
    g0 = C.c("st_prm")
    C.dma("scalar", sb["pcol"][:, :], APbc(M["prmd"], 0, 128, 24),
          wait=(("st_prm", g0),), sem="q_pcol")
    pcol_ld = C.c("q_pcol")

    def ccol(hh):
        return sb["pcol"][:, hh:hh + 1]

    def smcol(hh):
        return sb["pcol"][:, 8 + hh:9 + hh]

    def sm1col(hh):
        return sb["pcol"][:, 16 + hh:17 + hh]

    def emit_ln(src, out16, affine=False, fused_double=False, wcol=None,
                bcol=None, router_rows=False, src_wait=()):
        cq = ar["lncq32"]
        Rb, Bb = ar["Rb"], ar["Bb"]
        mucol = sb["lncol"][:, 0:16]
        vcol = sb["lncol"][:, 16:32]
        Rcol = sb["lncol"][:, 32:48]
        Bcol = sb["lncol"][:, 48:64]
        acc_s = sb["lncol"][:, 96:112]
        acc_q = sb["lncol"][:, 112:128]
        mm_marks = []
        for kc in range(KC):
            sl = slice(kc * TOWN, (kc + 1) * TOWN)
            w = list(src_wait) if kc == 0 else []
            if kc == 0:
                w.append(("pe", C.c("pe")))
                w.append(("dve", C.c("dve")))
            if kc >= 1:
                w.append(("pe", mm_marks[kc - 1][1]))
            C.act(lambda e, sl=sl: e.activation(cq[:, :], src[:, sl], AF.Square,
                                                bias=0.0, scale=1.0),
                  wait=tuple(w))
            ca = C.c("act")
            for tg in range(16):
                C.pe(lambda e, tg=tg, sl=sl: e.matmul(
                    psA[:, 1536 + tg:1536 + tg + 1],
                    src[:, sl.start + tg * 128: sl.start + tg * 128 + 128],
                    sb["ones_col32"][:, :], start=True, stop=True),
                    wait=((("act", ca),) +
                          ((("dve", mm_marks[kc - 1][0]),) if kc >= 1 else ())
                          if tg == 0 else ()))
                C.pe(lambda e, tg=tg: e.matmul(
                    psB[:, 1536 + tg:1536 + tg + 1],
                    cq[:, tg * 128:(tg + 1) * 128],
                    sb["ones_col32"][:, :], start=True, stop=True),
                    inc_n=1 if tg == 15 else 0)
            mmk = C.c("pe")
            if kc == 0:
                C.dve(lambda e: e.tensor_copy(acc_s, psA[:, 1536:1552]),
                      wait=(("pe", mmk),))
                C.dve(lambda e: e.tensor_copy(acc_q, psB[:, 1536:1552]), inc_n=1)
            else:
                C.dve(lambda e: e.tensor_add(acc_s, acc_s, psA[:, 1536:1552]),
                      wait=(("pe", mmk),))
                C.dve(lambda e: e.tensor_add(acc_q, acc_q, psB[:, 1536:1552]),
                      inc_n=1)
            mm_marks.append((C.c("dve"), mmk))
        C.dve(lambda e: e.tensor_scalar_mul(mucol, acc_s, 1.0 / D))
        C.dve(lambda e: e.tensor_scalar_mul(vcol, acc_q, 1.0 / D))
        C.dve(lambda e: e.tensor_mul(Bcol, mucol, mucol))
        C.dve(lambda e: e.tensor_sub(vcol, vcol, Bcol))
        eb, sc = (LNEPS * LNEPS, 1.0 + LNEPS) if fused_double else (LNEPS, 1.0)
        C.dve(lambda e, eb=eb, sc=sc: e.tensor_scalar(vcol, vcol, sc, eb,
                                                      ALU.mult, ALU.add), inc_n=1)
        var_d = C.c("dve")
        C.act(lambda e: e.activation(Bcol, vcol, AF.Ln, bias=0.0, scale=1.0),
              wait=(("dve", var_d),))
        C.act(lambda e: e.activation(Rcol, Bcol, AF.Exp, bias=0.0, scale=-0.5))
        r_a = C.c("act")
        C.dve(lambda e: e.tensor_mul(Bcol, mucol, Rcol), wait=(("act", r_a),))
        C.dve(lambda e: e.tensor_scalar_mul(Bcol, Bcol, -1.0))
        C.dve(lambda e: e.tensor_scalar_mul(sb["lncol"][:, 64:80], mucol, -1.0))
        if router_rows:
            C.dve(lambda e: e.tensor_copy(sb["rcols"][:, :], Rcol))
        rb_d = C.c("dve")
        C.pe(lambda e: e.transpose(psA[0:48, 0:128], sb["lncol"][:, 32 if not router_rows else 32:32], sb["ident"][:, :])
             if False else e.transpose(psA[0:48, 0:128], sb["lncol"][:, 32:80],
                                       sb["ident"][:, :]),
             wait=(("dve", rb_d),), inc_n=1)
        tr_pe = C.c("pe")
        C.dve(lambda e: e.tensor_copy(ar["lnrt"][0:48, :], psA[0:48, 0:128]),
              wait=(("pe", tr_pe),), inc_n=1)
        ev_d = C.c("dve")
        C.dma("gpsimd", M["lnrows"].ap(), ar["lnrt"][0:48, :],
              wait=(("dve", ev_d),), sem="st_lnr")
        g1 = C.c("st_lnr")
        C.dma("scalar", Rb[:, :], APbc(M["lnrows"], 0, 128, TOWN),
              wait=(("st_lnr", g1), ("dve", C.c("dve"))), sem="q_RB")
        C.dma("scalar", Bb[:, :], APbc(M["lnrows"], TOWN, 128, TOWN), sem="q_RB")
        if router_rows:
            C.dma("scalar", sb["rowpf"][0:1, :], APflat(M["lnrows"], 2 * TOWN, TOWN),
                  sem="q_RB")
        bc_v = C.c("q_RB")
        esf = ar["lnes"]
        for kc in range(KC):
            sl = slice(kc * TOWN, (kc + 1) * TOWN)
            C.dve(lambda e, sl=sl: e.tensor_mul(esf[:, :], src[:, sl], Rb[:, :]),
                  wait=(("q_RB", bc_v),) if kc == 0 else ())
            if affine:
                C.dve(lambda e: e.tensor_add(esf[:, :], esf[:, :], Bb[:, :]))
                C.dve(lambda e, sl=sl, kc=kc: e.tensor_scalar(
                    out16[:, sl], esf[:, :], wcol[:, kc:kc + 1],
                    bcol[:, kc:kc + 1], ALU.mult, ALU.add), inc_n=1)
            else:
                C.dve(lambda e, sl=sl: e.tensor_add(out16[:, sl], esf[:, :],
                                                    Bb[:, :]), inc_n=1)
        return C.c("dve")

    def emit_ag(out16own, xn_done):
        for kc in range(KC):
            C.dma("gpsimd", M["ag_in"].ap()[kc],
                  out16own[:, kc * TOWN:(kc + 1) * TOWN],
                  wait=(("dve", xn_done),) if kc == 0 else (), sem="st_ag")
        agw = C.c("st_ag")
        C.op("gpsimd", lambda e: e.collective_compute(
            "AllGather", ALU.bypass, replica_groups=M["groups"],
            ins=[M["ag_in"].ap().opt()], outs=[M["ag_out"].ap().opt()]),
            wait=(("st_ag", agw),), inc=(("cc", 1),))
        ccv = C.c("cc")
        for kc in range(KC):
            for rk in range(2):
                C.dma("sync", xnf[:, kc * T + rk * TOWN: kc * T + (rk + 1) * TOWN],
                      M["ag_out"].ap()[rk * KC + kc],
                      wait=(("cc", ccv), ("pe", C.c("pe"))) if (kc == 0 and rk == 0)
                      else (), sem="q_xnf")
        return C.c("q_xnf")

    xn_done = emit_ln(x, ar["xno"], affine=True, fused_double=True,
                      wcol=sb["lnw"], bcol=sb["lnb"],
                      src_wait=(("q_p0", x_ld),))
    xnf_ld = emit_ag(ar["xno"], xn_done)
    if DEBUG_STOP == 12:
        emit_output(C, M)
        return

    phase_sync()
    C.dma("sync", sb["ywin"][:, :].rearrange("p (k c) -> p k c", k=KC),
          wst["win"].ap().rearrange("k p c -> p k c")[:, :, 4096:4104],
          wait=(attn_w,), sem="q_ywin")
    yld = C.c("q_ywin")
    C.dma("scalar", sb["ybb"][:, :], APbc(h["byrow"], 0, 128, HN), sem="q_ybb")
    ybld = C.c("q_ybb")
    y_marks = []
    for tc in range(32):
        ps = psA[:, 0:HN] if tc % 2 == 0 else psA[:, 512:512 + HN]
        if tc == 0:
            w = [("q_ywin", yld), ("q_xnf", xnf_ld), ("q_ybb", ybld)]
        elif tc >= 2:
            w = [("act", y_marks[tc - 2])]
        else:
            w = []
        for kc in range(KC):
            C.pe(lambda e, tc=tc, kc=kc, ps=ps: e.matmul(
                ps, xnf[:, kc * T + tc * 128: kc * T + tc * 128 + 128],
                sb["ywin"][:, kc * HN:(kc + 1) * HN],
                start=(kc == 0), stop=(kc == KC - 1)),
                wait=tuple(w) if kc == 0 else (), inc_n=1 if kc == KC - 1 else 0)
        ypet = C.c("pe")
        C.dve(lambda e, ps=ps: e.tensor_add(ar["lnes"][:, 0:HN], ps, sb["ybb"][:, :]),
              wait=(("pe", ypet), ("act", C.c("act"))), inc_n=1)
        ydv = C.c("dve")
        C.act(lambda e, tc=tc: e.activation(sb["sigy"][:, tc * HN:(tc + 1) * HN],
                                            ar["lnes"][:, 0:HN], AF.Sigmoid,
                                            bias=0.0, scale=1.0),
              wait=(("dve", ydv),))
        y_marks.append(C.c("act"))
    posfm = ar["posfm"]
    tot_marks = []
    for tc in range(32):
        w = [("act", y_marks[-1])] if tc == 0 else []
        if tc >= 4:
            w.append(("dve", tot_marks[tc - 4]))
        C.pe(lambda e, tc=tc: e.matmul(
            psB[0:HN, (tc % 4) * 512:(tc % 4) * 512 + 129],
            sb["sigy"][:, tc * HN:(tc + 1) * HN], sb["tri_loc"][:, :],
            start=True, stop=True), wait=tuple(w), inc_n=1)
        tpe = C.c("pe")
        C.dve(lambda e, tc=tc: e.tensor_copy(
            posfm[:, tc * 128:(tc + 1) * 128],
            psB[0:HN, (tc % 4) * 512:(tc % 4) * 512 + 128]),
            wait=(("pe", tpe),))
        C.dve(lambda e, tc=tc: e.tensor_copy(
            sb["pos_t"][0:HN, tc:tc + 1],
            psB[0:HN, (tc % 4) * 512 + 128:(tc % 4) * 512 + 129]), inc_n=1)
        tot_marks.append(C.c("dve"))
    C.pe(lambda e: e.transpose(psA[0:32, 0:HN], sb["pos_t"][0:HN, 0:32],
                               sb["ident"][0:HN, 0:HN]),
         wait=(("dve", tot_marks[-1]),), inc_n=1)
    C.dve(lambda e: e.tensor_copy(ar["lnes"][0:32, 0:HN], psA[0:32, 0:HN]),
          wait=(("pe", C.c("pe")),), inc_n=1)
    C.pe(lambda e: e.matmul(psA[0:HN, 512:512 + 32], ar["lnes"][0:32, 0:HN],
                            sb["tri_off"][:, :], start=True, stop=True),
         wait=(("dve", C.c("dve")),), inc_n=1)
    C.dve(lambda e: e.tensor_copy(ar["lnes"][0:HN, 0:32], psA[0:HN, 512:512 + 32]),
          wait=(("pe", C.c("pe")),), inc_n=1)
    for tc in range(32):
        C.dve(lambda e, tc=tc: e.tensor_scalar_add(
            posfm[:, tc * 128:(tc + 1) * 128], posfm[:, tc * 128:(tc + 1) * 128],
            ar["lnes"][0:HN, tc:tc + 1]), inc_n=(1 if tc == 31 else 0))
    hi16, lo16, hi32 = ar["poshi16"], ar["poslo16"], ar["hi32h"]
    for hf in range(2):
        sl = slice(hf * TOWN, (hf + 1) * TOWN)
        C.dve(lambda e, sl=sl: e.tensor_copy(hi16[:, sl], posfm[:, sl]))
        C.dve(lambda e, sl=sl: e.tensor_copy(hi32[:, :], hi16[:, sl]))
        C.dve(lambda e, sl=sl: e.tensor_sub(posfm[:, sl], posfm[:, sl], hi32[:, :]))
        C.dve(lambda e, sl=sl: e.tensor_copy(lo16[:, sl], posfm[:, sl]))
        C.dve(lambda e, sl=sl: e.tensor_copy(posfm[:, sl], lo16[:, sl]))
        C.dve(lambda e, sl=sl: e.tensor_add(posfm[:, sl], posfm[:, sl], hi32[:, :]),
              inc_n=1)
    pos_d = C.c("dve")
    C.dma("gpsimd", M["poshd"].ap(), hi16[0:HN, :], wait=(("dve", pos_d),), sem="st_pos")
    C.dma("gpsimd", M["posld"].ap(), lo16[0:HN, :], sem="st_pos")
    pos_st = C.c("st_pos")
    for jc in range(32):
        ps = psB[:, 0:HN] if jc % 2 == 0 else psB[:, 512:512 + HN]
        C.pe(lambda e, jc=jc, ps=ps: e.transpose(
            ps, posfm[0:HN, jc * 128:(jc + 1) * 128], sb["ident"][0:HN, 0:HN]),
            wait=(("dve", pos_d if jc < 2 else C.c("dve")),), inc_n=1)
        tp = C.c("pe")
        C.dve(lambda e, jc=jc, ps=ps: e.tensor_copy(
            sb["pos_t"][:, jc * HN:(jc + 1) * HN], ps),
            wait=(("pe", tp),), inc_n=1)

    if DEBUG_STOP == 13:
        emit_output(C, M)
        return

    phase_sync()

    def win_load(job, bi):
        hh, blk = job // 4, job % 4
        wsem = "q_win0" if bi == 0 else "q_win1"
        C.dma("sync", ar["winblk"][:, bi * 1024:(bi + 1) * 1024].rearrange(
            "p (k c) -> p k c", k=KC),
            wst["win"].ap().rearrange("k p c -> p k c")[
                :, :, blk * 1024 + hh * 128: blk * 1024 + hh * 128 + 128],
            wait=(attn_w, ("pe", C.c("pe"))), sem=wsem)
        return (wsem, C.c(wsem))

    wl = {0: win_load(0, 0)}
    stg_free = 0
    for job in range(HN * 4):
        hh, blk = job // 4, job % 4
        bi = job % 2
        if job + 1 < HN * 4:
            wl[job + 1] = win_load(job + 1, (job + 1) % 2)
        wld = wl[job]
        evac_marks = []
        if blk == 2:
            C.dma("scalar", sb["vbb"][:, :], APbc(h["bvrow"], hh * 128, 128, 128),
                  wait=(("dve", C.c("dve")),), sem="q_vbb")
            vbld = C.c("q_vbb")
            for tc in range(32):
                ps = psB[:, (tc % 4) * 512:(tc % 4) * 512 + 128]
                w = []
                if tc == 0:
                    w = [wld, ("dve", C.c("dve")), ("act", C.c("act"))]
                elif tc >= 4:
                    w = [("dve", evac_marks[tc - 4])]
                for kc in range(KC):
                    C.pe(lambda e, tc=tc, kc=kc, ps=ps, bi=bi: e.matmul(
                        ps, xnf[:, kc * T + tc * 128: kc * T + tc * 128 + 128],
                        ar["winblk"][:, bi * 1024 + kc * 128: bi * 1024 + kc * 128 + 128],
                        start=(kc == 0), stop=(kc == KC - 1)),
                        wait=tuple(w) if kc == 0 else (),
                        inc_n=1 if kc == KC - 1 else 0)
                vpe = C.c("pe")
                C.dve(lambda e, tc=tc, ps=ps: e.tensor_add(
                    ar["stg16"][:, tc * 128:(tc + 1) * 128], ps, sb["vbb"][:, :]),
                    wait=(("pe", vpe), ("q_vbb", vbld)) +
                    ((("st_stg", stg_free),) if tc == 0 else ()), inc_n=1)
                evac_marks.append(C.c("dve"))
            C.dma("gpsimd", M["vst"].ap()[hh], ar["stg16"][:, :],
                  wait=(("dve", C.c("dve")),), sem="st_stg")
            stg_free = C.c("st_stg")
        else:
            for nt in range(8):
                ps = psB[:, (nt % 4) * 512:(nt % 4) * 512 + 512]
                w = []
                if nt == 0:
                    w = [wld, ("q_pcol", pcol_ld), ("dve", C.c("dve")),
                         ("act", C.c("act"))]
                elif nt >= 4:
                    w = [("dve" if blk != 3 else "act", evac_marks[nt - 4])]
                for kc in range(KC):
                    C.pe(lambda e, nt=nt, kc=kc, ps=ps, bi=bi: e.matmul(
                        ps, ar["winblk"][:, bi * 1024 + kc * 128: bi * 1024 + kc * 128 + 128],
                        xnf[:, kc * T + nt * 512: kc * T + nt * 512 + 512],
                        start=(kc == 0), stop=(kc == KC - 1)),
                        wait=tuple(w) if kc == 0 else (),
                        inc_n=1 if kc == KC - 1 else 0)
                ppe = C.c("pe")
                bix = (0 if blk == 0 else (1 if blk == 1 else 2)) * HN + hh
                bqs = sb["bqkp"][:, bix:bix + 1]
                osl = slice(nt * 512, (nt + 1) * 512)
                ww = (("pe", ppe),) + ((("st_stg", stg_free),) if nt == 0 else ())
                if blk == 0:
                    C.dve(lambda e, ps=ps, bqs=bqs, osl=osl, hh=hh: e.tensor_scalar(
                        ar["stg16"][:, osl], ps, bqs, ccol(hh), ALU.add, ALU.mult),
                        wait=ww, inc_n=1)
                    evac_marks.append(C.c("dve"))
                elif blk == 1:
                    C.dve(lambda e, ps=ps, bqs=bqs, osl=osl: e.tensor_scalar_add(
                        ar["stg16"][:, osl], ps, bqs), wait=ww, inc_n=1)
                    evac_marks.append(C.c("dve"))
                else:
                    C.act(lambda e, ps=ps, bqs=bqs, osl=osl: e.activation(
                        ar["stg16"][:, osl], ps, AF.Silu, bias=bqs, scale=1.0),
                        wait=ww)
                    evac_marks.append(C.c("act"))
            if blk == 0:
                C.dma("gpsimd", M["qst"].ap()[hh], ar["stg16"][:, :],
                      wait=(("dve", C.c("dve")),), sem="st_stg")
            elif blk == 1:
                C.dve(lambda e, hh=hh: e.tensor_scalar_mul(
                    ar["ksc"][:, :], ar["stg16"][:, :], smcol(hh)),
                    wait=(("st_pos", pos_st),))
                C.dve(lambda e, hh=hh: e.tensor_scalar_mul(
                    ar["stg16"][:, :], ar["stg16"][:, :], sm1col(hh)))
                C.dve(lambda e: e.tensor_add(ar["stg16"][:, 1:T],
                                             ar["stg16"][:, 1:T],
                                             ar["ksc"][:, 0:T - 1]), inc_n=1)
                C.dma("gpsimd", M["kst"].ap()[hh], ar["stg16"][:, :],
                      wait=(("dve", C.c("dve")),), sem="st_stg")
            else:
                C.dma("gpsimd", M["spst"].ap()[hh], ar["stg16"][:, :],
                      wait=(("act", C.c("act")),), sem="st_stg")
            stg_free = C.c("st_stg")
        conv_pump(6)
    proj_st = C.c("st_stg")

    if DEBUG_STOP == 14:
        emit_output(C, M)
        return

    phase_sync()
    C.dma("sync", ar["masks1"][:, :], h["mask1m"].ap(),
          wait=(("pe", C.c("pe")), ("dve", C.c("dve"))), sem="q_mask")
    C.dma("sync", ar["masks2"][:, :], h["mask2m"].ap(), sem="q_mask")
    mask_ld = C.c("q_mask")

    def head_load(hh, guard_pe):
        slot = hh % 2
        o = slot * T
        hsem = "q_hl0" if slot == 0 else "q_hl1"
        w = (("st_stg", proj_st), ("pe", guard_pe), ("dve", C.c("dve")),
             ("act", C.c("act")))
        C.dma("sync", ar["qh"][:, o:o + T], M["qst"].ap()[hh], wait=w, sem=hsem)
        C.dma("sync", ar["kh"][:, o:o + T], M["kst"].ap()[hh], sem=hsem)
        C.dma("sync", ar["vh"][:, o:o + T], M["vst"].ap()[hh], sem=hsem)
        C.dma("sync", ar["sph"][:, o:o + T], M["spst"].ap()[hh], sem=hsem)
        return (hsem, C.c(hsem))

    def rowp_load(hh, guard_pe):
        C.dma("sync", sb["rowp16"][0:1, :], APflat(M["poshd"], hh * T, T),
              wait=(("st_pos", pos_st), ("pe", guard_pe)), sem="q_rp")
        C.dma("sync", sb["rowp16"][1:2, :], APflat(M["posld"], hh * T, T),
              sem="q_rp")
        return ("q_rp", C.c("q_rp"))

    hload = {0: head_load(0, 0)}
    if DEBUG_STOP == 23:
        C.op("gpsimd", None, wait=(hload[0], rowp_load(0, 0), ("q_mask", C.c("q_mask"))))
        emit_output(C, M)
        return
    for hh in range(HN):
        slot = hh % 2
        o = slot * T
        ld_q = hload[hh]
        rp_ld = rowp_load(hh, C.c("pe"))
        if hh + 1 < HN:
            hload[hh + 1] = head_load(hh + 1, C.c("pe"))
        poshilo_r = sb["rowp16"][0:2, :]
        negm_r = sb["rowp16"][64:65, :]
        qh, kh, vh, sph = ar["qh"], ar["kh"], ar["vh"], ar["sph"]
        psum_war = {0: 0, 1: 0}
        for qi in range(32):
            jt_tot = qi // 4 + 1
            chunks = [(0, min(jt_tot, 4))]
            if jt_tot > 4:
                chunks.append((4, jt_tot))
            mA = sb["colscr"][:, 0:1]
            mB = sb["colscr"][:, 1:2]
            lA = sb["colscr"][:, 2:3]
            lB = sb["colscr"][:, 3:4]
            for ci, (j0, j1) in enumerate(chunks):
                psid = ci if len(chunks) == 2 else qi % 2
                ps = psA if psid == 0 else psB
                clen = (j1 - j0) * 512
                for jt in range(j0, j1):
                    sl = slice((jt - j0) * 512, (jt - j0) * 512 + 512)
                    w = []
                    if jt == j0:
                        w = [("act", psum_war[psid])]
                        if qi == 0 and ci == 0:
                            w += [ld_q, rp_ld, ("q_mask", mask_ld),
                                  ("dve", C.c("dve")), ("act", C.c("act"))]
                    C.pe(lambda e, sl=sl, jt=jt, qi=qi, ps=ps: e.matmul(
                        ps[:, sl], qh[:, o + qi * 128: o + qi * 128 + 128],
                        kh[:, o + jt * 512: o + jt * 512 + 512],
                        start=True, stop=(DEBUG_STOP == 24)), wait=tuple(w))
                    if DEBUG_STOP == 24 and hh == 0 and qi == 0:
                        C.pe(lambda e: e.matmul(
                            psB[:, 0:512], qh[:, o:o + 128],
                            kh[:, o:o + 512], start=True, stop=True), inc_n=1)
                        emit_output(C, M)
                        return
                    C.pe(lambda e, sl=sl, jt=jt, ps=ps: e.matmul(
                        ps[:, sl], sb["ones16m"][0:2, :],
                        poshilo_r[:, jt * 512: jt * 512 + 512],
                        start=False, stop=True),
                        inc_n=1 if jt == j1 - 1 else 0)
                mm_pe = C.c("pe")
                if DEBUG_STOP == 19 and hh == 0 and qi == 0:
                    emit_output(C, M)
                    return
                if j1 == jt_tot:
                    dd = qi % 4
                    msl = slice((jt_tot - 1 - j0) * 512, (jt_tot - j0) * 512)
                    C.dve(lambda e, msl=msl, dd=dd, ps=ps: e.tensor_add(
                        ps[:, msl], ps[:, msl],
                        ar["masks1"][:, 384 - 128 * dd: 384 - 128 * dd + 512]),
                        wait=(("pe", mm_pe),), inc_n=1)
                if DEBUG_STOP == 20 and hh == 0 and qi == 0:
                    emit_output(C, M)
                    return
                nb = clen // 512
                for jb in range(nb):
                    C.dve(lambda e, ps=ps, jb=jb: e.reduce_max(
                        sb["colscr"][:, 4 + jb:5 + jb],
                        ps[:, jb * 512:(jb + 1) * 512], AX),
                        wait=(("pe", mm_pe),) if jb == 0 else ())
                dst = mA if ci == 0 else mB
                C.dve(lambda e, dst=dst: e.tensor_copy(dst, sb["colscr"][:, 4:5]))
                for jb in range(1, nb):
                    C.dve(lambda e, dst=dst, jb=jb: e.tensor_max(
                        dst, dst, sb["colscr"][:, 4 + jb:5 + jb]), inc_n=1)
            if DEBUG_STOP == 21 and hh == 0 and qi == 0:
                emit_output(C, M)
                return
            if len(chunks) == 2:
                C.dve(lambda e: e.tensor_max(mA, mA, mB))
            C.dve(lambda e: e.tensor_copy(sb["colscr3"][:, 0:1], mA))
            C.dve(lambda e, qi=qi: e.tensor_scalar_mul(
                sb["negm32"][:, qi:qi + 1], sb["colscr3"][:, 0:1], -1.0), inc_n=1)
            neg_d = C.c("dve")
            bi = 0
            for ci, (j0, j1) in enumerate(chunks):
                psid = ci if len(chunks) == 2 else qi % 2
                ps = psA if psid == 0 else psB
                clen = (j1 - j0) * 512
                for jb in range(clen // 512):
                    C.act(lambda e, ps=ps, jb=jb, qi=qi, bi=bi: e.activation(
                        ar["pthrow"][:, jb * 512:(jb + 1) * 512],
                        ps[:, jb * 512:(jb + 1) * 512], AF.Exp,
                        bias=sb["negm32"][:, qi:qi + 1], scale=1.0,
                        accum_out=sb["colscr"][:, 8 + bi:9 + bi]),
                        wait=(("dve", C.c("dve")),) if bi == 0 else ())
                    la = C.c("act")
                    if bi == 0:
                        C.dve(lambda e: e.tensor_copy(lA, sb["colscr"][:, 8:9]),
                              wait=(("act", la),))
                    else:
                        C.dve(lambda e, bi=bi: e.tensor_add(
                            lA, lA, sb["colscr"][:, 8 + bi:9 + bi]),
                            wait=(("act", la),))
                    bi += 1
                psum_war[psid] = C.c("act")
            C.dve(lambda e, qi=qi: e.reciprocal(sb["linv"][:, qi:qi + 1], lA),
                  inc_n=1)
            if DEBUG_STOP == 22 and hh == 0 and qi == 0:
                emit_output(C, M)
                return
        p1_d = C.c("dve")
        p1_a = C.c("act")
        if DEBUG_STOP == 16 and hh == 0:
            emit_output(C, M)
            return
        C.pe(lambda e: e.transpose(psA[0:32, 0:128], sb["negm32"][:, 0:32],
                                   sb["ident"][:, :]),
             wait=(("dve", p1_d), ("act", p1_a)), inc_n=1)
        C.dve(lambda e: e.tensor_copy(ar["nm32t"][:, :], psA[0:32, 0:128]),
              wait=(("pe", C.c("pe")),))
        C.dve(lambda e: e.tensor_copy(ar["nm16t"][:, :], ar["nm32t"][:, :]),
              inc_n=1)
        C.dma("gpsimd", M["mrowd"].ap(), ar["nm16t"][:, :],
              wait=(("dve", C.c("dve")),), sem="st_nm")
        C.pe(lambda e: e.transpose(psA[0:32, 128:256], sb["linv"][:, 0:32],
                                   sb["ident"][:, :]), inc_n=1)
        C.dve(lambda e: e.tensor_copy(ar["nm32t"][:, :], psA[0:32, 128:256]),
              wait=(("pe", C.c("pe")),), inc_n=1)
        C.dma("gpsimd", M["linvd"].ap()[hh].rearrange("(a b) -> a b", a=32),
              ar["nm32t"][:, :], wait=(("dve", C.c("dve")),), sem="st_nm")
        bounce_g = C.c("st_nm")
        C.dma("sync", negm_r[:, :], APflat(M["mrowd"], 0, T),
              wait=(("st_nm", bounce_g),), sem="q_nm")
        nm_ld = C.c("q_nm")
        if DEBUG_STOP == 17 and hh == 0:
            emit_output(C, M)
            return
        exp_war = [0, 0]
        for s in range(8):
            jcn = 4 * s + 4
            C.dma("scalar", ar["invlb"][:, :],
                  APbc(M["linvd"], hh * T + s * 512, 128, 512),
                  wait=(("st_nm", bounce_g), ("dve", C.c("dve"))), sem="q_inv")
            inv_ld = C.c("q_inv")
            av_done = 0
            for jc in range(jcn):
                pslot = jc % 2
                psT = psA[:, pslot * 512:(pslot + 1) * 512]
                w = [("act", exp_war[pslot])]
                if jc == 0 and s == 0:
                    w.append(("q_nm", nm_ld))
                if jc >= 2:
                    w.append(("pe", av_done))
                C.pe(lambda e, jc=jc, s=s, psT=psT: e.matmul(
                    psT, kh[:, o + jc * 128: o + jc * 128 + 128],
                    qh[:, o + s * 512: o + s * 512 + 512],
                    start=True, stop=False), wait=tuple(w))
                C.pe(lambda e, s=s, psT=psT: e.matmul(
                    psT, sb["ones16m"][64:65, :], negm_r[:, s * 512: s * 512 + 512],
                    start=False, stop=True), inc_n=1)
                mmT = C.c("pe")
                dd = jc - 4 * s
                if dd >= 0:
                    C.dve(lambda e, dd=dd, psT=psT: e.tensor_add(
                        psT, psT,
                        ar["masks2"][:, 384 - 128 * dd:384 - 128 * dd + 512]),
                        wait=(("pe", mmT),), inc_n=1)
                    mwd = (("dve", C.c("dve")),)
                else:
                    mwd = (("pe", mmT),)
                C.act(lambda e, jc=jc, hh=hh, psT=psT, pslot=pslot: e.activation(
                    ar["pts"][:, pslot * 512:(pslot + 1) * 512], psT, AF.Exp,
                    bias=sb["pos_t"][:, jc * HN + hh: jc * HN + hh + 1], scale=1.0),
                    wait=mwd)
                exp_war[pslot] = C.c("act")
                C.pe(lambda e, jc=jc, jcn=jcn, pslot=pslot: e.matmul(
                    psB[:, 0:512], vh[:, o + jc * 128: o + jc * 128 + 128],
                    ar["pts"][:, pslot * 512:(pslot + 1) * 512],
                    start=(jc == 0), stop=(jc == jcn - 1)),
                    wait=(("act", exp_war[pslot]),) +
                    ((("dve", C.c("dve")),) if jc == 0 else ()),
                    inc_n=1)
                av_done = C.c("pe")
            C.dve(lambda e: e.tensor_mul(ar["esmall"][:, :], psB[:, 0:512],
                                         ar["invlb"][:, :]),
                  wait=(("pe", av_done), ("q_inv", inv_ld)))
            C.dve(lambda e, s=s: e.tensor_mul(
                ar["gtile"][:, :], ar["esmall"][:, :],
                sph[:, o + s * 512: o + s * 512 + 512]),
                wait=(("st_g", C.c("st_g")),), inc_n=1)
            g_d = C.c("dve")
            C.dma("gpsimd", M["gst"].ap()[hh][:, s * 512:(s + 1) * 512],
                  ar["gtile"][:, :], wait=(("dve", g_d),), sem="st_g")
            if DEBUG_STOP == 18 and hh == 0 and s == 0:
                emit_output(C, M)
                return
        conv_pump(20)
    g_store = C.c("st_g")

    if DEBUG_STOP == 15:
        emit_output(C, M)
        return

    phase_sync()
    C.dma("sync", ar["wout_s"][:, :].rearrange("p (k c) -> p k c", k=KC),
          wst["woutT"].ap().rearrange("k p c -> p k c"),
          wait=(attn_w, ("pe", C.c("pe"))), sem="q_wout")
    wo_ld = C.c("q_wout")
    ev_prev = 0
    for nt in range(8):
        for hh2 in range(HN):
            C.dma("sync", ar["gld"][:, hh2 * 512:(hh2 + 1) * 512],
                  M["gst"].ap()[hh2][:, nt * 512:(nt + 1) * 512],
                  wait=((("st_g", g_store), ("pe", C.c("pe"))) if hh2 == 0
                        else ()), sem="q_gld")
        gl = C.c("q_gld")
        evd = {}
        for dm in range(KC):
            ps = psA[:, (dm % 4) * 512:(dm % 4) * 512 + 512]
            for hh2 in range(HN):
                C.pe(lambda e, dm=dm, hh2=hh2, ps=ps: e.matmul(
                    ps, ar["wout_s"][:, hh2 * D + dm * 128: hh2 * D + dm * 128 + 128],
                    ar["gld"][:, hh2 * 512:(hh2 + 1) * 512],
                    start=(hh2 == 0), stop=(hh2 == HN - 1)),
                    wait=(("q_gld", gl), ("q_wout", wo_ld),
                          ("dve", evd.get(dm - 4, ev_prev))) if hh2 == 0 else (),
                    inc_n=1 if hh2 == HN - 1 else 0)
            wpe = C.c("pe")
            C.dve(lambda e, dm=dm, ps=ps: e.tensor_copy(
                ar["aotile"][:, dm * 512:(dm + 1) * 512], ps),
                wait=(("pe", wpe), ("st_ao", C.c("st_ao"))), inc_n=1)
            evd[dm] = C.c("dve")
        ev_prev = C.c("dve")
        for dm in range(KC):
            C.dma("gpsimd",
                  M["rs_in"].ap()[nt // 4, dm][:, (nt % 4) * 512:(nt % 4) * 512 + 512],
                  ar["aotile"][:, dm * 512:(dm + 1) * 512],
                  wait=(("dve", ev_prev),) if dm == 0 else (), sem="st_ao")
        conv_pump(8)
    rs_w = C.c("st_ao")
    C.op("gpsimd", lambda e: e.collective_compute(
        "ReduceScatter", ALU.add, replica_groups=M["groups"],
        ins=[M["rs_in"].ap().opt()], outs=[M["rs_out"].ap().opt()]),
        wait=(("st_ao", rs_w),), inc=(("cc", 1),))
    rs_cc = C.c("cc")

    def ln_residual_from_dram(rs_cc, use_oln):
        cb, cq = ar["lncb"], ar["lncq"]
        Rb, Bb = ar["Rb"], ar["Bb"]
        mucol = sb["lncol"][:, 0:16]
        vcol = sb["lncol"][:, 16:32]
        Rcol = sb["lncol"][:, 32:48]
        Bcol = sb["lncol"][:, 48:64]
        acc_s = sb["lncol"][:, 96:112]
        acc_q = sb["lncol"][:, 112:128]
        mm_marks = []
        for kc in range(KC):
            pp = (kc % 2) * TOWN
            C.dma("sync", ar["lnes"][:, :], M["rs_out"].ap()[kc],
                  wait=((("cc", rs_cc), ("pe", C.c("pe")), ("dve", C.c("dve")))
                        if kc == 0 else (("dve", C.c("dve")),)), sem="q_ao3")
            al = C.c("q_ao3")
            w2 = [("q_ao3", al)]
            if kc == 0:
                w2.append(("pe", C.c("pe")))
            if kc >= 1:
                w2.append(("dve", mm_marks[kc - 1]))
            C.dve(lambda e, pp=pp: e.tensor_copy(cb[:, pp:pp + TOWN],
                                                 ar["lnes"][:, :]),
                  wait=tuple(w2), inc_n=1)
            cd = C.c("dve")
            C.act(lambda e, pp=pp: e.activation(cq[:, pp:pp + TOWN],
                                                cb[:, pp:pp + TOWN], AF.Square,
                                                bias=0.0, scale=1.0),
                  wait=(("dve", cd),))
            ca = C.c("act")
            for tg in range(16):
                C.pe(lambda e, tg=tg, pp=pp: e.matmul(
                    psA[:, 1536 + tg:1536 + tg + 1],
                    cb[:, pp + tg * 128:pp + tg * 128 + 128],
                    sb["ones_col16"][:, :], start=True, stop=True),
                    wait=((("dve", cd), ("act", ca)) if tg == 0 else ()))
                C.pe(lambda e, tg=tg, pp=pp: e.matmul(
                    psB[:, 1536 + tg:1536 + tg + 1],
                    cq[:, pp + tg * 128:pp + tg * 128 + 128],
                    sb["ones_col16"][:, :], start=True, stop=True),
                    inc_n=1 if tg == 15 else 0)
            mmk = C.c("pe")
            if kc == 0:
                C.dve(lambda e: e.tensor_copy(acc_s, psA[:, 1536:1552]),
                      wait=(("pe", mmk),))
                C.dve(lambda e: e.tensor_copy(acc_q, psB[:, 1536:1552]), inc_n=1)
            else:
                C.dve(lambda e: e.tensor_add(acc_s, acc_s, psA[:, 1536:1552]),
                      wait=(("pe", mmk),))
                C.dve(lambda e: e.tensor_add(acc_q, acc_q, psB[:, 1536:1552]),
                      inc_n=1)
            mm_marks.append(C.c("dve"))
        C.dve(lambda e: e.tensor_scalar_mul(mucol, acc_s, 1.0 / D))
        C.dve(lambda e: e.tensor_scalar_mul(vcol, acc_q, 1.0 / D))
        C.dve(lambda e: e.tensor_mul(Bcol, mucol, mucol))
        C.dve(lambda e: e.tensor_sub(vcol, vcol, Bcol))
        C.dve(lambda e: e.tensor_scalar(vcol, vcol, 1.0, LNEPS, ALU.mult, ALU.add),
              inc_n=1)
        var_d = C.c("dve")
        C.act(lambda e: e.activation(Bcol, vcol, AF.Ln, bias=0.0, scale=1.0),
              wait=(("dve", var_d),))
        C.act(lambda e: e.activation(Rcol, Bcol, AF.Exp, bias=0.0, scale=-0.5))
        r_a = C.c("act")
        C.dve(lambda e: e.tensor_mul(Bcol, mucol, Rcol), wait=(("act", r_a),))
        C.dve(lambda e: e.tensor_scalar_mul(Bcol, Bcol, -1.0), inc_n=1)
        rb_d = C.c("dve")
        C.pe(lambda e: e.transpose(psA[0:32, 0:128], sb["lncol"][:, 32:64],
                                   sb["ident"][:, :]),
             wait=(("dve", rb_d),), inc_n=1)
        tr_pe = C.c("pe")
        C.dve(lambda e: e.tensor_copy(ar["lnrt"][0:32, :], psA[0:32, 0:128]),
              wait=(("pe", tr_pe),), inc_n=1)
        ev_d = C.c("dve")
        C.dma("gpsimd", M["lnrows"].ap()[0:2], ar["lnrt"][0:32, :],
              wait=(("dve", ev_d),), sem="st_lnr")
        g1 = C.c("st_lnr")
        C.dma("scalar", Rb[:, :], APbc(M["lnrows"], 0, 128, TOWN),
              wait=(("st_lnr", g1), ("dve", C.c("dve"))), sem="q_RB")
        C.dma("scalar", Bb[:, :], APbc(M["lnrows"], TOWN, 128, TOWN), sem="q_RB")
        bc_v = C.c("q_RB")
        for kc in range(KC):
            sl = slice(kc * TOWN, (kc + 1) * TOWN)
            C.dma("sync", ar["lnes"][:, :], M["rs_out"].ap()[kc],
                  wait=(("dve", C.c("dve")),), sem="q_ao3")
            al = C.c("q_ao3")
            C.dve(lambda e: e.tensor_mul(ar["lnes2"][:, :], ar["lnes"][:, :],
                                         Rb[:, :]),
                  wait=(("q_ao3", al), ("q_RB", bc_v)))
            C.dve(lambda e: e.tensor_add(ar["lnes2"][:, :], ar["lnes2"][:, :],
                                         Bb[:, :]))
            C.dve(lambda e, kc=kc: e.tensor_scalar(
                ar["lnes2"][:, :], ar["lnes2"][:, :], sb["olnw"][:, kc:kc + 1],
                sb["olnb"][:, kc:kc + 1], ALU.mult, ALU.add))
            C.dve(lambda e, sl=sl: e.tensor_add(x[:, sl], x[:, sl],
                                                ar["lnes2"][:, :]), inc_n=1)

    ln_residual_from_dram(rs_cc, True)
    if DEBUG_STOP == 1:
        emit_output(C, M)
        return

    def emit_ffn(call):
        phase_sync()
        conv_pump(len(conv_jobs))
        ffn_w = ("q_cvoF", C.c("q_cvoF"))
        xn_done = emit_ln(x, ar["xno"], affine=False, router_rows=True)
        xno = ar["xno"]
        C.dma("sync", ar["wr32"][:, :].rearrange("p (k c) -> p k c", k=KC),
              h["wr"].ap().rearrange("k p c -> p k c"),
              wait=(("dve", xn_done), ("pe", C.c("pe"))), sem="q_wr")
        wr_ld = C.c("q_wr")
        for kc in range(KC):
            C.pe(lambda e, kc=kc: e.matmul(
                psB[0:1, 1024:1031], sb["ones_col32"][:, :],
                ar["wr32"][:, kc * 7:(kc + 1) * 7],
                start=(kc == 0), stop=(kc == KC - 1)),
                wait=((("q_wr", wr_ld),)) if kc == 0 else (),
                inc_n=1 if kc == KC - 1 else 0)
        swr_pe = C.c("pe")
        C.dve(lambda e: e.tensor_scalar_mul(sb["prows"][0:1, 40:47],
                                            psB[0:1, 1024:1031], -1.0),
              wait=(("pe", swr_pe),), inc_n=1)
        swr_d = C.c("dve")
        C.dma("scalar", sb["rbb"][:, :], APbc(h["rbias"], 0, 128, 7),
              wait=(("dve", C.c("dve")),), sem="q_rbb")
        rb_ld = C.c("q_rbb")
        wt_marks = {}
        for tg in range(16):
            psl = psB[:, 1536 + (tg % 2) * 16: 1536 + (tg % 2) * 16 + 7]
            w = [("dve", xn_done)] if tg == 0 else []
            if tg >= 2:
                w.append(("dve", wt_marks[tg - 2]))
            for kc in range(KC):
                C.pe(lambda e, tg=tg, kc=kc, psl=psl: e.matmul(
                    psl, x[:, kc * TOWN + tg * 128: kc * TOWN + tg * 128 + 128],
                    ar["wr32"][:, kc * 7:(kc + 1) * 7],
                    start=(kc == 0), stop=False),
                    wait=tuple(w) if kc == 0 else ())
            C.pe(lambda e, tg=tg, psl=psl: e.matmul(
                psl, sb["rowpf"][0:1, tg * 128:(tg + 1) * 128],
                sb["prows"][0:1, 40:47], start=False, stop=True),
                wait=(("dve", swr_d),) if tg == 0 else (), inc_n=1)
            rpe = C.c("pe")
            C.dve(lambda e, tg=tg, psl=psl: e.tensor_scalar_mul(
                ar["ltile"][:, 0:7], psl, sb["rcols"][:, tg:tg + 1]),
                wait=(("pe", rpe),))
            C.dve(lambda e: e.tensor_mul(ar["ltile"][:, 0:7], ar["ltile"][:, 0:7],
                                         sb["rbb"][:, :]),
                  wait=(("q_rbb", rb_ld),), inc_n=1)
            lg_d = C.c("dve")
            C.act(lambda e: e.activation(ar["ptile"][:, 0:7], ar["ltile"][:, 0:7],
                                         AF.Sigmoid, bias=0.0, scale=1.0),
                  wait=(("dve", lg_d),))
            pr_a = C.c("act")
            c0 = sb["colscr"][:, 0:1]
            c1 = sb["colscr"][:, 1:2]
            C.dve(lambda e: e.reduce_max(c0, ar["ptile"][:, 0:7], AX),
                  wait=(("act", pr_a),))
            C.dve(lambda e: e.tensor_single_scalar(
                ar["ltile"][:, 0:7], ar["ptile"][:, 0:7], c0, ALU.is_ge))
            C.dve(lambda e: e.tensor_scalar_mul(ar["ltile"][:, 0:7],
                                                ar["ltile"][:, 0:7], 1e30))
            C.dve(lambda e: e.tensor_sub(ar["ltile"][:, 0:7], ar["ptile"][:, 0:7],
                                         ar["ltile"][:, 0:7]))
            C.dve(lambda e: e.reduce_max(c1, ar["ltile"][:, 0:7], AX))
            C.dve(lambda e: e.tensor_single_scalar(
                ar["ltile"][:, 0:7], ar["ptile"][:, 0:7], c1, ALU.is_ge))
            C.dve(lambda e: e.tensor_add(c0, c0, c1))
            C.dve(lambda e: e.reciprocal(c1, c0))
            C.dve(lambda e: e.tensor_mul(ar["ltile"][:, 0:7], ar["ltile"][:, 0:7],
                                         ar["ptile"][:, 0:7]))
            C.dve(lambda e: e.tensor_scalar_mul(ar["ltile"][:, 0:7],
                                                ar["ltile"][:, 0:7], c1), inc_n=1)
            wv_d = C.c("dve")
            C.pe(lambda e, tg=tg: e.transpose(
                psB[0:7, 1280 + (tg % 2) * 128: 1280 + (tg % 2) * 128 + 128],
                ar["ltile"][:, 0:7], sb["ident"][:, :]),
                wait=(("dve", wv_d),), inc_n=1)
            tpe = C.c("pe")
            C.dve(lambda e, tg=tg: e.tensor_copy(
                sb["wt7"][0:7, tg * 128:(tg + 1) * 128],
                psB[0:7, 1280 + (tg % 2) * 128: 1280 + (tg % 2) * 128 + 128]),
                wait=(("pe", tpe),), inc_n=1)
            wt_marks[tg] = C.c("dve")
        C.dma("gpsimd", M["wrowd"].ap(), sb["wt7"][0:7, :],
              wait=(("dve", wt_marks[15]),), sem="st_wrow")
        wrow_st = C.c("st_wrow")

        psG = [psA[:, 0:512], psA[:, 512:1024]]
        psU = [psA[:, 1024:1536], psA[:, 1536:2048]]
        psD = [psB[:, 0:512], psB[:, 512:1024]]
        slot_war = {}
        hist = dict(g_act=0, u_dve=0, d_dve={0: 0, 1: 0})
        ln_guard = C.c("dve")

        def ring_load(eidx, jc, what, first):
            wsrc = wst["gates"] if what == "g" else wst["ups"]
            buf = ar["gring"] if what == "g" else ar["uring"]
            slx = jc % 4
            rsem = f"q_{what}{slx}"
            w = []
            if first:
                w += [ffn_w, ("dve", ln_guard), ("st_wrow", wrow_st)]
            pw = slot_war.get((what, slx), 0)
            if pw:
                w.append(("pe", pw))
            C.dma("sync", buf[:, slx * 1024:(slx + 1) * 1024].rearrange(
                "p (k c) -> p k c", k=KC),
                wsrc.ap()[eidx].rearrange("k p c -> p k c")[:, :, jc * 128:(jc + 1) * 128],
                wait=tuple(w), sem=rsem)
            return (rsem, C.c(rsem))

        for nt in range(2):
            tb = nt * 1024
            for eidx in range(NEXP):
                conv_pump(4)
                first_e = (nt == 0 and eidx == 0)
                if eidx > 0:
                    C.dma("scalar", ar["wbb"][:, 0:1024],
                          APbc(M["wrowd"], (eidx - 1) * TOWN + tb, 128, 1024),
                          wait=(("st_wrow", wrow_st), ("dve", hist["d_dve"][0]),
                                ("dve", hist["d_dve"][1])), sem="q_wbb")
                wb_ld = C.c("q_wbb")
                for ns in range(2):
                    nb = tb + ns * 512
                    ring_ld = {}
                    for j0 in range(4):
                        ring_ld[("g", j0)] = ring_load(eidx, j0, "g",
                                                       first_e and ns == 0 and j0 == 0)
                        ring_ld[("u", j0)] = ring_load(eidx, j0, "u", False)
                    for jc in range(JCN):
                        gl = ring_ld[("g", jc)]
                        ul = ring_ld[("u", jc)]
                        gsl = ar["gring"][:, (jc % 4) * 1024:(jc % 4) * 1024 + 1024]
                        usl = ar["uring"][:, (jc % 4) * 1024:(jc % 4) * 1024 + 1024]
                        pg, pu = psG[jc % 2], psU[jc % 2]
                        for kc in range(KC):
                            C.pe(lambda e, kc=kc, gsl=gsl, pg=pg, nb=nb: e.matmul(
                                pg, gsl[:, kc * 128:(kc + 1) * 128],
                                xno[:, kc * TOWN + nb: kc * TOWN + nb + 512],
                                start=(kc == 0), stop=(kc == KC - 1)),
                                wait=(gl, ("act", hist["g_act"]))
                                if kc == 0 else ())
                        for kc in range(KC):
                            C.pe(lambda e, kc=kc, usl=usl, pu=pu, nb=nb: e.matmul(
                                pu, usl[:, kc * 128:(kc + 1) * 128],
                                xno[:, kc * TOWN + nb: kc * TOWN + nb + 512],
                                start=(kc == 0), stop=(kc == KC - 1)),
                                wait=(ul, ("dve", hist["u_dve"]))
                                if kc == 0 else (),
                                inc_n=1 if kc == KC - 1 else 0)
                        gu_pe = C.c("pe")
                        slot_war[("g", jc % 4)] = gu_pe
                        slot_war[("u", jc % 4)] = gu_pe
                        if jc + 4 < JCN:
                            ring_ld[("g", jc + 4)] = ring_load(eidx, jc + 4, "g", False)
                            ring_ld[("u", jc + 4)] = ring_load(eidx, jc + 4, "u", False)
                        C.act(lambda e, pg=pg: e.activation(
                            ar["sg"][:, :], pg, AF.Silu, bias=0.0, scale=1.0),
                            wait=(("pe", gu_pe),))
                        hist["g_act"] = C.c("act")
                        C.dve(lambda e, jc=jc, pu=pu: e.tensor_mul(
                            ar["hbuf"][:, jc * 512:(jc + 1) * 512], pu,
                            ar["sg"][:, :]),
                            wait=(("act", hist["g_act"]),), inc_n=1)
                        hist["u_dve"] = C.c("dve")
                    h_done = C.c("dve")
                    for half in range(2):
                        for jc in range(JCN):
                            w = ()
                            if jc == 0:
                                w = (("dve", hist["d_dve"][0]),
                                     ("dve", hist["d_dve"][1]),
                                     ("dve", ln_guard), ("st_wrow", wrow_st),
                                     ffn_w)
                            C.dma("sync", ar["dhalf"][:, jc * 512:(jc + 1) * 512],
                                  wst["downs"].ap()[eidx, jc][:, half * 512:(half + 1) * 512],
                                  wait=w, sem="q_dh")
                        d_ld = C.c("q_dh")
                        for dmi in range(4):
                            dm = half * 4 + dmi
                            pdp = psD[dmi % 2]
                            for jc in range(JCN):
                                C.pe(lambda e, jc=jc, dmi=dmi, pdp=pdp: e.matmul(
                                    pdp, ar["dhalf"][:, jc * 512 + dmi * 128:
                                                     jc * 512 + dmi * 128 + 128],
                                    ar["hbuf"][:, jc * 512:(jc + 1) * 512],
                                    start=(jc == 0), stop=(jc == JCN - 1)),
                                    wait=(("q_dh", d_ld), ("dve", h_done),
                                          ("dve", hist["d_dve"][dmi % 2]))
                                    if jc == 0 else (),
                                    inc_n=1 if jc == JCN - 1 else 0)
                            dpe = C.c("pe")
                            xsl = x[:, dm * TOWN + nb: dm * TOWN + nb + 512]
                            if eidx == 0:
                                C.dve(lambda e, xsl=xsl, pdp=pdp: e.tensor_add(
                                    xsl, xsl, pdp), wait=(("pe", dpe),), inc_n=1)
                            else:
                                C.dve(lambda e, pdp=pdp, ns=ns: e.tensor_mul(
                                    ar["fes"][:, :], pdp,
                                    ar["wbb"][:, ns * 512:(ns + 1) * 512]),
                                    wait=(("pe", dpe), ("q_wbb", wb_ld)))
                                C.dve(lambda e, xsl=xsl: e.tensor_add(
                                    xsl, xsl, ar["fes"][:, :]), inc_n=1)
                            hist["d_dve"][dmi % 2] = C.c("dve")

    emit_ffn(0)
    if DEBUG_STOP == 2:
        emit_output(C, M)
        return

    phase_sync()
    xn_done = emit_ln(x, ar["xno"], affine=False)
    xnf_ld = emit_ag(ar["xno"], xn_done)
    C.dma("sync", ar["wproj"][0:64, :].rearrange("p (hh m) -> p hh m", hh=HN),
          wst["projT"].ap().rearrange("hh p m -> p hh m"),
          wait=(attn_w, ("dve", xn_done), ("pe", C.c("pe"))), sem="q_wproj")
    C.dma("sync", ar["wproj"][64:128, :].rearrange("p (hh m) -> p hh m", hh=HN),
          wst["projT"].ap().rearrange("hh p m -> p hh m"), sem="q_wproj")
    wproj_ld = C.c("q_wproj")

    def wqkv_load(hh, guard_pe):
        for i, wname in enumerate(["wqT", "wkT", "wvT"]):
            C.dma("sync", ar["wqkv3"][:, i * 512:(i + 1) * 512].rearrange(
                "p (k c) -> p k c", k=KC),
                wst[wname].ap().rearrange("k p c -> p k c")[
                    :, :, hh * DK:(hh + 1) * DK],
                wait=(("pe", guard_pe), attn_w) if i == 0 else (), sem="q_wqkv")
        return C.c("q_wqkv")

    mg_prev = 0
    for hh in range(HN):
        wq_ld = wqkv_load(hh, C.c("pe"))
        qk_marks = []
        for i in range(2):
            for nt2 in range(8):
                ps = psA[0:DK, (nt2 % 4) * 512:(nt2 % 4) * 512 + 512]
                idx = i * 8 + nt2
                w = []
                if idx == 0:
                    w = [("q_wqkv", wq_ld), ("q_xnf", xnf_ld), ("dve", mg_prev)]
                elif idx >= 4:
                    w = [("dve", qk_marks[idx - 4])]
                for kc in range(KC):
                    C.pe(lambda e, i=i, nt2=nt2, kc=kc, ps=ps: e.matmul(
                        ps, ar["wqkv3"][:, i * 512 + kc * DK: i * 512 + (kc + 1) * DK],
                        xnf[:, kc * T + nt2 * 512: kc * T + nt2 * 512 + 512],
                        start=(kc == 0), stop=(kc == KC - 1)),
                        wait=tuple(w) if kc == 0 else (),
                        inc_n=1 if kc == KC - 1 else 0)
                qpe = C.c("pe")
                dst = (ar["qkh"][0:64, nt2 * 512:(nt2 + 1) * 512] if i == 0
                       else ar["qkh"][64:128, nt2 * 512:(nt2 + 1) * 512])
                C.dve(lambda e, dst=dst, ps=ps: e.tensor_scalar_mul(dst, ps, 0.125),
                      wait=(("pe", qpe),), inc_n=1)
                qk_marks.append(C.c("dve"))
        qk_d = C.c("dve")
        if DEBUG_STOP == 32 and hh == 0:
            emit_output(C, M)
            return
        v_marks = []
        for tc in range(32):
            ps = psB[:, 1024 + (tc % 2) * 512: 1024 + (tc % 2) * 512 + 64]
            w = [("dve", qk_marks[12])] if tc == 0 else (
                [("dve", v_marks[tc - 2])] if tc >= 2 else [])
            for kc in range(KC):
                C.pe(lambda e, tc=tc, kc=kc, ps=ps: e.matmul(
                    ps, xnf[:, kc * T + tc * 128: kc * T + tc * 128 + 128],
                    ar["wqkv3"][:, 2 * 512 + kc * DK: 2 * 512 + (kc + 1) * DK],
                    start=(kc == 0), stop=(kc == KC - 1)),
                    wait=tuple(w) if kc == 0 else (), inc_n=1 if kc == KC - 1 else 0)
            vpe = C.c("pe")
            C.dve(lambda e, tc=tc, ps=ps: e.tensor_copy(
                ar["vb_h"][:, tc * 65: tc * 65 + 64], ps),
                wait=(("pe", vpe),), inc_n=1)
            v_marks.append(C.c("dve"))
        C.dve(lambda e: e.memset(
            ar["vb_h"][:, :].rearrange("p (a b) -> p a b", b=65)[:, :, 64:65], 1.0),
            inc_n=1)
        v_d = C.c("dve")
        if DEBUG_STOP == 33 and hh == 0:
            emit_output(C, M)
            return
        phk_marks = {}
        for tc in range(32):
            ps = psA[:, 1024 + (tc % 2) * 512: 1024 + (tc % 2) * 512 + MF]
            w = [("dve", qk_d)] if tc == 0 else (
                [("dve", phk_marks[tc - 2])] if tc >= 2 else [])
            C.pe(lambda e, tc=tc, hh=hh, ps=ps: e.matmul(
                ps, ar["qkh"][64:128, tc * 128:(tc + 1) * 128],
                ar["wproj"][64:128, hh * MF:(hh + 1) * MF],
                start=True, stop=True),
                wait=tuple(w) + ((("q_wproj", wproj_ld),) if (hh == 0 and tc == 0) else ()),
                inc_n=1)
            xw_pe = C.c("pe")
            C.dve(lambda e, ps=ps: e.reduce_max(sb["colscr"][:, 0:1], ps, AX),
                  wait=(("pe", xw_pe),), inc_n=1)
            C.act(lambda e, tc=tc: e.activation(
                ar["sqs"][:, :], ar["qkh"][64:128, tc * 128:(tc + 1) * 128],
                AF.Square, bias=0.0, scale=1.0),
                wait=(("dve", qk_d), ("pe", C.c("pe"))))
            sq_a = C.c("act")
            C.pe(lambda e, tc=tc: e.transpose(
                psB[:, (tc % 2) * 512:(tc % 2) * 512 + 64], ar["sqs"][:, :],
                sb["ident"][0:64, 0:64]), wait=(("act", sq_a),), inc_n=1)
            tq_pe = C.c("pe")
            C.dve(lambda e, tc=tc: e.reduce_sum(
                sb["colscr"][:, 1:2], psB[:, (tc % 2) * 512:(tc % 2) * 512 + 64], AX),
                wait=(("pe", tq_pe),))
            C.dve(lambda e: e.tensor_scalar(sb["colscr"][:, 1:2],
                                            sb["colscr"][:, 1:2], -0.5, -LN16,
                                            ALU.mult, ALU.add))
            C.dve(lambda e: e.tensor_sub(sb["colscr"][:, 1:2], sb["colscr"][:, 1:2],
                                         sb["colscr"][:, 0:1]), inc_n=1)
            bc_d = C.c("dve")
            C.act(lambda e, ps=ps: e.activation(
                ar["bes2"][:, :], ps, AF.Exp, bias=sb["colscr"][:, 1:2], scale=1.0),
                wait=(("dve", bc_d),))
            ph_a = C.c("act")
            C.dve(lambda e: e.tensor_scalar_add(ar["phk16"][:, :], ar["bes2"][:, :],
                                                EPSB / 16.0),
                  wait=(("act", ph_a),), inc_n=1)
            phk_d = C.c("dve")
            phk_marks[tc] = phk_d
            for mc in range(2):
                C.pe(lambda e, tc=tc, mc=mc: e.matmul(
                    psB[:, 1024 + mc * 512: 1024 + mc * 512 + 65],
                    ar["phk16"][:, mc * 128:(mc + 1) * 128],
                    ar["vb_h"][:, tc * 65:(tc + 1) * 65],
                    start=(tc == 0), stop=(tc == 31)),
                    wait=(("dve", max(phk_d, v_d)),) if mc == 0 else (),
                    inc_n=1 if (tc == 31 and mc == 1) else 0)
        kv_pe = C.c("pe")
        C.dve(lambda e: e.tensor_copy(ar["kvh"][:, 0:65], psB[:, 1024:1089]),
              wait=(("pe", kv_pe),))
        C.dve(lambda e: e.tensor_copy(ar["kvh"][:, 65:130], psB[:, 1536:1601]))
        C.dve(lambda e: e.tensor_copy(ar["kvh16"][:, :], ar["kvh"][:, :]), inc_n=1)
        kv16_d = C.c("dve")
        if DEBUG_STOP == 34 and hh == 0:
            emit_output(C, M)
            return
        q1_marks = {}
        for tc in range(32):
            ps = psA[:, 1024 + (tc % 2) * 512: 1024 + (tc % 2) * 512 + MF]
            w = [("dve", kv16_d)] if tc == 0 else (
                [("dve", q1_marks[tc - 2])] if tc >= 2 else [])
            C.pe(lambda e, tc=tc, hh=hh, ps=ps: e.matmul(
                ps, ar["qkh"][0:64, tc * 128:(tc + 1) * 128],
                ar["wproj"][0:64, hh * MF:(hh + 1) * MF],
                start=True, stop=True), wait=tuple(w), inc_n=1)
            xw_pe = C.c("pe")
            C.dve(lambda e, ps=ps: e.reduce_max(sb["colscr"][:, 0:1], ps, AX),
                  wait=(("pe", xw_pe),), inc_n=1)
            C.act(lambda e, tc=tc: e.activation(
                ar["sqs"][:, :], ar["qkh"][0:64, tc * 128:(tc + 1) * 128],
                AF.Square, bias=0.0, scale=1.0), wait=(("pe", C.c("pe")),))
            sq_a = C.c("act")
            C.pe(lambda e, tc=tc: e.transpose(
                psB[:, (tc % 2) * 512:(tc % 2) * 512 + 64], ar["sqs"][:, :],
                sb["ident"][0:64, 0:64]), wait=(("act", sq_a),), inc_n=1)
            tq_pe = C.c("pe")
            C.dve(lambda e, tc=tc: e.reduce_sum(
                sb["colscr"][:, 1:2], psB[:, (tc % 2) * 512:(tc % 2) * 512 + 64], AX),
                wait=(("pe", tq_pe),))
            C.dve(lambda e: e.tensor_scalar(sb["colscr"][:, 1:2],
                                            sb["colscr"][:, 1:2], -0.5, -LN16,
                                            ALU.mult, ALU.add))
            C.dve(lambda e: e.tensor_sub(sb["colscr"][:, 1:2], sb["colscr"][:, 1:2],
                                         sb["colscr"][:, 0:1]), inc_n=1)
            bq_d = C.c("dve")
            if DEBUG_STOP == 38:
                q1_marks[tc] = C.c("dve")
                continue
            C.pe(lambda e, tc=tc: e.transpose(
                psB[0:1, 1024 + (tc % 2) * 512: 1024 + (tc % 2) * 512 + 128],
                sb["colscr"][:, 1:2], sb["ident"][:, :]),
                wait=(("dve", bq_d),), inc_n=1)
            tb_pe = C.c("pe")
            if DEBUG_STOP == 39:
                q1_marks[tc] = C.c("dve")
                continue
            C.dve(lambda e, tc=tc: e.tensor_copy(
                sb["rowp16"][0:1, tc * 128:(tc + 1) * 128],
                psB[0:1, 1024 + (tc % 2) * 512: 1024 + (tc % 2) * 512 + 128]),
                wait=(("pe", tb_pe),))
            C.dve(lambda e, tc=tc: e.tensor_copy(
                sb["rowpf"][0:1, 0:128],
                sb["rowp16"][0:1, tc * 128:(tc + 1) * 128]))
            if DEBUG_STOP == 40:
                q1_marks[tc] = C.c("dve")
                continue
            C.dve(lambda e, tc=tc: e.tensor_sub(
                sb["rowpf"][0:1, 128:256],
                psB[0:1, 1024 + (tc % 2) * 512: 1024 + (tc % 2) * 512 + 128],
                sb["rowpf"][0:1, 0:128]))
            if DEBUG_STOP == 41:
                q1_marks[tc] = C.c("dve")
                continue
            C.dve(lambda e, tc=tc: e.tensor_copy(
                sb["rowp16"][32:33, tc * 128:(tc + 1) * 128],
                sb["rowpf"][0:1, 128:256]), inc_n=1)
            q1_marks[tc] = C.c("dve")
        brq_d = C.c("dve")
        if DEBUG_STOP in (37, 38, 39, 40, 41, 42, 43) and hh == 0:
            emit_output(C, M)
            return
        C.dma("gpsimd", M["brql"].ap(), sb["rowp16"][32:33, :],
              wait=(("dve", brq_d),), sem="st_nm")
        blq = C.c("st_nm")
        C.dma("sync", sb["rowp16"][1:2, :], APflat(M["brql"], 0, T),
              wait=(("st_nm", blq),), sem="q_nm")
        brq_nm = C.c("q_nm")
        if DEBUG_STOP == 35 and hh == 0:
            emit_output(C, M)
            return
        nd_marks = {}
        for nt2 in range(8):
            for mc in range(2):
                ps = psA[0:128, mc * 512: mc * 512 + 512]
                w = [("dve", brq_d), ("q_nm", brq_nm)] if (nt2 == 0 and mc == 0) else []
                if nt2 >= 1:
                    w.append(("dve", nd_marks.get(nt2 - 1, 0)))
                C.pe(lambda e, nt2=nt2, mc=mc, hh=hh, ps=ps: e.matmul(
                    ps, ar["wproj"][0:64, hh * MF + mc * 128: hh * MF + mc * 128 + 128],
                    ar["qkh"][0:64, nt2 * 512:(nt2 + 1) * 512],
                    start=True, stop=False), wait=tuple(w))
                C.pe(lambda e, nt2=nt2, ps=ps: e.matmul(
                    ps, sb["ones16m"][0:2, :],
                    sb["rowp16"][0:2, nt2 * 512:(nt2 + 1) * 512],
                    start=False, stop=True), inc_n=1)
                p2_pe = C.c("pe")
                C.act(lambda e, mc=mc, ps=ps: e.activation(
                    ar["bes2"][:, 0:MF] if False else ar["lnes"][:, mc * 512:(mc + 1) * 512],
                    ps, AF.Exp, bias=0.0, scale=1.0), wait=(("pe", p2_pe),))
                e_a = C.c("act")
                C.dve(lambda e, mc=mc: e.tensor_scalar_add(
                    ar["phq"][:, mc * 512:(mc + 1) * 512],
                    ar["lnes"][:, mc * 512:(mc + 1) * 512], EPSB / 16.0),
                    wait=(("act", e_a),), inc_n=1)
            phq_d = C.c("dve")
            for mc in range(2):
                C.pe(lambda e, mc=mc: e.matmul(
                    psB[0:65, 1024:1536], ar["kvh16"][:, mc * 65:(mc + 1) * 65],
                    ar["phq"][:, mc * 512:(mc + 1) * 512],
                    start=(mc == 0), stop=(mc == 1)),
                    wait=(("dve", phq_d),) if mc == 0 else (),
                    inc_n=1 if mc == 1 else 0)
            nd_pe = C.c("pe")
            C.dve(lambda e: e.tensor_scalar_add(sb["rowpf"][0:1, 1024:1536],
                                                psB[64:65, 1024:1536], EPSB),
                  wait=(("pe", nd_pe),))
            C.dve(lambda e: e.reciprocal(sb["rowpf"][0:1, 1536:2048],
                                         sb["rowpf"][0:1, 1024:1536]), inc_n=1)
            nd_d = C.c("dve")
            C.pe(lambda e: e.matmul(psB[0:64, 0:512], sb["ones_row64"][:, :],
                                    sb["rowpf"][0:1, 1536:2048],
                                    start=True, stop=True),
                 wait=(("dve", nd_d),), inc_n=1)
            ib_pe = C.c("pe")
            C.dve(lambda e: e.tensor_copy(ar["invb"][:, :], psB[0:64, 0:512]),
                  wait=(("pe", ib_pe),))
            C.dve(lambda e: e.tensor_mul(ar["mgt"][0:64, :], psB[0:64, 1024:1536],
                                         ar["invb"][:, :]),
                  wait=(("st_mg", C.c("st_mg")),), inc_n=1)
            mg_d = C.c("dve")
            nd_marks[nt2] = mg_d
            if DEBUG_STOP == 36 and hh == 0 and nt2 == 0:
                emit_output(C, M)
                return
            C.dma("gpsimd",
                  M["mgst"].ap()[hh // 2][(hh % 2) * 64:(hh % 2) * 64 + 64,
                                          nt2 * 512:(nt2 + 1) * 512],
                  ar["mgt"][0:64, :], wait=(("dve", mg_d),), sem="st_mg")
        mg_prev = C.c("dve")
    mg_store = C.c("st_mg")
    phase_sync()
    C.dma("sync", ar["wout_s"][:, 0:4 * D].rearrange("p (k c) -> p k c", k=4),
          wst["woT"].ap().rearrange("k p c -> p k c"),
          wait=(attn_w, ("pe", C.c("pe"))), sem="q_wo2")
    wo2_ld = C.c("q_wo2")
    ev_prev = 0
    for nt in range(8):
        for kc2 in range(4):
            C.dma("sync", ar["gld"][:, kc2 * 512:(kc2 + 1) * 512],
                  M["mgst"].ap()[kc2][:, nt * 512:(nt + 1) * 512],
                  wait=((("st_mg", mg_store), ("pe", C.c("pe"))) if kc2 == 0
                        else ()), sem="q_gld")
        ml = C.c("q_gld")
        evd = {}
        for dm in range(KC):
            ps = psA[:, (dm % 4) * 512:(dm % 4) * 512 + 512]
            for kc2 in range(4):
                C.pe(lambda e, dm=dm, kc2=kc2, ps=ps: e.matmul(
                    ps, ar["wout_s"][:, kc2 * D + dm * 128: kc2 * D + dm * 128 + 128],
                    ar["gld"][:, kc2 * 512:(kc2 + 1) * 512],
                    start=(kc2 == 0), stop=(kc2 == 3)),
                    wait=(("q_gld", ml), ("q_wo2", wo2_ld),
                          ("dve", evd.get(dm - 4, ev_prev)))
                    if kc2 == 0 else (), inc_n=1 if kc2 == 3 else 0)
            wpe = C.c("pe")
            C.dve(lambda e, dm=dm, ps=ps: e.tensor_copy(
                ar["aotile"][:, dm * 512:(dm + 1) * 512], ps),
                wait=(("pe", wpe), ("st_ao", C.c("st_ao"))), inc_n=1)
            evd[dm] = C.c("dve")
        ev_prev = C.c("dve")
        for dm in range(KC):
            C.dma("gpsimd",
                  M["rs_in"].ap()[nt // 4, dm][:, (nt % 4) * 512:(nt % 4) * 512 + 512],
                  ar["aotile"][:, dm * 512:(dm + 1) * 512],
                  wait=(("dve", ev_prev),) if dm == 0 else (), sem="st_ao")
    rs_w = C.c("st_ao")
    C.op("gpsimd", lambda e: e.collective_compute(
        "ReduceScatter", ALU.add, replica_groups=M["groups"],
        ins=[M["rs_in"].ap().opt()], outs=[M["rs_out"].ap().opt()]),
        wait=(("st_ao", rs_w),), inc=(("cc", 1),))
    rs2_cc = C.c("cc")
    for kc in range(KC):
        sl = slice(kc * TOWN, (kc + 1) * TOWN)
        C.dma("sync", ar["lnes"][:, :], M["rs_out"].ap()[kc],
              wait=((("cc", rs2_cc), ("dve", C.c("dve")))
                    if kc == 0 else (("dve", C.c("dve")),)), sem="q_ao3")
        al = C.c("q_ao3")
        C.dve(lambda e, sl=sl: e.tensor_add(x[:, sl], x[:, sl], ar["lnes"][:, :]),
              wait=(("q_ao3", al),))
        C.dve(lambda e, sl=sl, kc=kc: e.tensor_scalar_add(
            x[:, sl], x[:, sl], sb["bo"][:, kc:kc + 1]), inc_n=1)
    if DEBUG_STOP == 3:
        emit_output(C, M)
        return

    emit_ffn(1)
    emit_output(C, M)


def emit_output(C, M):
    x = M["sb"]["x"]
    w = (("dve", C.c("dve")), ("pe", C.c("pe")), ("act", C.c("act")))
    for kc in range(KC):
        C.dma("gpsimd", M["out_fm"].ap()[kc], x[:, kc * TOWN:(kc + 1) * TOWN],
              wait=w if kc == 0 else (), sem="st_out")
    fin = C.c("st_out")
    C.op("gpsimd", lambda e: e.memset(M["sb"]["colscr"][:, 5:6], 0.0),
         wait=(("st_out", fin),), inc=(("pool", 1),))



def kernel(**inputs):
    f32 = np.float32
    x_full = np.asarray(inputs["L0_attn_a_x"], f32)
    W_in = np.asarray(inputs["L0_attn_a_W_in"], f32)
    b_in = np.asarray(inputs["L0_attn_a_b_in"], f32)
    in_ln_w = np.asarray(inputs["L0_attn_a_in_ln_w"], f32)
    in_ln_b = np.asarray(inputs["L0_attn_a_in_ln_b"], f32)
    W_out = np.asarray(inputs["L0_attn_a_W_out"], f32)
    out_ln_w = np.asarray(inputs["L0_attn_a_out_ln_w"], f32)
    out_ln_b = np.asarray(inputs["L0_attn_a_out_ln_b"], f32)
    smear = np.asarray(inputs["L0_attn_a_smear_factor"], f32)
    lscale = np.asarray(inputs["L0_attn_a_log_scale"], f32)
    Wq = np.asarray(inputs["L0_attn_b_Wq"], f32)
    Wk = np.asarray(inputs["L0_attn_b_Wk"], f32)
    Wv = np.asarray(inputs["L0_attn_b_Wv"], f32)
    Wo = np.asarray(inputs["L0_attn_b_Wo"], f32)
    bo = np.asarray(inputs["L0_attn_b_bo"], f32)
    projw = np.asarray(inputs["L0_attn_b_proj"], f32)
    sg_ = np.asarray(inputs["L0_ffn_shared_gate"], f32)
    su_ = np.asarray(inputs["L0_ffn_shared_up"], f32)
    sd_ = np.asarray(inputs["L0_ffn_shared_down"], f32)
    rg_ = np.asarray(inputs["L0_ffn_routed_gate"], f32)
    ru_ = np.asarray(inputs["L0_ffn_routed_up"], f32)
    rd_ = np.asarray(inputs["L0_ffn_routed_down"], f32)
    Wr = np.asarray(inputs["L0_ffn_Wr"], f32)
    rbias = np.asarray(inputs["L0_ffn_routing_bias"], f32)

    import ml_dtypes
    ac = np.ascontiguousarray
    WinT = W_in.T
    gates = ac(np.concatenate([sg_[None], rg_], 0).reshape(NEXP, KC, 128, HID))
    ups = ac(np.concatenate([su_[None], ru_], 0).reshape(NEXP, KC, 128, HID))
    downs = ac(np.concatenate([sd_[None], rd_], 0).reshape(NEXP, JCN, 128, D))

    ident = np.eye(128, dtype=f32)
    ones_col16 = np.ones((128, 1), dtype=ml_dtypes.bfloat16)
    ones_row16 = np.ones((1, 128), dtype=ml_dtypes.bfloat16)
    ones16m = np.ones((128, 128), dtype=ml_dtypes.bfloat16)
    ones_col64_16 = np.ones((64, 1), dtype=ml_dtypes.bfloat16)
    ones_row64 = np.ones((1, 64), dtype=f32)
    ones_col32 = np.ones((128, 1), dtype=f32)
    tri_loc = np.zeros((128, 129), dtype=f32)
    tri_loc[:, :128] = np.tril(np.ones((128, 128), f32)).T
    tri_loc[:, 128] = 1.0
    tri_off = np.triu(np.ones((32, 32), f32), k=1)
    NEGB = np.float32(-1e9)
    iu = np.arange(896)[None, :]
    ip = np.arange(128)[:, None]
    mask1m = np.where(iu > 384 + ip, NEGB, 0).astype(f32)
    mask2m = np.where(ip > iu - 384, NEGB, 0).astype(f32)

    in_maps = []
    for c in range(8):
        b, r = c // 2, c % 2
        ysl = slice(8192 + 8 * r, 8192 + 8 * r + 8)
        win_own = np.concatenate(
            [WinT[:, blk * 2048 + 1024 * r: blk * 2048 + 1024 * r + 1024]
             for blk in range(4)] + [WinT[:, ysl]], axis=1)
        bqkp = np.stack([
            b_in[0 * 2048 + 1024 * r: 0 * 2048 + 1024 * r + 1024].reshape(8, 128),
            b_in[1 * 2048 + 1024 * r: 1 * 2048 + 1024 * r + 1024].reshape(8, 128),
            b_in[3 * 2048 + 1024 * r: 3 * 2048 + 1024 * r + 1024].reshape(8, 128),
        ], axis=0)
        bqkp = ac(bqkp.reshape(24, 128).T)
        m = dict(
            xT=ac(x_full[b, r * TOWN:(r + 1) * TOWN, :].T.reshape(KC, 128, TOWN)),
            win=ac(win_own.reshape(KC, 128, 4104)),
            bqkp=bqkp,
            bvrow=ac(b_in[2 * 2048 + 1024 * r: 2 * 2048 + 1024 * r + 1024].reshape(8, 128)),
            byrow=ac(b_in[ysl][None, :]),
            lnw=ac(in_ln_w.reshape(KC, 128).T), lnb=ac(in_ln_b.reshape(KC, 128).T),
            olnw=ac(out_ln_w.reshape(KC, 128).T), olnb=ac(out_ln_b.reshape(KC, 128).T),
            woutT=ac(W_out.T[1024 * r:1024 * r + 1024].reshape(KC, 128, D)),
            smear=ac(smear[8 * r:8 * r + 8][None, :]),
            lscale=ac(lscale[8 * r:8 * r + 8][None, :]),
            wqT=ac(Wq.T[:, 512 * r:512 * r + 512].reshape(KC, 128, 512)),
            wkT=ac(Wk.T[:, 512 * r:512 * r + 512].reshape(KC, 128, 512)),
            wvT=ac(Wv.T[:, 512 * r:512 * r + 512].reshape(KC, 128, 512)),
            woT=ac(Wo.T[512 * r:512 * r + 512].reshape(4, 128, D)),
            bo=ac(bo.reshape(KC, 128).T),
            projT=ac(projw[8 * r:8 * r + 8].transpose(0, 2, 1)),
            gates=gates, ups=ups, downs=downs,
            wr=ac(Wr.reshape(KC, 128, 7)), rbias=ac(rbias[None, :]),
            ident=ident, ones_col16=ones_col16, ones_row16=ones_row16,
            ones16m=ones16m,
            ones_col64_16=ones_col64_16, ones_row64=ones_row64,
            ones_col32=ones_col32, tri_loc=tri_loc, tri_off=tri_off,
            mask1m=mask1m, mask2m=mask2m,
        )
        in_maps.append(m)

    nc = build_program()
    res = run_bass_kernel_spmd(nc, in_maps, core_ids=list(range(8)))
    out = np.empty((4, T, D), dtype=np.float32)
    for c in range(8):
        b, r = c // 2, c % 2
        o = res.results[c]["out_fm"]
        out[b, r * TOWN:(r + 1) * TOWN, :] = o.transpose(2, 0, 1).reshape(TOWN, D)
    return out


if __name__ == "__main__":
    import time
    t0 = time.time()
    nc = build_program()
    print(f"build {time.time()-t0:.1f}s n_inst={nc.n_instructions()}")


# version 02
# speedup vs baseline: 1.2355x; 1.2355x over previous
import numpy as np
import concourse.bass as bass
import concourse.mybir as mybir
from concourse.bass_utils import run_bass_kernel_spmd

F32 = mybir.dt.float32
BF16 = mybir.dt.bfloat16
AF = mybir.ActivationFunctionType
ALU = mybir.AluOpType
AX = mybir.AxisListType.X

D = 1024
KC = 8
T = 4096
TOWN = 2048
HN = 8
DK = 64
MF = 256
HID = 2816
JCN = 22
NEXP = 8
LNEPS = 1e-5
LN16 = float(np.log(16.0))
LNS128 = 0.5 * float(np.log(128.0))
EPSB = 1e-6
KB = 1024

DEBUG_STOP = None


def APbc(handle, off, p, n):
    return bass.AP(handle.ap().tensor, off, [[0, p], [1, n]])


def APflat(handle, off, n):
    return bass.AP(handle.ap().tensor, off, [[0, 1], [1, n]])


class Ctx:

    def __init__(self, nc, eng, name, sems):
        self.nc, self.e, self.name, self.sems = nc, eng, name, sems
        self.cnt = {k: 0 for k in sems}
        self._scope = None
        self._sid = None

    def scope(self, name):
        if self._scope is not None:
            self.nc.leave_named_scope(self._scope, self._sid, False)
            self._scope = None
        if name is not None:
            self._sid, _ = self.nc.enter_named_scope(name, False)
            self._scope = name

    def c(self, s):
        return self.cnt[s]

    def op(self, eng, fn, wait=(), inc=()):
        for s, n in inc:
            self.cnt[s] += n
        if eng != self.name:
            return
        for s, v in wait:
            if v > 0:
                self.e.wait_ge(self.sems[s], v)
        if fn is None:
            return
        ins = fn(self.e)
        for s, n in inc:
            ins.then_inc(self.sems[s], n)

    def dma(self, q, out, in_, wait=(), sem=None, **kw):
        assert sem is not None
        self.op(q, lambda e: e.dma_start(out=out, in_=in_, **kw),
                wait=wait, inc=((sem, 16),))
        return self.cnt[sem]

    def pe(self, fn, wait=(), inc_n=0):
        self.op("tensor", fn, wait=wait, inc=((("pe", inc_n),) if inc_n else ()))
        return self.cnt["pe"]

    def dve(self, fn, wait=(), inc_n=0):
        w = tuple(wait) + (("dve", self.cnt["dve"]),)
        self.op("vector", fn, wait=w, inc=(("dve", 1),))
        return self.cnt["dve"]

    def act(self, fn, wait=(), inc_n=1):
        w = tuple(wait) + (("act", self.cnt["act"]),)
        self.op("scalar", fn, wait=w, inc=(("act", 1),))
        return self.cnt["act"]

    def pool(self, fn, wait=(), inc_n=1):
        w = tuple(wait) + (("pool", self.cnt["pool"]),)
        self.op("gpsimd", fn, wait=w, inc=(("pool", 1),))
        return self.cnt["pool"]


def build_program():
    nc = bass.Bass("TRN2", target_bir_lowering=False)

    def din(name, shape, dt=F32):
        return nc.dram_tensor(name, list(shape), dt, kind="ExternalInput")

    h = {}
    h["xT"] = din("xT", [KC, 128, TOWN])
    h["win"] = din("win", [KC, 128, 4104])
    h["bqkp"] = din("bqkp", [128, 3 * HN])
    h["bvrow"] = din("bvrow", [HN, 128])
    h["byrow"] = din("byrow", [1, HN])
    h["lnw"] = din("lnw", [128, KC]); h["lnb"] = din("lnb", [128, KC])
    h["olnw"] = din("olnw", [128, KC]); h["olnb"] = din("olnb", [128, KC])
    h["woutT"] = din("woutT", [KC, 128, D])
    h["smear"] = din("smear", [1, HN])
    h["lscale"] = din("lscale", [1, HN])
    h["wqT"] = din("wqT", [KC, 128, 512])
    h["wkT"] = din("wkT", [KC, 128, 512])
    h["wvT"] = din("wvT", [KC, 128, 512])
    h["woT"] = din("woT", [4, 128, D])
    h["bo"] = din("bo", [128, KC])
    h["projT"] = din("projT", [HN, DK, MF])
    h["gates"] = din("gates", [NEXP, KC, 128, HID])
    h["ups"] = din("ups", [NEXP, KC, 128, HID])
    h["downs"] = din("downs", [NEXP, JCN, 128, D])
    h["wr"] = din("wr", [KC, 128, 7])
    h["rbias"] = din("rbias", [1, 7])
    h["ident"] = din("ident", [128, 128])
    h["ones_col16"] = din("ones_col16", [128, 1], BF16)
    h["ones_row16"] = din("ones_row16", [1, 128], BF16)
    h["ones16m"] = din("ones16m", [128, 128], BF16)
    h["ones_col64_16"] = din("ones_col64_16", [64, 1], BF16)
    h["ones_row64"] = din("ones_row64", [1, 64])
    h["ones_col32"] = din("ones_col32", [128, 1])
    h["tri_loc"] = din("tri_loc", [128, 129])
    h["tri_off"] = din("tri_off", [32, 32])
    h["mask1m"] = din("mask1m", [128, 896])
    h["mask2m"] = din("mask2m", [128, 896])

    out_fm = nc.dram_tensor("out_fm", [KC, 128, TOWN], F32, kind="ExternalOutput")

    def dint(name, shape, dt):
        return nc.dram_tensor(name, list(shape), dt, kind="Internal")

    wst = {k: dint("wst_" + k, shp, BF16) for k, shp in [
        ("win", [KC, 128, 4104]), ("woutT", [KC, 128, D]),
        ("wqT", [KC, 128, 512]), ("wkT", [KC, 128, 512]),
        ("wvT", [KC, 128, 512]), ("woT", [4, 128, D]),
        ("projT", [HN, DK, MF]), ("gates", [NEXP, KC, 128, HID]),
        ("ups", [NEXP, KC, 128, HID]), ("downs", [NEXP, JCN, 128, D])]}
    M = dict(nc=nc, h=h, wst=wst, out_fm=out_fm)
    M["ag_in"] = dint("ag_in", [KC, 128, TOWN], BF16)
    M["ag_out"] = dint("ag_out", [2 * KC, 128, TOWN], BF16)
    M["rs_in"] = dint("rs_in", [2, KC, 128, TOWN], F32)
    M["rs_out"] = dint("rs_out", [KC, 128, TOWN], F32)
    M["qst"] = dint("qst", [HN, 128, T], BF16)
    M["kst"] = dint("kst", [HN, 128, T], BF16)
    M["vst"] = dint("vst", [HN, 128, T], BF16)
    M["spst"] = dint("spst", [HN, 128, T], BF16)
    M["gst"] = dint("gst", [HN, 128, T], BF16)
    M["mgst"] = dint("mgst", [4, 128, T], BF16)
    M["rowd"] = dint("rowd", [2, TOWN], F32)
    M["lnrows"] = dint("lnrows", [3, TOWN], F32)
    M["poshd"] = dint("poshd", [HN, T], BF16)
    M["posld"] = dint("posld", [HN, T], BF16)
    M["mrowd"] = dint("mrowd", [32, 128], BF16)
    M["brql"] = dint("brql", [1, T], BF16)
    M["linvd"] = dint("linvd", [HN, T], F32)
    M["wrowd"] = dint("wrowd", [7, TOWN], F32)
    M["prmd"] = dint("prmd", [1, 3 * HN], F32)
    M["groups"] = [[0, 1], [2, 3], [4, 5], [6, 7]]

    sb = {}

    def asb(name, shape, dt=F32):
        sb[name] = nc.alloc_sbuf_tensor("s_" + name, list(shape), dt)
        return sb[name]

    asb("x", [128, KC * TOWN])
    asb("ident", [128, 128])
    asb("ones_col16", [128, 1], BF16)
    asb("ones_row16", [1, 128], BF16)
    asb("ones16m", [128, 128], BF16)
    asb("ones_col64_16", [64, 1], BF16)
    asb("ones_row64", [1, 64])
    asb("ones_col32", [128, 1])
    asb("tri_loc", [128, 129])
    asb("tri_off", [32, 32])
    asb("lnw", [128, KC]); asb("lnb", [128, KC])
    asb("olnw", [128, KC]); asb("olnb", [128, KC])
    asb("bo", [128, KC])
    asb("bqkp", [128, 3 * HN])
    asb("pcol", [128, 3 * HN])
    asb("prows", [1, 64])
    asb("ywin", [128, KC * HN], BF16)
    asb("ybb", [128, HN])
    asb("vbb", [128, 128])
    asb("sigy", [128, 32 * HN])
    asb("pos_t", [128, 32 * HN])
    asb("negm32", [128, 32])
    asb("linv", [128, 32])
    asb("rowpf", [128, 2048])
    asb("lncol", [128, 128])
    asb("rowp16", [128, T], BF16)
    asb("rcols", [128, 16])
    asb("rbb", [128, 7])
    asb("colscr", [128, 24])
    asb("colscr3", [128, 8], BF16)
    asb("wt7", [7, 16 * 128])

    ARENA_B = 112 * KB
    arena = nc.alloc_sbuf_tensor("arena", [128, ARENA_B], mybir.dt.uint8)
    abase = nc.lookup_mloc(arena).addr
    ar = {}

    def av(name, off, shape, dt=F32):
        t = nc.alloc_sbuf_tensor_at("a_" + name, list(shape), dt, offset=abase + off)
        nb = int(np.prod(shape[1:])) * mybir.dt.size(dt)
        assert off + nb <= ARENA_B, (name, off, nb)
        ar[name] = t
        return t

    av("xnf", 0, [128, KC * T], BF16)
    av("xno", 0, [128, KC * TOWN], BF16)
    av("lncb", 64 * KB, [128, 2 * TOWN], BF16)
    av("lncq32", 64 * KB, [128, TOWN])
    av("lncq", 72 * KB, [128, 2 * TOWN], BF16)
    av("Rb", 80 * KB, [128, TOWN])
    av("Bb", 88 * KB, [128, TOWN])
    av("lnes", 96 * KB, [128, TOWN])
    av("lnes2", 64 * KB, [128, TOWN])
    av("lnrt", 104 * KB, [48, 128])
    av("posfm", 64 * KB, [8, T])
    av("poshi16", 80 * KB, [8, T], BF16)
    av("poslo16", 88 * KB, [8, T], BF16)
    av("hi32h", 96 * KB, [8, TOWN])
    av("winblk", 64 * KB, [128, 2 * KC * 128], BF16)
    av("stg16", 68 * KB, [128, T], BF16)
    av("ksc", 76 * KB, [128, T], BF16)
    av("qh", 0, [128, 2 * T], BF16)
    av("kh", 16 * KB, [128, 2 * T], BF16)
    av("vh", 32 * KB, [128, 2 * T], BF16)
    av("sph", 48 * KB, [128, 2 * T], BF16)
    av("pts", 64 * KB, [128, 2 * 512], BF16)
    av("pthrow", 66 * KB, [128, 2048], BF16)
    av("esmall", 70 * KB, [128, 512])
    av("gtile", 72 * KB, [128, 512], BF16)
    av("invlb", 73 * KB, [128, 512])
    av("nm32t", 75 * KB, [32, 128])
    av("nm16t", 75 * KB + 512, [32, 128], BF16)
    av("masks1", 76 * KB, [128, 896])
    av("masks2", 76 * KB + 3584, [128, 896])
    av("wout_s", 0, [128, KC * D], BF16)
    av("gld", 16 * KB, [128, HN * 512], BF16)
    av("aotile", 24 * KB, [128, KC * 512])
    av("hbuf", 32 * KB, [128, JCN * 1024], BF16)
    av("dquart", 76 * KB, [128, JCN * 256], BF16)
    av("gring", 87 * KB, [128, 4 * 1024], BF16)
    av("uring", 95 * KB, [128, 4 * 1024], BF16)
    av("wbb", 103 * KB, [128, 1024])
    av("sg", 107 * KB, [128, 1024])
    av("fes", 107 * KB, [128, 512])
    av("ltile", 111 * KB, [128, 8])
    av("ptile", 111 * KB + 64, [128, 8])
    av("wr32", 111 * KB + 128, [128, KC * 7])
    av("wqkv3", 64 * KB, [128, 3 * 512], BF16)
    av("wproj", 67 * KB, [128, HN * MF], BF16)
    av("qkh", 71 * KB, [128, T], BF16)
    av("vb_h", 79 * KB, [128, 32 * 65], BF16)
    av("phk16", 83 * KB + 512, [128, MF], BF16)
    av("sqs", 84 * KB, [64, 128])
    av("bes2", 84 * KB + 512, [128, MF])
    av("kvh", 85 * KB + 512, [128, 2 * 65])
    av("kvh16", 86 * KB + 64, [128, 2 * 65], BF16)
    av("phq", 86 * KB + 512, [128, 2 * 512], BF16)
    av("invb", 88 * KB + 512, [64, 512])
    av("mgt", 90 * KB + 512, [128, 512], BF16)
    M["sb"], M["ar"] = sb, ar

    M["psA"] = nc.alloc_psum_tensor("psA", [128, 2048], F32)
    M["psB"] = nc.alloc_psum_tensor("psB", [128, 2048], F32)

    semnames = ["q_p0", "q_prm", "q_ywin", "q_xnf", "q_win0", "q_win1",
                "q_mask", "q_hl0", "q_hl1", "q_rp", "q_nm", "q_gld", "q_wout",
                "q_wo2", "q_wr", "q_wqkv", "q_wproj", "q_mg", "q_dh",
                "q_g0", "q_g1", "q_g2", "q_g3", "q_u0", "q_u1", "q_u2", "q_u3",
                "q_cvi", "q_cvoA", "q_cvoF",
                "q_pcol", "q_ybb", "q_vbb", "q_RB", "q_inv", "q_wbb", "q_rbb",
                "st_prm", "st_lnr", "st_pos", "st_stg", "st_nm", "st_g",
                "st_ao", "st_wrow", "st_mg", "st_out", "st_ag", "q_ao3",
                "pe", "dve", "act", "pool", "cc"]
    sems = {s: nc.alloc_semaphore(name=s) for s in semnames}
    M["sems"] = sems

    with nc.Block() as block:
        @block.sync
        def _(e):
            _C = Ctx(nc, e, "sync", sems)
            emit_program(_C, M)
            _C.scope(None)

        @block.tensor
        def _(e):
            _C = Ctx(nc, e, "tensor", sems)
            emit_program(_C, M)
            _C.scope(None)

        @block.vector
        def _(e):
            _C = Ctx(nc, e, "vector", sems)
            emit_program(_C, M)
            _C.scope(None)

        @block.scalar
        def _(e):
            _C = Ctx(nc, e, "scalar", sems)
            emit_program(_C, M)
            _C.scope(None)

        @block.gpsimd
        def _(e):
            _C = Ctx(nc, e, "gpsimd", sems)
            emit_program(_C, M)
            _C.scope(None)

    return nc


def emit_program(C, M):
    h, wst, sb, ar = M["h"], M["wst"], M["sb"], M["ar"]
    psA, psB = M["psA"], M["psB"]
    x = sb["x"]
    xnf = ar["xnf"]

    def phase_sync():
        w = tuple((s, C.c(s)) for s in ("pe", "dve", "act"))
        for en in ("sync", "tensor", "vector", "scalar", "gpsimd"):
            C.op(en, None, wait=w)

    C.scope("P0_load")
    for kc in range(KC):
        C.dma("sync", x[:, kc * TOWN:(kc + 1) * TOWN], h["xT"].ap()[kc], sem="q_p0")
    for name in ["ident", "ones_col16", "ones_row16", "ones16m", "ones_col64_16",
                 "ones_row64", "ones_col32", "tri_loc", "tri_off",
                 "lnw", "lnb", "olnw", "olnb", "bo", "bqkp"]:
        C.dma("sync", sb[name][:, :], h[name].ap(), sem="q_p0")
    x_ld = C.c("q_p0")

    conv_jobs = [(k, None, "q_cvoA") for k in
                 ["win", "woutT", "wqT", "wkT", "wvT", "woT", "projT"]]
    N_ATTN_JOBS = len(conv_jobs)
    for k in ["gates", "ups", "downs"]:
        for o in range(NEXP):
            conv_jobs.append((k, o, "q_cvoF"))
    cs = dict(i=0)

    def conv_pump(n):
        for _ in range(n):
            i = cs["i"]
            if i >= len(conv_jobs):
                return
            cs["i"] = i + 1
            dkey, o, osem = conv_jobs[i]
            sv, dv = h[dkey].ap(), wst[dkey].ap()
            if o is None:
                C.dma("gpsimd", dv, sv, sem=osem)
            else:
                C.dma("gpsimd", dv[o], sv[o], sem=osem)

    conv_pump(N_ATTN_JOBS)
    attn_w = ("q_cvoA", C.c("q_cvoA"))
    if DEBUG_STOP == 11:
        emit_output(C, M)
        return

    C.dma("sync", sb["prows"][0:1, 0:HN], h["lscale"].ap(), sem="q_prm")
    C.dma("sync", sb["prows"][0:1, 8:16], h["smear"].ap(), sem="q_prm")
    ldp = C.c("q_prm")
    C.dve(lambda e: e.tensor_scalar(sb["prows"][0:1, 48:56], sb["prows"][0:1, 0:8],
                                    -2.0, -LNS128, ALU.mult, ALU.add),
          wait=(("q_prm", ldp),), inc_n=1)
    C.act(lambda e: e.activation(sb["prows"][0:1, 16:24], sb["prows"][0:1, 48:56],
                                 AF.Exp, bias=0.0, scale=1.0),
          wait=(("dve", C.c("dve")),))
    C.act(lambda e: e.activation(sb["prows"][0:1, 24:32], sb["prows"][0:1, 8:16],
                                 AF.Sigmoid, bias=0.0, scale=1.0))
    pa = C.c("act")
    C.dve(lambda e: e.tensor_scalar(sb["prows"][0:1, 32:40], sb["prows"][0:1, 24:32],
                                    -1.0, 1.0, ALU.mult, ALU.add),
          wait=(("act", pa),), inc_n=1)
    pd = C.c("dve")
    C.dma("gpsimd", M["prmd"].ap()[0:1, :], sb["prows"][0:1, 16:40],
          wait=(("dve", pd),), sem="st_prm")
    g0 = C.c("st_prm")
    C.dma("scalar", sb["pcol"][:, :], APbc(M["prmd"], 0, 128, 24),
          wait=(("st_prm", g0),), sem="q_pcol")
    pcol_ld = C.c("q_pcol")

    def ccol(hh):
        return sb["pcol"][:, hh:hh + 1]

    def smcol(hh):
        return sb["pcol"][:, 8 + hh:9 + hh]

    def sm1col(hh):
        return sb["pcol"][:, 16 + hh:17 + hh]

    def emit_ln(src, out16, affine=False, fused_double=False, wcol=None,
                bcol=None, router_rows=False, src_wait=()):
        cq = ar["lncq32"]
        Rb, Bb = ar["Rb"], ar["Bb"]
        mucol = sb["lncol"][:, 0:16]
        vcol = sb["lncol"][:, 16:32]
        Rcol = sb["lncol"][:, 32:48]
        Bcol = sb["lncol"][:, 48:64]
        acc_s = sb["lncol"][:, 96:112]
        acc_q = sb["lncol"][:, 112:128]
        mm_marks = []
        for kc in range(KC):
            sl = slice(kc * TOWN, (kc + 1) * TOWN)
            w = list(src_wait) if kc == 0 else []
            if kc == 0:
                w.append(("pe", C.c("pe")))
                w.append(("dve", C.c("dve")))
            if kc >= 1:
                w.append(("pe", mm_marks[kc - 1][1]))
            C.act(lambda e, sl=sl: e.activation(cq[:, :], src[:, sl], AF.Square,
                                                bias=0.0, scale=1.0),
                  wait=tuple(w))
            ca = C.c("act")
            for tg in range(16):
                C.pe(lambda e, tg=tg, sl=sl: e.matmul(
                    psA[:, 1536 + tg:1536 + tg + 1],
                    src[:, sl.start + tg * 128: sl.start + tg * 128 + 128],
                    sb["ones_col32"][:, :], start=True, stop=True),
                    wait=((("act", ca),) +
                          ((("dve", mm_marks[kc - 1][0]),) if kc >= 1 else ())
                          if tg == 0 else ()))
                C.pe(lambda e, tg=tg: e.matmul(
                    psB[:, 1536 + tg:1536 + tg + 1],
                    cq[:, tg * 128:(tg + 1) * 128],
                    sb["ones_col32"][:, :], start=True, stop=True),
                    inc_n=1 if tg == 15 else 0)
            mmk = C.c("pe")
            if kc == 0:
                C.dve(lambda e: e.tensor_copy(acc_s, psA[:, 1536:1552]),
                      wait=(("pe", mmk),))
                C.dve(lambda e: e.tensor_copy(acc_q, psB[:, 1536:1552]), inc_n=1)
            else:
                C.dve(lambda e: e.tensor_add(acc_s, acc_s, psA[:, 1536:1552]),
                      wait=(("pe", mmk),))
                C.dve(lambda e: e.tensor_add(acc_q, acc_q, psB[:, 1536:1552]),
                      inc_n=1)
            mm_marks.append((C.c("dve"), mmk))
        C.dve(lambda e: e.tensor_scalar_mul(mucol, acc_s, 1.0 / D))
        C.dve(lambda e: e.tensor_scalar_mul(vcol, acc_q, 1.0 / D))
        C.dve(lambda e: e.tensor_mul(Bcol, mucol, mucol))
        C.dve(lambda e: e.tensor_sub(vcol, vcol, Bcol))
        eb, sc = (LNEPS * LNEPS, 1.0 + LNEPS) if fused_double else (LNEPS, 1.0)
        C.dve(lambda e, eb=eb, sc=sc: e.tensor_scalar(vcol, vcol, sc, eb,
                                                      ALU.mult, ALU.add), inc_n=1)
        var_d = C.c("dve")
        C.act(lambda e: e.activation(Bcol, vcol, AF.Ln, bias=0.0, scale=1.0),
              wait=(("dve", var_d),))
        C.act(lambda e: e.activation(Rcol, Bcol, AF.Exp, bias=0.0, scale=-0.5))
        r_a = C.c("act")
        C.dve(lambda e: e.tensor_mul(Bcol, mucol, Rcol), wait=(("act", r_a),))
        C.dve(lambda e: e.tensor_scalar_mul(Bcol, Bcol, -1.0))
        C.dve(lambda e: e.tensor_scalar_mul(sb["lncol"][:, 64:80], mucol, -1.0))
        if router_rows:
            C.dve(lambda e: e.tensor_copy(sb["rcols"][:, :], Rcol))
        rb_d = C.c("dve")
        C.pe(lambda e: e.transpose(psA[0:48, 0:128], sb["lncol"][:, 32 if not router_rows else 32:32], sb["ident"][:, :])
             if False else e.transpose(psA[0:48, 0:128], sb["lncol"][:, 32:80],
                                       sb["ident"][:, :]),
             wait=(("dve", rb_d),), inc_n=1)
        tr_pe = C.c("pe")
        C.dve(lambda e: e.tensor_copy(ar["lnrt"][0:48, :], psA[0:48, 0:128]),
              wait=(("pe", tr_pe),), inc_n=1)
        ev_d = C.c("dve")
        C.dma("gpsimd", M["lnrows"].ap(), ar["lnrt"][0:48, :],
              wait=(("dve", ev_d),), sem="st_lnr")
        g1 = C.c("st_lnr")
        C.dma("scalar", Rb[:, :], APbc(M["lnrows"], 0, 128, TOWN),
              wait=(("st_lnr", g1), ("dve", C.c("dve"))), sem="q_RB")
        C.dma("scalar", Bb[:, :], APbc(M["lnrows"], TOWN, 128, TOWN), sem="q_RB")
        if router_rows:
            C.dma("scalar", sb["rowpf"][0:1, :], APflat(M["lnrows"], 2 * TOWN, TOWN),
                  sem="q_RB")
        bc_v = C.c("q_RB")
        esf = ar["lnes"]
        for kc in range(KC):
            sl = slice(kc * TOWN, (kc + 1) * TOWN)
            C.dve(lambda e, sl=sl: e.tensor_mul(esf[:, :], src[:, sl], Rb[:, :]),
                  wait=(("q_RB", bc_v),) if kc == 0 else ())
            if affine:
                C.dve(lambda e: e.tensor_add(esf[:, :], esf[:, :], Bb[:, :]))
                C.dve(lambda e, sl=sl, kc=kc: e.tensor_scalar(
                    out16[:, sl], esf[:, :], wcol[:, kc:kc + 1],
                    bcol[:, kc:kc + 1], ALU.mult, ALU.add), inc_n=1)
            else:
                C.dve(lambda e, sl=sl: e.tensor_add(out16[:, sl], esf[:, :],
                                                    Bb[:, :]), inc_n=1)
        return C.c("dve")

    def emit_ag(out16own, xn_done):
        for kc in range(KC):
            C.dma("gpsimd", M["ag_in"].ap()[kc],
                  out16own[:, kc * TOWN:(kc + 1) * TOWN],
                  wait=(("dve", xn_done),) if kc == 0 else (), sem="st_ag")
        agw = C.c("st_ag")
        C.op("gpsimd", lambda e: e.collective_compute(
            "AllGather", ALU.bypass, replica_groups=M["groups"],
            ins=[M["ag_in"].ap().opt()], outs=[M["ag_out"].ap().opt()]),
            wait=(("st_ag", agw),), inc=(("cc", 1),))
        ccv = C.c("cc")
        for kc in range(KC):
            for rk in range(2):
                C.dma("sync", xnf[:, kc * T + rk * TOWN: kc * T + (rk + 1) * TOWN],
                      M["ag_out"].ap()[rk * KC + kc],
                      wait=(("cc", ccv), ("pe", C.c("pe"))) if (kc == 0 and rk == 0)
                      else (), sem="q_xnf")
        return C.c("q_xnf")

    C.scope("P1_ln")
    xn_done = emit_ln(x, ar["xno"], affine=True, fused_double=True,
                      wcol=sb["lnw"], bcol=sb["lnb"],
                      src_wait=(("q_p0", x_ld),))
    xnf_ld = emit_ag(ar["xno"], xn_done)
    if DEBUG_STOP == 12:
        emit_output(C, M)
        return

    phase_sync()
    C.scope("P2_pos")
    C.dma("sync", sb["ywin"][:, :].rearrange("p (k c) -> p k c", k=KC),
          wst["win"].ap().rearrange("k p c -> p k c")[:, :, 4096:4104],
          wait=(attn_w,), sem="q_ywin")
    yld = C.c("q_ywin")
    C.dma("scalar", sb["ybb"][:, :], APbc(h["byrow"], 0, 128, HN), sem="q_ybb")
    ybld = C.c("q_ybb")
    y_marks = []
    for tc in range(32):
        ps = psA[:, 0:HN] if tc % 2 == 0 else psA[:, 512:512 + HN]
        if tc == 0:
            w = [("q_ywin", yld), ("q_xnf", xnf_ld), ("q_ybb", ybld)]
        elif tc >= 2:
            w = [("act", y_marks[tc - 2])]
        else:
            w = []
        for kc in range(KC):
            C.pe(lambda e, tc=tc, kc=kc, ps=ps: e.matmul(
                ps, xnf[:, kc * T + tc * 128: kc * T + tc * 128 + 128],
                sb["ywin"][:, kc * HN:(kc + 1) * HN],
                start=(kc == 0), stop=(kc == KC - 1)),
                wait=tuple(w) if kc == 0 else (), inc_n=1 if kc == KC - 1 else 0)
        ypet = C.c("pe")
        C.dve(lambda e, ps=ps: e.tensor_add(ar["lnes"][:, 0:HN], ps, sb["ybb"][:, :]),
              wait=(("pe", ypet), ("act", C.c("act"))), inc_n=1)
        ydv = C.c("dve")
        C.act(lambda e, tc=tc: e.activation(sb["sigy"][:, tc * HN:(tc + 1) * HN],
                                            ar["lnes"][:, 0:HN], AF.Sigmoid,
                                            bias=0.0, scale=1.0),
              wait=(("dve", ydv),))
        y_marks.append(C.c("act"))
    posfm = ar["posfm"]
    tot_marks = []
    for tc in range(32):
        w = [("act", y_marks[-1])] if tc == 0 else []
        if tc >= 4:
            w.append(("dve", tot_marks[tc - 4]))
        C.pe(lambda e, tc=tc: e.matmul(
            psB[0:HN, (tc % 4) * 512:(tc % 4) * 512 + 129],
            sb["sigy"][:, tc * HN:(tc + 1) * HN], sb["tri_loc"][:, :],
            start=True, stop=True), wait=tuple(w), inc_n=1)
        tpe = C.c("pe")
        C.dve(lambda e, tc=tc: e.tensor_copy(
            posfm[:, tc * 128:(tc + 1) * 128],
            psB[0:HN, (tc % 4) * 512:(tc % 4) * 512 + 128]),
            wait=(("pe", tpe),))
        C.dve(lambda e, tc=tc: e.tensor_copy(
            sb["pos_t"][0:HN, tc:tc + 1],
            psB[0:HN, (tc % 4) * 512 + 128:(tc % 4) * 512 + 129]), inc_n=1)
        tot_marks.append(C.c("dve"))
    C.pe(lambda e: e.transpose(psA[0:32, 0:HN], sb["pos_t"][0:HN, 0:32],
                               sb["ident"][0:HN, 0:HN]),
         wait=(("dve", tot_marks[-1]),), inc_n=1)
    C.dve(lambda e: e.tensor_copy(ar["lnes"][0:32, 0:HN], psA[0:32, 0:HN]),
          wait=(("pe", C.c("pe")),), inc_n=1)
    C.pe(lambda e: e.matmul(psA[0:HN, 512:512 + 32], ar["lnes"][0:32, 0:HN],
                            sb["tri_off"][:, :], start=True, stop=True),
         wait=(("dve", C.c("dve")),), inc_n=1)
    C.dve(lambda e: e.tensor_copy(ar["lnes"][0:HN, 0:32], psA[0:HN, 512:512 + 32]),
          wait=(("pe", C.c("pe")),), inc_n=1)
    for tc in range(32):
        C.dve(lambda e, tc=tc: e.tensor_scalar_add(
            posfm[:, tc * 128:(tc + 1) * 128], posfm[:, tc * 128:(tc + 1) * 128],
            ar["lnes"][0:HN, tc:tc + 1]), inc_n=(1 if tc == 31 else 0))
    hi16, lo16, hi32 = ar["poshi16"], ar["poslo16"], ar["hi32h"]
    for hf in range(2):
        sl = slice(hf * TOWN, (hf + 1) * TOWN)
        C.dve(lambda e, sl=sl: e.tensor_copy(hi16[:, sl], posfm[:, sl]))
        C.dve(lambda e, sl=sl: e.tensor_copy(hi32[:, :], hi16[:, sl]))
        C.dve(lambda e, sl=sl: e.tensor_sub(posfm[:, sl], posfm[:, sl], hi32[:, :]))
        C.dve(lambda e, sl=sl: e.tensor_copy(lo16[:, sl], posfm[:, sl]))
        C.dve(lambda e, sl=sl: e.tensor_copy(posfm[:, sl], lo16[:, sl]))
        C.dve(lambda e, sl=sl: e.tensor_add(posfm[:, sl], posfm[:, sl], hi32[:, :]),
              inc_n=1)
    pos_d = C.c("dve")
    C.dma("gpsimd", M["poshd"].ap(), hi16[0:HN, :], wait=(("dve", pos_d),), sem="st_pos")
    C.dma("gpsimd", M["posld"].ap(), lo16[0:HN, :], sem="st_pos")
    pos_st = C.c("st_pos")
    for jc in range(32):
        ps = psB[:, 0:HN] if jc % 2 == 0 else psB[:, 512:512 + HN]
        C.pe(lambda e, jc=jc, ps=ps: e.transpose(
            ps, posfm[0:HN, jc * 128:(jc + 1) * 128], sb["ident"][0:HN, 0:HN]),
            wait=(("dve", pos_d if jc < 2 else C.c("dve")),), inc_n=1)
        tp = C.c("pe")
        C.dve(lambda e, jc=jc, ps=ps: e.tensor_copy(
            sb["pos_t"][:, jc * HN:(jc + 1) * HN], ps),
            wait=(("pe", tp),), inc_n=1)

    if DEBUG_STOP == 13:
        emit_output(C, M)
        return

    phase_sync()
    C.scope("P3_proj")

    def win_load(job, bi):
        hh, blk = job // 4, job % 4
        wsem = "q_win0" if bi == 0 else "q_win1"
        C.dma("sync", ar["winblk"][:, bi * 1024:(bi + 1) * 1024].rearrange(
            "p (k c) -> p k c", k=KC),
            wst["win"].ap().rearrange("k p c -> p k c")[
                :, :, blk * 1024 + hh * 128: blk * 1024 + hh * 128 + 128],
            wait=(attn_w, ("pe", C.c("pe"))), sem=wsem)
        return (wsem, C.c(wsem))

    wl = {0: win_load(0, 0)}
    stg_free = 0
    for job in range(HN * 4):
        hh, blk = job // 4, job % 4
        bi = job % 2
        if job + 1 < HN * 4:
            wl[job + 1] = win_load(job + 1, (job + 1) % 2)
        wld = wl[job]
        evac_marks = []
        if blk == 2:
            C.dma("scalar", sb["vbb"][:, :], APbc(h["bvrow"], hh * 128, 128, 128),
                  wait=(("dve", C.c("dve")),), sem="q_vbb")
            vbld = C.c("q_vbb")
            for tc in range(32):
                ps = psB[:, (tc % 4) * 512:(tc % 4) * 512 + 128]
                w = []
                if tc == 0:
                    w = [wld, ("dve", C.c("dve")), ("act", C.c("act"))]
                elif tc >= 4:
                    w = [("dve", evac_marks[tc - 4])]
                for kc in range(KC):
                    C.pe(lambda e, tc=tc, kc=kc, ps=ps, bi=bi: e.matmul(
                        ps, xnf[:, kc * T + tc * 128: kc * T + tc * 128 + 128],
                        ar["winblk"][:, bi * 1024 + kc * 128: bi * 1024 + kc * 128 + 128],
                        start=(kc == 0), stop=(kc == KC - 1)),
                        wait=tuple(w) if kc == 0 else (),
                        inc_n=1 if kc == KC - 1 else 0)
                vpe = C.c("pe")
                C.dve(lambda e, tc=tc, ps=ps: e.tensor_add(
                    ar["stg16"][:, tc * 128:(tc + 1) * 128], ps, sb["vbb"][:, :]),
                    wait=(("pe", vpe), ("q_vbb", vbld)) +
                    ((("st_stg", stg_free),) if tc == 0 else ()), inc_n=1)
                evac_marks.append(C.c("dve"))
            C.dma("gpsimd", M["vst"].ap()[hh], ar["stg16"][:, :],
                  wait=(("dve", C.c("dve")),), sem="st_stg")
            stg_free = C.c("st_stg")
        else:
            for nt in range(8):
                ps = psB[:, (nt % 4) * 512:(nt % 4) * 512 + 512]
                w = []
                if nt == 0:
                    w = [wld, ("q_pcol", pcol_ld), ("dve", C.c("dve")),
                         ("act", C.c("act"))]
                elif nt >= 4:
                    w = [("dve" if blk != 3 else "act", evac_marks[nt - 4])]
                for kc in range(KC):
                    C.pe(lambda e, nt=nt, kc=kc, ps=ps, bi=bi: e.matmul(
                        ps, ar["winblk"][:, bi * 1024 + kc * 128: bi * 1024 + kc * 128 + 128],
                        xnf[:, kc * T + nt * 512: kc * T + nt * 512 + 512],
                        start=(kc == 0), stop=(kc == KC - 1)),
                        wait=tuple(w) if kc == 0 else (),
                        inc_n=1 if kc == KC - 1 else 0)
                ppe = C.c("pe")
                bix = (0 if blk == 0 else (1 if blk == 1 else 2)) * HN + hh
                bqs = sb["bqkp"][:, bix:bix + 1]
                osl = slice(nt * 512, (nt + 1) * 512)
                ww = (("pe", ppe),) + ((("st_stg", stg_free),) if nt == 0 else ())
                if blk == 0:
                    C.dve(lambda e, ps=ps, bqs=bqs, osl=osl, hh=hh: e.tensor_scalar(
                        ar["stg16"][:, osl], ps, bqs, ccol(hh), ALU.add, ALU.mult),
                        wait=ww, inc_n=1)
                    evac_marks.append(C.c("dve"))
                elif blk == 1:
                    C.dve(lambda e, ps=ps, bqs=bqs, osl=osl: e.tensor_scalar_add(
                        ar["stg16"][:, osl], ps, bqs), wait=ww, inc_n=1)
                    evac_marks.append(C.c("dve"))
                else:
                    C.act(lambda e, ps=ps, bqs=bqs, osl=osl: e.activation(
                        ar["stg16"][:, osl], ps, AF.Silu, bias=bqs, scale=1.0),
                        wait=ww)
                    evac_marks.append(C.c("act"))
            if blk == 0:
                C.dma("gpsimd", M["qst"].ap()[hh], ar["stg16"][:, :],
                      wait=(("dve", C.c("dve")),), sem="st_stg")
            elif blk == 1:
                C.dve(lambda e, hh=hh: e.tensor_scalar_mul(
                    ar["ksc"][:, :], ar["stg16"][:, :], smcol(hh)),
                    wait=(("st_pos", pos_st),))
                C.dve(lambda e, hh=hh: e.tensor_scalar_mul(
                    ar["stg16"][:, :], ar["stg16"][:, :], sm1col(hh)))
                C.dve(lambda e: e.tensor_add(ar["stg16"][:, 1:T],
                                             ar["stg16"][:, 1:T],
                                             ar["ksc"][:, 0:T - 1]), inc_n=1)
                C.dma("gpsimd", M["kst"].ap()[hh], ar["stg16"][:, :],
                      wait=(("dve", C.c("dve")),), sem="st_stg")
            else:
                C.dma("gpsimd", M["spst"].ap()[hh], ar["stg16"][:, :],
                      wait=(("act", C.c("act")),), sem="st_stg")
            stg_free = C.c("st_stg")
        pass
    proj_st = C.c("st_stg")

    if DEBUG_STOP == 14:
        emit_output(C, M)
        return

    phase_sync()
    C.scope("P4_attn")
    C.dma("sync", ar["masks1"][:, :], h["mask1m"].ap(),
          wait=(("pe", C.c("pe")), ("dve", C.c("dve"))), sem="q_mask")
    C.dma("sync", ar["masks2"][:, :], h["mask2m"].ap(), sem="q_mask")
    mask_ld = C.c("q_mask")

    def head_load(hh, guard_pe):
        slot = hh % 2
        o = slot * T
        hsem = "q_hl0" if slot == 0 else "q_hl1"
        w = (("st_stg", proj_st), ("pe", guard_pe), ("dve", C.c("dve")),
             ("act", C.c("act")))
        C.dma("sync", ar["qh"][:, o:o + T], M["qst"].ap()[hh], wait=w, sem=hsem)
        C.dma("sync", ar["kh"][:, o:o + T], M["kst"].ap()[hh], sem=hsem)
        C.dma("sync", ar["vh"][:, o:o + T], M["vst"].ap()[hh], sem=hsem)
        C.dma("sync", ar["sph"][:, o:o + T], M["spst"].ap()[hh], sem=hsem)
        return (hsem, C.c(hsem))

    def rowp_load(hh, guard_pe):
        C.dma("sync", sb["rowp16"][0:1, :], APflat(M["poshd"], hh * T, T),
              wait=(("st_pos", pos_st), ("pe", guard_pe)), sem="q_rp")
        C.dma("sync", sb["rowp16"][1:2, :], APflat(M["posld"], hh * T, T),
              sem="q_rp")
        return ("q_rp", C.c("q_rp"))

    hload = {0: head_load(0, 0)}
    if DEBUG_STOP == 23:
        C.op("gpsimd", None, wait=(hload[0], rowp_load(0, 0), ("q_mask", C.c("q_mask"))))
        emit_output(C, M)
        return
    for hh in range(HN):
        conv_pump(3)
        slot = hh % 2
        o = slot * T
        ld_q = hload[hh]
        rp_ld = rowp_load(hh, C.c("pe"))
        if hh + 1 < HN:
            hload[hh + 1] = head_load(hh + 1, C.c("pe"))
        poshilo_r = sb["rowp16"][0:2, :]
        negm_r = sb["rowp16"][64:65, :]
        qh, kh, vh, sph = ar["qh"], ar["kh"], ar["vh"], ar["sph"]
        psum_war = {0: 0, 1: 0}
        for qi in range(32):
            jt_tot = qi // 4 + 1
            chunks = [(0, min(jt_tot, 4))]
            if jt_tot > 4:
                chunks.append((4, jt_tot))
            mA = sb["colscr"][:, 0:1]
            mB = sb["colscr"][:, 1:2]
            lA = sb["colscr"][:, 2:3]
            lB = sb["colscr"][:, 3:4]
            for ci, (j0, j1) in enumerate(chunks):
                psid = ci if len(chunks) == 2 else qi % 2
                ps = psA if psid == 0 else psB
                clen = (j1 - j0) * 512
                for jt in range(j0, j1):
                    sl = slice((jt - j0) * 512, (jt - j0) * 512 + 512)
                    w = []
                    if jt == j0:
                        w = [("act", psum_war[psid])]
                        if qi == 0 and ci == 0:
                            w += [ld_q, rp_ld, ("q_mask", mask_ld),
                                  ("dve", C.c("dve")), ("act", C.c("act"))]
                    C.pe(lambda e, sl=sl, jt=jt, qi=qi, ps=ps: e.matmul(
                        ps[:, sl], qh[:, o + qi * 128: o + qi * 128 + 128],
                        kh[:, o + jt * 512: o + jt * 512 + 512],
                        start=True, stop=(DEBUG_STOP == 24)), wait=tuple(w))
                    if DEBUG_STOP == 24 and hh == 0 and qi == 0:
                        C.pe(lambda e: e.matmul(
                            psB[:, 0:512], qh[:, o:o + 128],
                            kh[:, o:o + 512], start=True, stop=True), inc_n=1)
                        emit_output(C, M)
                        return
                    C.pe(lambda e, sl=sl, jt=jt, ps=ps: e.matmul(
                        ps[:, sl], sb["ones16m"][0:2, :],
                        poshilo_r[:, jt * 512: jt * 512 + 512],
                        start=False, stop=True),
                        inc_n=1 if jt == j1 - 1 else 0)
                mm_pe = C.c("pe")
                if DEBUG_STOP == 19 and hh == 0 and qi == 0:
                    emit_output(C, M)
                    return
                if j1 == jt_tot:
                    dd = qi % 4
                    msl = slice((jt_tot - 1 - j0) * 512, (jt_tot - j0) * 512)
                    C.dve(lambda e, msl=msl, dd=dd, ps=ps: e.tensor_add(
                        ps[:, msl], ps[:, msl],
                        ar["masks1"][:, 384 - 128 * dd: 384 - 128 * dd + 512]),
                        wait=(("pe", mm_pe),), inc_n=1)
                if DEBUG_STOP == 20 and hh == 0 and qi == 0:
                    emit_output(C, M)
                    return
                nb = clen // 512
                for jb in range(nb):
                    C.dve(lambda e, ps=ps, jb=jb: e.reduce_max(
                        sb["colscr"][:, 4 + jb:5 + jb],
                        ps[:, jb * 512:(jb + 1) * 512], AX),
                        wait=(("pe", mm_pe),) if jb == 0 else ())
                dst = mA if ci == 0 else mB
                C.dve(lambda e, dst=dst: e.tensor_copy(dst, sb["colscr"][:, 4:5]))
                for jb in range(1, nb):
                    C.dve(lambda e, dst=dst, jb=jb: e.tensor_max(
                        dst, dst, sb["colscr"][:, 4 + jb:5 + jb]), inc_n=1)
            if DEBUG_STOP == 21 and hh == 0 and qi == 0:
                emit_output(C, M)
                return
            if len(chunks) == 2:
                C.dve(lambda e: e.tensor_max(mA, mA, mB))
            C.dve(lambda e: e.tensor_copy(sb["colscr3"][:, 0:1], mA))
            C.dve(lambda e, qi=qi: e.tensor_scalar_mul(
                sb["negm32"][:, qi:qi + 1], sb["colscr3"][:, 0:1], -1.0), inc_n=1)
            neg_d = C.c("dve")
            bi = 0
            for ci, (j0, j1) in enumerate(chunks):
                psid = ci if len(chunks) == 2 else qi % 2
                ps = psA if psid == 0 else psB
                clen = (j1 - j0) * 512
                for jb in range(clen // 512):
                    C.act(lambda e, ps=ps, jb=jb, qi=qi, bi=bi: e.activation(
                        ar["pthrow"][:, jb * 512:(jb + 1) * 512],
                        ps[:, jb * 512:(jb + 1) * 512], AF.Exp,
                        bias=sb["negm32"][:, qi:qi + 1], scale=1.0,
                        accum_out=sb["colscr"][:, 8 + bi:9 + bi]),
                        wait=(("dve", C.c("dve")),) if bi == 0 else ())
                    la = C.c("act")
                    if bi == 0:
                        C.dve(lambda e: e.tensor_copy(lA, sb["colscr"][:, 8:9]),
                              wait=(("act", la),))
                    else:
                        C.dve(lambda e, bi=bi: e.tensor_add(
                            lA, lA, sb["colscr"][:, 8 + bi:9 + bi]),
                            wait=(("act", la),))
                    bi += 1
                psum_war[psid] = C.c("act")
            C.dve(lambda e, qi=qi: e.reciprocal(sb["linv"][:, qi:qi + 1], lA),
                  inc_n=1)
            if DEBUG_STOP == 22 and hh == 0 and qi == 0:
                emit_output(C, M)
                return
        p1_d = C.c("dve")
        p1_a = C.c("act")
        if DEBUG_STOP == 16 and hh == 0:
            emit_output(C, M)
            return
        C.pe(lambda e: e.transpose(psA[0:32, 0:128], sb["negm32"][:, 0:32],
                                   sb["ident"][:, :]),
             wait=(("dve", p1_d), ("act", p1_a)), inc_n=1)
        C.dve(lambda e: e.tensor_copy(ar["nm32t"][:, :], psA[0:32, 0:128]),
              wait=(("pe", C.c("pe")),))
        C.dve(lambda e: e.tensor_copy(ar["nm16t"][:, :], ar["nm32t"][:, :]),
              inc_n=1)
        C.dma("gpsimd", M["mrowd"].ap(), ar["nm16t"][:, :],
              wait=(("dve", C.c("dve")),), sem="st_nm")
        C.pe(lambda e: e.transpose(psA[0:32, 128:256], sb["linv"][:, 0:32],
                                   sb["ident"][:, :]), inc_n=1)
        C.dve(lambda e: e.tensor_copy(ar["nm32t"][:, :], psA[0:32, 128:256]),
              wait=(("pe", C.c("pe")),), inc_n=1)
        C.dma("gpsimd", M["linvd"].ap()[hh].rearrange("(a b) -> a b", a=32),
              ar["nm32t"][:, :], wait=(("dve", C.c("dve")),), sem="st_nm")
        bounce_g = C.c("st_nm")
        C.dma("sync", negm_r[:, :], APflat(M["mrowd"], 0, T),
              wait=(("st_nm", bounce_g),), sem="q_nm")
        nm_ld = C.c("q_nm")
        if DEBUG_STOP == 17 and hh == 0:
            emit_output(C, M)
            return
        exp_war = [0, 0]
        for s in range(8):
            jcn = 4 * s + 4
            C.dma("scalar", ar["invlb"][:, :],
                  APbc(M["linvd"], hh * T + s * 512, 128, 512),
                  wait=(("st_nm", bounce_g), ("dve", C.c("dve"))), sem="q_inv")
            inv_ld = C.c("q_inv")
            av_done = 0
            for jc in range(jcn):
                pslot = jc % 2
                psT = psA[:, pslot * 512:(pslot + 1) * 512]
                w = [("act", exp_war[pslot])]
                if jc == 0 and s == 0:
                    w.append(("q_nm", nm_ld))
                if jc >= 2:
                    w.append(("pe", av_done))
                C.pe(lambda e, jc=jc, s=s, psT=psT: e.matmul(
                    psT, kh[:, o + jc * 128: o + jc * 128 + 128],
                    qh[:, o + s * 512: o + s * 512 + 512],
                    start=True, stop=False), wait=tuple(w))
                C.pe(lambda e, s=s, psT=psT: e.matmul(
                    psT, sb["ones16m"][64:65, :], negm_r[:, s * 512: s * 512 + 512],
                    start=False, stop=True), inc_n=1)
                mmT = C.c("pe")
                dd = jc - 4 * s
                if dd >= 0:
                    C.dve(lambda e, dd=dd, psT=psT: e.tensor_add(
                        psT, psT,
                        ar["masks2"][:, 384 - 128 * dd:384 - 128 * dd + 512]),
                        wait=(("pe", mmT),), inc_n=1)
                    mwd = (("dve", C.c("dve")),)
                else:
                    mwd = (("pe", mmT),)
                C.act(lambda e, jc=jc, hh=hh, psT=psT, pslot=pslot: e.activation(
                    ar["pts"][:, pslot * 512:(pslot + 1) * 512], psT, AF.Exp,
                    bias=sb["pos_t"][:, jc * HN + hh: jc * HN + hh + 1], scale=1.0),
                    wait=mwd)
                exp_war[pslot] = C.c("act")
                C.pe(lambda e, jc=jc, jcn=jcn, pslot=pslot: e.matmul(
                    psB[:, 0:512], vh[:, o + jc * 128: o + jc * 128 + 128],
                    ar["pts"][:, pslot * 512:(pslot + 1) * 512],
                    start=(jc == 0), stop=(jc == jcn - 1)),
                    wait=(("act", exp_war[pslot]),) +
                    ((("dve", C.c("dve")),) if jc == 0 else ()),
                    inc_n=1)
                av_done = C.c("pe")
            C.dve(lambda e: e.tensor_mul(ar["esmall"][:, :], psB[:, 0:512],
                                         ar["invlb"][:, :]),
                  wait=(("pe", av_done), ("q_inv", inv_ld)))
            C.dve(lambda e, s=s: e.tensor_mul(
                ar["gtile"][:, :], ar["esmall"][:, :],
                sph[:, o + s * 512: o + s * 512 + 512]),
                wait=(("st_g", C.c("st_g")),), inc_n=1)
            g_d = C.c("dve")
            C.dma("gpsimd", M["gst"].ap()[hh][:, s * 512:(s + 1) * 512],
                  ar["gtile"][:, :], wait=(("dve", g_d),), sem="st_g")
            if DEBUG_STOP == 18 and hh == 0 and s == 0:
                emit_output(C, M)
                return
        conv_pump(8)
    g_store = C.c("st_g")

    if DEBUG_STOP == 15:
        emit_output(C, M)
        return

    phase_sync()
    C.scope("P5_wout")
    C.dma("sync", ar["wout_s"][:, :].rearrange("p (k c) -> p k c", k=KC),
          wst["woutT"].ap().rearrange("k p c -> p k c"),
          wait=(attn_w, ("pe", C.c("pe"))), sem="q_wout")
    wo_ld = C.c("q_wout")
    ev_prev = 0
    for nt in range(8):
        for hh2 in range(HN):
            C.dma("sync", ar["gld"][:, hh2 * 512:(hh2 + 1) * 512],
                  M["gst"].ap()[hh2][:, nt * 512:(nt + 1) * 512],
                  wait=((("st_g", g_store), ("pe", C.c("pe"))) if hh2 == 0
                        else ()), sem="q_gld")
        gl = C.c("q_gld")
        evd = {}
        for dm in range(KC):
            ps = psA[:, (dm % 4) * 512:(dm % 4) * 512 + 512]
            for hh2 in range(HN):
                C.pe(lambda e, dm=dm, hh2=hh2, ps=ps: e.matmul(
                    ps, ar["wout_s"][:, hh2 * D + dm * 128: hh2 * D + dm * 128 + 128],
                    ar["gld"][:, hh2 * 512:(hh2 + 1) * 512],
                    start=(hh2 == 0), stop=(hh2 == HN - 1)),
                    wait=(("q_gld", gl), ("q_wout", wo_ld),
                          ("dve", evd.get(dm - 4, ev_prev))) if hh2 == 0 else (),
                    inc_n=1 if hh2 == HN - 1 else 0)
            wpe = C.c("pe")
            C.dve(lambda e, dm=dm, ps=ps: e.tensor_copy(
                ar["aotile"][:, dm * 512:(dm + 1) * 512], ps),
                wait=(("pe", wpe), ("st_ao", C.c("st_ao"))), inc_n=1)
            evd[dm] = C.c("dve")
        ev_prev = C.c("dve")
        for dm in range(KC):
            C.dma("gpsimd",
                  M["rs_in"].ap()[nt // 4, dm][:, (nt % 4) * 512:(nt % 4) * 512 + 512],
                  ar["aotile"][:, dm * 512:(dm + 1) * 512],
                  wait=(("dve", ev_prev),) if dm == 0 else (), sem="st_ao")
        conv_pump(8)
    rs_w = C.c("st_ao")
    C.op("gpsimd", lambda e: e.collective_compute(
        "ReduceScatter", ALU.add, replica_groups=M["groups"],
        ins=[M["rs_in"].ap().opt()], outs=[M["rs_out"].ap().opt()]),
        wait=(("st_ao", rs_w),), inc=(("cc", 1),))
    rs_cc = C.c("cc")

    def ln_residual_from_dram(rs_cc, use_oln):
        cb, cq = ar["lncb"], ar["lncq"]
        Rb, Bb = ar["Rb"], ar["Bb"]
        mucol = sb["lncol"][:, 0:16]
        vcol = sb["lncol"][:, 16:32]
        Rcol = sb["lncol"][:, 32:48]
        Bcol = sb["lncol"][:, 48:64]
        acc_s = sb["lncol"][:, 96:112]
        acc_q = sb["lncol"][:, 112:128]
        mm_marks = []
        for kc in range(KC):
            pp = (kc % 2) * TOWN
            C.dma("sync", ar["lnes"][:, :], M["rs_out"].ap()[kc],
                  wait=((("cc", rs_cc), ("pe", C.c("pe")), ("dve", C.c("dve")))
                        if kc == 0 else (("dve", C.c("dve")),)), sem="q_ao3")
            al = C.c("q_ao3")
            w2 = [("q_ao3", al)]
            if kc == 0:
                w2.append(("pe", C.c("pe")))
            if kc >= 1:
                w2.append(("dve", mm_marks[kc - 1]))
            C.dve(lambda e, pp=pp: e.tensor_copy(cb[:, pp:pp + TOWN],
                                                 ar["lnes"][:, :]),
                  wait=tuple(w2), inc_n=1)
            cd = C.c("dve")
            C.act(lambda e, pp=pp: e.activation(cq[:, pp:pp + TOWN],
                                                cb[:, pp:pp + TOWN], AF.Square,
                                                bias=0.0, scale=1.0),
                  wait=(("dve", cd),))
            ca = C.c("act")
            for tg in range(16):
                C.pe(lambda e, tg=tg, pp=pp: e.matmul(
                    psA[:, 1536 + tg:1536 + tg + 1],
                    cb[:, pp + tg * 128:pp + tg * 128 + 128],
                    sb["ones_col16"][:, :], start=True, stop=True),
                    wait=((("dve", cd), ("act", ca)) if tg == 0 else ()))
                C.pe(lambda e, tg=tg, pp=pp: e.matmul(
                    psB[:, 1536 + tg:1536 + tg + 1],
                    cq[:, pp + tg * 128:pp + tg * 128 + 128],
                    sb["ones_col16"][:, :], start=True, stop=True),
                    inc_n=1 if tg == 15 else 0)
            mmk = C.c("pe")
            if kc == 0:
                C.dve(lambda e: e.tensor_copy(acc_s, psA[:, 1536:1552]),
                      wait=(("pe", mmk),))
                C.dve(lambda e: e.tensor_copy(acc_q, psB[:, 1536:1552]), inc_n=1)
            else:
                C.dve(lambda e: e.tensor_add(acc_s, acc_s, psA[:, 1536:1552]),
                      wait=(("pe", mmk),))
                C.dve(lambda e: e.tensor_add(acc_q, acc_q, psB[:, 1536:1552]),
                      inc_n=1)
            mm_marks.append(C.c("dve"))
        C.dve(lambda e: e.tensor_scalar_mul(mucol, acc_s, 1.0 / D))
        C.dve(lambda e: e.tensor_scalar_mul(vcol, acc_q, 1.0 / D))
        C.dve(lambda e: e.tensor_mul(Bcol, mucol, mucol))
        C.dve(lambda e: e.tensor_sub(vcol, vcol, Bcol))
        C.dve(lambda e: e.tensor_scalar(vcol, vcol, 1.0, LNEPS, ALU.mult, ALU.add),
              inc_n=1)
        var_d = C.c("dve")
        C.act(lambda e: e.activation(Bcol, vcol, AF.Ln, bias=0.0, scale=1.0),
              wait=(("dve", var_d),))
        C.act(lambda e: e.activation(Rcol, Bcol, AF.Exp, bias=0.0, scale=-0.5))
        r_a = C.c("act")
        C.dve(lambda e: e.tensor_mul(Bcol, mucol, Rcol), wait=(("act", r_a),))
        C.dve(lambda e: e.tensor_scalar_mul(Bcol, Bcol, -1.0), inc_n=1)
        rb_d = C.c("dve")
        C.pe(lambda e: e.transpose(psA[0:32, 0:128], sb["lncol"][:, 32:64],
                                   sb["ident"][:, :]),
             wait=(("dve", rb_d),), inc_n=1)
        tr_pe = C.c("pe")
        C.dve(lambda e: e.tensor_copy(ar["lnrt"][0:32, :], psA[0:32, 0:128]),
              wait=(("pe", tr_pe),), inc_n=1)
        ev_d = C.c("dve")
        C.dma("gpsimd", M["lnrows"].ap()[0:2], ar["lnrt"][0:32, :],
              wait=(("dve", ev_d),), sem="st_lnr")
        g1 = C.c("st_lnr")
        C.dma("scalar", Rb[:, :], APbc(M["lnrows"], 0, 128, TOWN),
              wait=(("st_lnr", g1), ("dve", C.c("dve"))), sem="q_RB")
        C.dma("scalar", Bb[:, :], APbc(M["lnrows"], TOWN, 128, TOWN), sem="q_RB")
        bc_v = C.c("q_RB")
        for kc in range(KC):
            sl = slice(kc * TOWN, (kc + 1) * TOWN)
            C.dma("sync", ar["lnes"][:, :], M["rs_out"].ap()[kc],
                  wait=(("dve", C.c("dve")),), sem="q_ao3")
            al = C.c("q_ao3")
            C.dve(lambda e: e.tensor_mul(ar["lnes2"][:, :], ar["lnes"][:, :],
                                         Rb[:, :]),
                  wait=(("q_ao3", al), ("q_RB", bc_v)))
            C.dve(lambda e: e.tensor_add(ar["lnes2"][:, :], ar["lnes2"][:, :],
                                         Bb[:, :]))
            C.dve(lambda e, kc=kc: e.tensor_scalar(
                ar["lnes2"][:, :], ar["lnes2"][:, :], sb["olnw"][:, kc:kc + 1],
                sb["olnb"][:, kc:kc + 1], ALU.mult, ALU.add))
            C.dve(lambda e, sl=sl: e.tensor_add(x[:, sl], x[:, sl],
                                                ar["lnes2"][:, :]), inc_n=1)

    ln_residual_from_dram(rs_cc, True)
    if DEBUG_STOP == 1:
        emit_output(C, M)
        return

    def emit_ffn(call):
        phase_sync()
        C.scope(f"F{call}_ffn")
        conv_pump(len(conv_jobs))
        ffn_w = ("q_cvoF", C.c("q_cvoF"))
        xn_done = emit_ln(x, ar["xno"], affine=False, router_rows=True)
        xno = ar["xno"]
        C.dma("sync", ar["wr32"][:, :].rearrange("p (k c) -> p k c", k=KC),
              h["wr"].ap().rearrange("k p c -> p k c"),
              wait=(("dve", xn_done), ("pe", C.c("pe"))), sem="q_wr")
        wr_ld = C.c("q_wr")
        for kc in range(KC):
            C.pe(lambda e, kc=kc: e.matmul(
                psB[0:1, 1024:1031], sb["ones_col32"][:, :],
                ar["wr32"][:, kc * 7:(kc + 1) * 7],
                start=(kc == 0), stop=(kc == KC - 1)),
                wait=((("q_wr", wr_ld),)) if kc == 0 else (),
                inc_n=1 if kc == KC - 1 else 0)
        swr_pe = C.c("pe")
        C.dve(lambda e: e.tensor_scalar_mul(sb["prows"][0:1, 40:47],
                                            psB[0:1, 1024:1031], -1.0),
              wait=(("pe", swr_pe),), inc_n=1)
        swr_d = C.c("dve")
        C.dma("scalar", sb["rbb"][:, :], APbc(h["rbias"], 0, 128, 7),
              wait=(("dve", C.c("dve")),), sem="q_rbb")
        rb_ld = C.c("q_rbb")
        wt_marks = {}
        for tg in range(16):
            psl = psB[:, 1536 + (tg % 2) * 16: 1536 + (tg % 2) * 16 + 7]
            w = [("dve", xn_done)] if tg == 0 else []
            if tg >= 2:
                w.append(("dve", wt_marks[tg - 2]))
            for kc in range(KC):
                C.pe(lambda e, tg=tg, kc=kc, psl=psl: e.matmul(
                    psl, x[:, kc * TOWN + tg * 128: kc * TOWN + tg * 128 + 128],
                    ar["wr32"][:, kc * 7:(kc + 1) * 7],
                    start=(kc == 0), stop=False),
                    wait=tuple(w) if kc == 0 else ())
            C.pe(lambda e, tg=tg, psl=psl: e.matmul(
                psl, sb["rowpf"][0:1, tg * 128:(tg + 1) * 128],
                sb["prows"][0:1, 40:47], start=False, stop=True),
                wait=(("dve", swr_d),) if tg == 0 else (), inc_n=1)
            rpe = C.c("pe")
            C.dve(lambda e, tg=tg, psl=psl: e.tensor_scalar_mul(
                ar["ltile"][:, 0:7], psl, sb["rcols"][:, tg:tg + 1]),
                wait=(("pe", rpe),))
            C.dve(lambda e: e.tensor_mul(ar["ltile"][:, 0:7], ar["ltile"][:, 0:7],
                                         sb["rbb"][:, :]),
                  wait=(("q_rbb", rb_ld),), inc_n=1)
            lg_d = C.c("dve")
            C.act(lambda e: e.activation(ar["ptile"][:, 0:7], ar["ltile"][:, 0:7],
                                         AF.Sigmoid, bias=0.0, scale=1.0),
                  wait=(("dve", lg_d),))
            pr_a = C.c("act")
            c0 = sb["colscr"][:, 0:1]
            c1 = sb["colscr"][:, 1:2]
            C.dve(lambda e: e.reduce_max(c0, ar["ptile"][:, 0:7], AX),
                  wait=(("act", pr_a),))
            C.dve(lambda e: e.tensor_single_scalar(
                ar["ltile"][:, 0:7], ar["ptile"][:, 0:7], c0, ALU.is_ge))
            C.dve(lambda e: e.tensor_scalar_mul(ar["ltile"][:, 0:7],
                                                ar["ltile"][:, 0:7], 1e30))
            C.dve(lambda e: e.tensor_sub(ar["ltile"][:, 0:7], ar["ptile"][:, 0:7],
                                         ar["ltile"][:, 0:7]))
            C.dve(lambda e: e.reduce_max(c1, ar["ltile"][:, 0:7], AX))
            C.dve(lambda e: e.tensor_single_scalar(
                ar["ltile"][:, 0:7], ar["ptile"][:, 0:7], c1, ALU.is_ge))
            C.dve(lambda e: e.tensor_add(c0, c0, c1))
            C.dve(lambda e: e.reciprocal(c1, c0))
            C.dve(lambda e: e.tensor_mul(ar["ltile"][:, 0:7], ar["ltile"][:, 0:7],
                                         ar["ptile"][:, 0:7]))
            C.dve(lambda e: e.tensor_scalar_mul(ar["ltile"][:, 0:7],
                                                ar["ltile"][:, 0:7], c1), inc_n=1)
            wv_d = C.c("dve")
            C.pe(lambda e, tg=tg: e.transpose(
                psB[0:7, 1280 + (tg % 2) * 128: 1280 + (tg % 2) * 128 + 128],
                ar["ltile"][:, 0:7], sb["ident"][:, :]),
                wait=(("dve", wv_d),), inc_n=1)
            tpe = C.c("pe")
            C.dve(lambda e, tg=tg: e.tensor_copy(
                sb["wt7"][0:7, tg * 128:(tg + 1) * 128],
                psB[0:7, 1280 + (tg % 2) * 128: 1280 + (tg % 2) * 128 + 128]),
                wait=(("pe", tpe),), inc_n=1)
            wt_marks[tg] = C.c("dve")
        C.dma("gpsimd", M["wrowd"].ap(), sb["wt7"][0:7, :],
              wait=(("dve", wt_marks[15]),), sem="st_wrow")
        wrow_st = C.c("st_wrow")

        ln_guard = C.c("dve")
        st = M.setdefault("ffn_state", dict(
            g_act={}, u_dve={}, d_dve={}, mul_last={0: 0, 1: 0},
            down_pe=0, slot_war={}, pending={}))

        def ring_load(eidx, jc, what, first):
            wsrc = wst["gates"] if what == "g" else wst["ups"]
            buf = ar["gring"] if what == "g" else ar["uring"]
            slx = jc % 4
            rsem = f"q_{what}{slx}"
            w = []
            if first:
                w += [ffn_w, ("dve", ln_guard), ("st_wrow", wrow_st)]
            pw = st["slot_war"].get((what, slx), 0)
            if pw:
                w.append(("pe", pw))
            C.dma("sync", buf[:, slx * 1024:(slx + 1) * 1024].rearrange(
                "p (k c) -> p k c", k=KC),
                wsrc.ap()[eidx].rearrange("k p c -> p k c")[:, :, jc * 128:(jc + 1) * 128],
                wait=tuple(w), sem=rsem)
            return (rsem, C.c(rsem))

        exps = [(nt, e) for nt in range(2) for e in range(NEXP)]

        def prefill(key, first=False):
            nt_, e_ = key
            d = {}
            for j0 in range(4):
                d[("g", j0)] = ring_load(e_, j0, "g", first and j0 == 0)
                d[("u", j0)] = ring_load(e_, j0, "u", False)
            st["pending"][key] = d

        prefill(exps[0], first=True)
        for ei, (nt, eidx) in enumerate(exps):
            tb = nt * 1024
            ring_ld = st["pending"].pop((nt, eidx))
            if eidx > 0:
                C.dma("scalar", ar["wbb"][:, 0:1024],
                      APbc(M["wrowd"], (eidx - 1) * TOWN + tb, 128, 1024),
                      wait=(("st_wrow", wrow_st),)
                      + tuple(("dve", v) for v in st["d_dve"].values()),
                      sem="q_wbb")
            wb_ld = C.c("q_wbb")
            for jc in range(JCN):
                gl = ring_ld[("g", jc)]
                ul = ring_ld[("u", jc)]
                gsl = ar["gring"][:, (jc % 4) * 1024:(jc % 4) * 1024 + 1024]
                usl = ar["uring"][:, (jc % 4) * 1024:(jc % 4) * 1024 + 1024]
                gate_pe = {}
                for ns in range(2):
                    pg = psA[:, (jc % 2) * 1024 + ns * 512:
                             (jc % 2) * 1024 + ns * 512 + 512]
                    w = [gl, ("act", st["g_act"].get((jc % 2, ns), 0))]
                    if jc < 2:
                        w.append(("dve", st["d_dve"].get((jc % 2, ns), 0)))
                    for kc in range(KC):
                        C.pe(lambda e, kc=kc, gsl=gsl, pg=pg, ns=ns: e.matmul(
                            pg, gsl[:, kc * 128:(kc + 1) * 128],
                            xno[:, kc * TOWN + tb + ns * 512:
                                 kc * TOWN + tb + ns * 512 + 512],
                            start=(kc == 0), stop=(kc == KC - 1)),
                            wait=tuple(w) if kc == 0 else (),
                            inc_n=1 if kc == KC - 1 else 0)
                    gate_pe[ns] = C.c("pe")
                for ns in range(2):
                    pu = psB[:, (jc % 2) * 1024 + ns * 512:
                             (jc % 2) * 1024 + ns * 512 + 512]
                    w = [ul, ("dve", st["u_dve"].get((jc % 2, ns), 0))]
                    for kc in range(KC):
                        C.pe(lambda e, kc=kc, usl=usl, pu=pu, ns=ns: e.matmul(
                            pu, usl[:, kc * 128:(kc + 1) * 128],
                            xno[:, kc * TOWN + tb + ns * 512:
                                 kc * TOWN + tb + ns * 512 + 512],
                            start=(kc == 0), stop=(kc == KC - 1)),
                            wait=tuple(w) if kc == 0 else (),
                            inc_n=1 if kc == KC - 1 else 0)
                up_pe = C.c("pe")
                st["slot_war"][("g", jc % 4)] = up_pe
                st["slot_war"][("u", jc % 4)] = up_pe
                if jc + 4 < JCN:
                    ring_ld[("g", jc + 4)] = ring_load(eidx, jc + 4, "g", False)
                    ring_ld[("u", jc + 4)] = ring_load(eidx, jc + 4, "u", False)
                for ns in range(2):
                    pg = psA[:, (jc % 2) * 1024 + ns * 512:
                             (jc % 2) * 1024 + ns * 512 + 512]
                    pu = psB[:, (jc % 2) * 1024 + ns * 512:
                             (jc % 2) * 1024 + ns * 512 + 512]
                    sil_w = [("pe", gate_pe[ns]), ("dve", st["mul_last"][ns])]
                    if jc == 0:
                        sil_w += [("dve", v) for v in st["d_dve"].values()]
                    C.act(lambda e, pg=pg, ns=ns: e.activation(
                        ar["sg"][:, ns * 512:(ns + 1) * 512], pg,
                        AF.Silu, bias=0.0, scale=1.0), wait=tuple(sil_w))
                    sg_a = C.c("act")
                    st["g_act"][(jc % 2, ns)] = sg_a
                    C.dve(lambda e, jc=jc, pu=pu, ns=ns: e.tensor_mul(
                        ar["hbuf"][:, jc * 1024 + ns * 512:
                                   jc * 1024 + ns * 512 + 512], pu,
                        ar["sg"][:, ns * 512:(ns + 1) * 512]),
                        wait=(("act", sg_a), ("pe", up_pe),
                              ("pe", st["down_pe"])), inc_n=1)
                    mul_d = C.c("dve")
                    st["u_dve"][(jc % 2, ns)] = mul_d
                    st["mul_last"][ns] = mul_d
            h_done = C.c("dve")
            if ei + 1 < len(exps):
                prefill(exps[ei + 1])
            dq_war = 0
            for q4 in range(4):
                w = [("dve", h_done)] if q4 == 0 else []
                if dq_war:
                    w.append(("pe", dq_war))
                if q4 == 0:
                    w += [ffn_w, ("st_wrow", wrow_st)]
                C.dma("scalar", ar["dquart"][:, :].rearrange(
                    "p (j c) -> p j c", j=JCN),
                    wst["downs"].ap()[eidx].rearrange(
                        "j p c -> p j c")[:, :, q4 * 256:(q4 + 1) * 256],
                    wait=tuple(w), sem="q_dh")
                d_ld = C.c("q_dh")
                for dmi2 in range(2):
                    dm = q4 * 2 + dmi2
                    for ns in range(2):
                        pdp = psA[:, (dm % 2) * 1024 + ns * 512:
                                  (dm % 2) * 1024 + ns * 512 + 512]
                        for jc in range(JCN):
                            C.pe(lambda e, jc=jc, dmi2=dmi2, pdp=pdp, ns=ns: e.matmul(
                                pdp, ar["dquart"][:, jc * 256 + dmi2 * 128:
                                                  jc * 256 + dmi2 * 128 + 128],
                                ar["hbuf"][:, jc * 1024 + ns * 512:
                                           jc * 1024 + ns * 512 + 512],
                                start=(jc == 0), stop=(jc == JCN - 1)),
                                wait=(("q_dh", d_ld), ("dve", h_done),
                                      ("dve", st["d_dve"].get((dm % 2, ns), 0)),
                                      ("act", st["g_act"].get((dm % 2, ns), 0)))
                                if jc == 0 else (),
                                inc_n=1 if jc == JCN - 1 else 0)
                        dpe = C.c("pe")
                        xsl = x[:, dm * TOWN + tb + ns * 512:
                                dm * TOWN + tb + ns * 512 + 512]
                        if eidx == 0:
                            C.dve(lambda e, xsl=xsl, pdp=pdp: e.tensor_add(
                                xsl, xsl, pdp), wait=(("pe", dpe),), inc_n=1)
                        else:
                            C.dve(lambda e, pdp=pdp, ns=ns: e.tensor_mul(
                                ar["fes"][:, 0:512], pdp,
                                ar["wbb"][:, ns * 512:(ns + 1) * 512]),
                                wait=(("pe", dpe), ("q_wbb", wb_ld)))
                            C.dve(lambda e, xsl=xsl: e.tensor_add(
                                xsl, xsl, ar["fes"][:, 0:512]), inc_n=1)
                        st["d_dve"][(dm % 2, ns)] = C.c("dve")
                dq_war = C.c("pe")
            st["down_pe"] = C.c("pe")
        M.pop("ffn_state")
    emit_ffn(0)
    if DEBUG_STOP == 2:
        emit_output(C, M)
        return

    phase_sync()
    C.scope("B0_ln")
    xn_done = emit_ln(x, ar["xno"], affine=False)
    xnf_ld = emit_ag(ar["xno"], xn_done)
    C.dma("sync", ar["wproj"][0:64, :].rearrange("p (hh m) -> p hh m", hh=HN),
          wst["projT"].ap().rearrange("hh p m -> p hh m"),
          wait=(attn_w, ("dve", xn_done), ("pe", C.c("pe"))), sem="q_wproj")
    C.dma("sync", ar["wproj"][64:128, :].rearrange("p (hh m) -> p hh m", hh=HN),
          wst["projT"].ap().rearrange("hh p m -> p hh m"), sem="q_wproj")
    wproj_ld = C.c("q_wproj")

    def wqkv_load(hh, guard_pe):
        for i, wname in enumerate(["wqT", "wkT", "wvT"]):
            C.dma("sync", ar["wqkv3"][:, i * 512:(i + 1) * 512].rearrange(
                "p (k c) -> p k c", k=KC),
                wst[wname].ap().rearrange("k p c -> p k c")[
                    :, :, hh * DK:(hh + 1) * DK],
                wait=(("pe", guard_pe), attn_w) if i == 0 else (), sem="q_wqkv")
        return C.c("q_wqkv")

    mg_prev = 0
    for hh in range(HN):
        C.scope("B1_qkv")
        wq_ld = wqkv_load(hh, C.c("pe"))
        qk_marks = []
        for i in range(2):
            for nt2 in range(8):
                ps = psA[0:DK, (nt2 % 4) * 512:(nt2 % 4) * 512 + 512]
                idx = i * 8 + nt2
                w = []
                if idx == 0:
                    w = [("q_wqkv", wq_ld), ("q_xnf", xnf_ld), ("dve", mg_prev)]
                elif idx >= 4:
                    w = [("dve", qk_marks[idx - 4])]
                for kc in range(KC):
                    C.pe(lambda e, i=i, nt2=nt2, kc=kc, ps=ps: e.matmul(
                        ps, ar["wqkv3"][:, i * 512 + kc * DK: i * 512 + (kc + 1) * DK],
                        xnf[:, kc * T + nt2 * 512: kc * T + nt2 * 512 + 512],
                        start=(kc == 0), stop=(kc == KC - 1)),
                        wait=tuple(w) if kc == 0 else (),
                        inc_n=1 if kc == KC - 1 else 0)
                qpe = C.c("pe")
                dst = (ar["qkh"][0:64, nt2 * 512:(nt2 + 1) * 512] if i == 0
                       else ar["qkh"][64:128, nt2 * 512:(nt2 + 1) * 512])
                C.dve(lambda e, dst=dst, ps=ps: e.tensor_scalar_mul(dst, ps, 0.125),
                      wait=(("pe", qpe),), inc_n=1)
                qk_marks.append(C.c("dve"))
        qk_d = C.c("dve")
        if DEBUG_STOP == 32 and hh == 0:
            emit_output(C, M)
            return
        C.scope("B2_v")
        v_marks = []
        for tc in range(32):
            ps = psB[:, 1024 + (tc % 2) * 512: 1024 + (tc % 2) * 512 + 64]
            w = [("dve", qk_marks[12])] if tc == 0 else (
                [("dve", v_marks[tc - 2])] if tc >= 2 else [])
            for kc in range(KC):
                C.pe(lambda e, tc=tc, kc=kc, ps=ps: e.matmul(
                    ps, xnf[:, kc * T + tc * 128: kc * T + tc * 128 + 128],
                    ar["wqkv3"][:, 2 * 512 + kc * DK: 2 * 512 + (kc + 1) * DK],
                    start=(kc == 0), stop=(kc == KC - 1)),
                    wait=tuple(w) if kc == 0 else (), inc_n=1 if kc == KC - 1 else 0)
            vpe = C.c("pe")
            C.dve(lambda e, tc=tc, ps=ps: e.tensor_copy(
                ar["vb_h"][:, tc * 65: tc * 65 + 64], ps),
                wait=(("pe", vpe),), inc_n=1)
            v_marks.append(C.c("dve"))
        C.dve(lambda e: e.memset(
            ar["vb_h"][:, :].rearrange("p (a b) -> p a b", b=65)[:, :, 64:65], 1.0),
            inc_n=1)
        v_d = C.c("dve")
        if DEBUG_STOP == 33 and hh == 0:
            emit_output(C, M)
            return
        C.scope("B3_phk")
        phk_marks = {}
        for tc in range(32):
            ps = psA[:, 1024 + (tc % 2) * 512: 1024 + (tc % 2) * 512 + MF]
            w = [("dve", qk_d)] if tc == 0 else (
                [("dve", phk_marks[tc - 2])] if tc >= 2 else [])
            C.pe(lambda e, tc=tc, hh=hh, ps=ps: e.matmul(
                ps, ar["qkh"][64:128, tc * 128:(tc + 1) * 128],
                ar["wproj"][64:128, hh * MF:(hh + 1) * MF],
                start=True, stop=True),
                wait=tuple(w) + ((("q_wproj", wproj_ld),) if (hh == 0 and tc == 0) else ()),
                inc_n=1)
            xw_pe = C.c("pe")
            C.dve(lambda e, ps=ps: e.reduce_max(sb["colscr"][:, 0:1], ps, AX),
                  wait=(("pe", xw_pe),), inc_n=1)
            C.act(lambda e, tc=tc: e.activation(
                ar["sqs"][:, :], ar["qkh"][64:128, tc * 128:(tc + 1) * 128],
                AF.Square, bias=0.0, scale=1.0),
                wait=(("dve", qk_d), ("pe", C.c("pe"))))
            sq_a = C.c("act")
            C.pe(lambda e, tc=tc: e.transpose(
                psB[:, (tc % 2) * 512:(tc % 2) * 512 + 64], ar["sqs"][:, :],
                sb["ident"][0:64, 0:64]), wait=(("act", sq_a),), inc_n=1)
            tq_pe = C.c("pe")
            C.dve(lambda e, tc=tc: e.reduce_sum(
                sb["colscr"][:, 1:2], psB[:, (tc % 2) * 512:(tc % 2) * 512 + 64], AX),
                wait=(("pe", tq_pe),))
            C.dve(lambda e: e.tensor_scalar(sb["colscr"][:, 1:2],
                                            sb["colscr"][:, 1:2], -0.5, -LN16,
                                            ALU.mult, ALU.add))
            C.dve(lambda e: e.tensor_sub(sb["colscr"][:, 1:2], sb["colscr"][:, 1:2],
                                         sb["colscr"][:, 0:1]), inc_n=1)
            bc_d = C.c("dve")
            C.act(lambda e, ps=ps: e.activation(
                ar["bes2"][:, :], ps, AF.Exp, bias=sb["colscr"][:, 1:2], scale=1.0),
                wait=(("dve", bc_d),))
            ph_a = C.c("act")
            C.dve(lambda e: e.tensor_scalar_add(ar["phk16"][:, :], ar["bes2"][:, :],
                                                EPSB / 16.0),
                  wait=(("act", ph_a),), inc_n=1)
            phk_d = C.c("dve")
            phk_marks[tc] = phk_d
            for mc in range(2):
                C.pe(lambda e, tc=tc, mc=mc: e.matmul(
                    psB[:, 1024 + mc * 512: 1024 + mc * 512 + 65],
                    ar["phk16"][:, mc * 128:(mc + 1) * 128],
                    ar["vb_h"][:, tc * 65:(tc + 1) * 65],
                    start=(tc == 0), stop=(tc == 31)),
                    wait=(("dve", max(phk_d, v_d)),) if mc == 0 else (),
                    inc_n=1 if (tc == 31 and mc == 1) else 0)
        kv_pe = C.c("pe")
        C.dve(lambda e: e.tensor_copy(ar["kvh"][:, 0:65], psB[:, 1024:1089]),
              wait=(("pe", kv_pe),))
        C.dve(lambda e: e.tensor_copy(ar["kvh"][:, 65:130], psB[:, 1536:1601]))
        C.dve(lambda e: e.tensor_copy(ar["kvh16"][:, :], ar["kvh"][:, :]), inc_n=1)
        kv16_d = C.c("dve")
        if DEBUG_STOP == 34 and hh == 0:
            emit_output(C, M)
            return
        C.scope("B4_phq1")
        q1_marks = {}
        for tc in range(32):
            ps = psA[:, 1024 + (tc % 2) * 512: 1024 + (tc % 2) * 512 + MF]
            w = [("dve", kv16_d)] if tc == 0 else (
                [("dve", q1_marks[tc - 2])] if tc >= 2 else [])
            C.pe(lambda e, tc=tc, hh=hh, ps=ps: e.matmul(
                ps, ar["qkh"][0:64, tc * 128:(tc + 1) * 128],
                ar["wproj"][0:64, hh * MF:(hh + 1) * MF],
                start=True, stop=True), wait=tuple(w), inc_n=1)
            xw_pe = C.c("pe")
            C.dve(lambda e, ps=ps: e.reduce_max(sb["colscr"][:, 0:1], ps, AX),
                  wait=(("pe", xw_pe),), inc_n=1)
            C.act(lambda e, tc=tc: e.activation(
                ar["sqs"][:, :], ar["qkh"][0:64, tc * 128:(tc + 1) * 128],
                AF.Square, bias=0.0, scale=1.0), wait=(("pe", C.c("pe")),))
            sq_a = C.c("act")
            C.pe(lambda e, tc=tc: e.transpose(
                psB[:, (tc % 2) * 512:(tc % 2) * 512 + 64], ar["sqs"][:, :],
                sb["ident"][0:64, 0:64]), wait=(("act", sq_a),), inc_n=1)
            tq_pe = C.c("pe")
            C.dve(lambda e, tc=tc: e.reduce_sum(
                sb["colscr"][:, 1:2], psB[:, (tc % 2) * 512:(tc % 2) * 512 + 64], AX),
                wait=(("pe", tq_pe),))
            C.dve(lambda e: e.tensor_scalar(sb["colscr"][:, 1:2],
                                            sb["colscr"][:, 1:2], -0.5, -LN16,
                                            ALU.mult, ALU.add))
            C.dve(lambda e: e.tensor_sub(sb["colscr"][:, 1:2], sb["colscr"][:, 1:2],
                                         sb["colscr"][:, 0:1]), inc_n=1)
            bq_d = C.c("dve")
            if DEBUG_STOP == 38:
                q1_marks[tc] = C.c("dve")
                continue
            C.pe(lambda e, tc=tc: e.transpose(
                psB[0:1, 1024 + (tc % 2) * 512: 1024 + (tc % 2) * 512 + 128],
                sb["colscr"][:, 1:2], sb["ident"][:, :]),
                wait=(("dve", bq_d),), inc_n=1)
            tb_pe = C.c("pe")
            if DEBUG_STOP == 39:
                q1_marks[tc] = C.c("dve")
                continue
            C.dve(lambda e, tc=tc: e.tensor_copy(
                sb["rowp16"][0:1, tc * 128:(tc + 1) * 128],
                psB[0:1, 1024 + (tc % 2) * 512: 1024 + (tc % 2) * 512 + 128]),
                wait=(("pe", tb_pe),))
            C.dve(lambda e, tc=tc: e.tensor_copy(
                sb["rowpf"][0:1, 0:128],
                sb["rowp16"][0:1, tc * 128:(tc + 1) * 128]))
            if DEBUG_STOP == 40:
                q1_marks[tc] = C.c("dve")
                continue
            C.dve(lambda e, tc=tc: e.tensor_sub(
                sb["rowpf"][0:1, 128:256],
                psB[0:1, 1024 + (tc % 2) * 512: 1024 + (tc % 2) * 512 + 128],
                sb["rowpf"][0:1, 0:128]))
            if DEBUG_STOP == 41:
                q1_marks[tc] = C.c("dve")
                continue
            C.dve(lambda e, tc=tc: e.tensor_copy(
                sb["rowp16"][32:33, tc * 128:(tc + 1) * 128],
                sb["rowpf"][0:1, 128:256]), inc_n=1)
            q1_marks[tc] = C.c("dve")
        brq_d = C.c("dve")
        if DEBUG_STOP in (37, 38, 39, 40, 41, 42, 43) and hh == 0:
            emit_output(C, M)
            return
        C.dma("gpsimd", M["brql"].ap(), sb["rowp16"][32:33, :],
              wait=(("dve", brq_d),), sem="st_nm")
        blq = C.c("st_nm")
        C.dma("sync", sb["rowp16"][1:2, :], APflat(M["brql"], 0, T),
              wait=(("st_nm", blq),), sem="q_nm")
        brq_nm = C.c("q_nm")
        if DEBUG_STOP == 35 and hh == 0:
            emit_output(C, M)
            return
        C.scope("B5_phq2")
        nd_marks = {}
        for nt2 in range(8):
            for mc in range(2):
                ps = psA[0:128, mc * 512: mc * 512 + 512]
                w = [("dve", brq_d), ("q_nm", brq_nm)] if (nt2 == 0 and mc == 0) else []
                if nt2 >= 1:
                    w.append(("dve", nd_marks.get(nt2 - 1, 0)))
                C.pe(lambda e, nt2=nt2, mc=mc, hh=hh, ps=ps: e.matmul(
                    ps, ar["wproj"][0:64, hh * MF + mc * 128: hh * MF + mc * 128 + 128],
                    ar["qkh"][0:64, nt2 * 512:(nt2 + 1) * 512],
                    start=True, stop=False), wait=tuple(w))
                C.pe(lambda e, nt2=nt2, ps=ps: e.matmul(
                    ps, sb["ones16m"][0:2, :],
                    sb["rowp16"][0:2, nt2 * 512:(nt2 + 1) * 512],
                    start=False, stop=True), inc_n=1)
                p2_pe = C.c("pe")
                C.act(lambda e, mc=mc, ps=ps: e.activation(
                    ar["bes2"][:, 0:MF] if False else ar["lnes"][:, mc * 512:(mc + 1) * 512],
                    ps, AF.Exp, bias=0.0, scale=1.0), wait=(("pe", p2_pe),))
                e_a = C.c("act")
                C.dve(lambda e, mc=mc: e.tensor_scalar_add(
                    ar["phq"][:, mc * 512:(mc + 1) * 512],
                    ar["lnes"][:, mc * 512:(mc + 1) * 512], EPSB / 16.0),
                    wait=(("act", e_a),), inc_n=1)
            phq_d = C.c("dve")
            for mc in range(2):
                C.pe(lambda e, mc=mc: e.matmul(
                    psB[0:65, 1024:1536], ar["kvh16"][:, mc * 65:(mc + 1) * 65],
                    ar["phq"][:, mc * 512:(mc + 1) * 512],
                    start=(mc == 0), stop=(mc == 1)),
                    wait=(("dve", phq_d),) if mc == 0 else (),
                    inc_n=1 if mc == 1 else 0)
            nd_pe = C.c("pe")
            C.dve(lambda e: e.tensor_scalar_add(sb["rowpf"][0:1, 1024:1536],
                                                psB[64:65, 1024:1536], EPSB),
                  wait=(("pe", nd_pe),))
            C.dve(lambda e: e.reciprocal(sb["rowpf"][0:1, 1536:2048],
                                         sb["rowpf"][0:1, 1024:1536]), inc_n=1)
            nd_d = C.c("dve")
            C.pe(lambda e: e.matmul(psB[0:64, 0:512], sb["ones_row64"][:, :],
                                    sb["rowpf"][0:1, 1536:2048],
                                    start=True, stop=True),
                 wait=(("dve", nd_d),), inc_n=1)
            ib_pe = C.c("pe")
            C.dve(lambda e: e.tensor_copy(ar["invb"][:, :], psB[0:64, 0:512]),
                  wait=(("pe", ib_pe),))
            C.dve(lambda e: e.tensor_mul(ar["mgt"][0:64, :], psB[0:64, 1024:1536],
                                         ar["invb"][:, :]),
                  wait=(("st_mg", C.c("st_mg")),), inc_n=1)
            mg_d = C.c("dve")
            nd_marks[nt2] = mg_d
            if DEBUG_STOP == 36 and hh == 0 and nt2 == 0:
                emit_output(C, M)
                return
            C.dma("gpsimd",
                  M["mgst"].ap()[hh // 2][(hh % 2) * 64:(hh % 2) * 64 + 64,
                                          nt2 * 512:(nt2 + 1) * 512],
                  ar["mgt"][0:64, :], wait=(("dve", mg_d),), sem="st_mg")
        mg_prev = C.c("dve")
    mg_store = C.c("st_mg")
    phase_sync()
    C.scope("B6_wo")
    C.dma("sync", ar["wout_s"][:, 0:4 * D].rearrange("p (k c) -> p k c", k=4),
          wst["woT"].ap().rearrange("k p c -> p k c"),
          wait=(attn_w, ("pe", C.c("pe"))), sem="q_wo2")
    wo2_ld = C.c("q_wo2")
    ev_prev = 0
    for nt in range(8):
        for kc2 in range(4):
            C.dma("sync", ar["gld"][:, kc2 * 512:(kc2 + 1) * 512],
                  M["mgst"].ap()[kc2][:, nt * 512:(nt + 1) * 512],
                  wait=((("st_mg", mg_store), ("pe", C.c("pe"))) if kc2 == 0
                        else ()), sem="q_gld")
        ml = C.c("q_gld")
        evd = {}
        for dm in range(KC):
            ps = psA[:, (dm % 4) * 512:(dm % 4) * 512 + 512]
            for kc2 in range(4):
                C.pe(lambda e, dm=dm, kc2=kc2, ps=ps: e.matmul(
                    ps, ar["wout_s"][:, kc2 * D + dm * 128: kc2 * D + dm * 128 + 128],
                    ar["gld"][:, kc2 * 512:(kc2 + 1) * 512],
                    start=(kc2 == 0), stop=(kc2 == 3)),
                    wait=(("q_gld", ml), ("q_wo2", wo2_ld),
                          ("dve", evd.get(dm - 4, ev_prev)))
                    if kc2 == 0 else (), inc_n=1 if kc2 == 3 else 0)
            wpe = C.c("pe")
            C.dve(lambda e, dm=dm, ps=ps: e.tensor_copy(
                ar["aotile"][:, dm * 512:(dm + 1) * 512], ps),
                wait=(("pe", wpe), ("st_ao", C.c("st_ao"))), inc_n=1)
            evd[dm] = C.c("dve")
        ev_prev = C.c("dve")
        for dm in range(KC):
            C.dma("gpsimd",
                  M["rs_in"].ap()[nt // 4, dm][:, (nt % 4) * 512:(nt % 4) * 512 + 512],
                  ar["aotile"][:, dm * 512:(dm + 1) * 512],
                  wait=(("dve", ev_prev),) if dm == 0 else (), sem="st_ao")
    rs_w = C.c("st_ao")
    C.op("gpsimd", lambda e: e.collective_compute(
        "ReduceScatter", ALU.add, replica_groups=M["groups"],
        ins=[M["rs_in"].ap().opt()], outs=[M["rs_out"].ap().opt()]),
        wait=(("st_ao", rs_w),), inc=(("cc", 1),))
    rs2_cc = C.c("cc")
    for kc in range(KC):
        sl = slice(kc * TOWN, (kc + 1) * TOWN)
        C.dma("sync", ar["lnes"][:, :], M["rs_out"].ap()[kc],
              wait=((("cc", rs2_cc), ("dve", C.c("dve")))
                    if kc == 0 else (("dve", C.c("dve")),)), sem="q_ao3")
        al = C.c("q_ao3")
        C.dve(lambda e, sl=sl: e.tensor_add(x[:, sl], x[:, sl], ar["lnes"][:, :]),
              wait=(("q_ao3", al),))
        C.dve(lambda e, sl=sl, kc=kc: e.tensor_scalar_add(
            x[:, sl], x[:, sl], sb["bo"][:, kc:kc + 1]), inc_n=1)
    if DEBUG_STOP == 3:
        emit_output(C, M)
        return

    emit_ffn(1)
    emit_output(C, M)


def emit_output(C, M):
    x = M["sb"]["x"]
    w = (("dve", C.c("dve")), ("pe", C.c("pe")), ("act", C.c("act")))
    for kc in range(KC):
        C.dma("gpsimd", M["out_fm"].ap()[kc], x[:, kc * TOWN:(kc + 1) * TOWN],
              wait=w if kc == 0 else (), sem="st_out")
    fin = C.c("st_out")
    C.op("gpsimd", lambda e: e.memset(M["sb"]["colscr"][:, 5:6], 0.0),
         wait=(("st_out", fin),), inc=(("pool", 1),))



def kernel(**inputs):
    f32 = np.float32
    x_full = np.asarray(inputs["L0_attn_a_x"], f32)
    W_in = np.asarray(inputs["L0_attn_a_W_in"], f32)
    b_in = np.asarray(inputs["L0_attn_a_b_in"], f32)
    in_ln_w = np.asarray(inputs["L0_attn_a_in_ln_w"], f32)
    in_ln_b = np.asarray(inputs["L0_attn_a_in_ln_b"], f32)
    W_out = np.asarray(inputs["L0_attn_a_W_out"], f32)
    out_ln_w = np.asarray(inputs["L0_attn_a_out_ln_w"], f32)
    out_ln_b = np.asarray(inputs["L0_attn_a_out_ln_b"], f32)
    smear = np.asarray(inputs["L0_attn_a_smear_factor"], f32)
    lscale = np.asarray(inputs["L0_attn_a_log_scale"], f32)
    Wq = np.asarray(inputs["L0_attn_b_Wq"], f32)
    Wk = np.asarray(inputs["L0_attn_b_Wk"], f32)
    Wv = np.asarray(inputs["L0_attn_b_Wv"], f32)
    Wo = np.asarray(inputs["L0_attn_b_Wo"], f32)
    bo = np.asarray(inputs["L0_attn_b_bo"], f32)
    projw = np.asarray(inputs["L0_attn_b_proj"], f32)
    sg_ = np.asarray(inputs["L0_ffn_shared_gate"], f32)
    su_ = np.asarray(inputs["L0_ffn_shared_up"], f32)
    sd_ = np.asarray(inputs["L0_ffn_shared_down"], f32)
    rg_ = np.asarray(inputs["L0_ffn_routed_gate"], f32)
    ru_ = np.asarray(inputs["L0_ffn_routed_up"], f32)
    rd_ = np.asarray(inputs["L0_ffn_routed_down"], f32)
    Wr = np.asarray(inputs["L0_ffn_Wr"], f32)
    rbias = np.asarray(inputs["L0_ffn_routing_bias"], f32)

    import ml_dtypes
    ac = np.ascontiguousarray
    WinT = W_in.T
    gates = ac(np.concatenate([sg_[None], rg_], 0).reshape(NEXP, KC, 128, HID))
    ups = ac(np.concatenate([su_[None], ru_], 0).reshape(NEXP, KC, 128, HID))
    downs = ac(np.concatenate([sd_[None], rd_], 0).reshape(NEXP, JCN, 128, D))

    ident = np.eye(128, dtype=f32)
    ones_col16 = np.ones((128, 1), dtype=ml_dtypes.bfloat16)
    ones_row16 = np.ones((1, 128), dtype=ml_dtypes.bfloat16)
    ones16m = np.ones((128, 128), dtype=ml_dtypes.bfloat16)
    ones_col64_16 = np.ones((64, 1), dtype=ml_dtypes.bfloat16)
    ones_row64 = np.ones((1, 64), dtype=f32)
    ones_col32 = np.ones((128, 1), dtype=f32)
    tri_loc = np.zeros((128, 129), dtype=f32)
    tri_loc[:, :128] = np.tril(np.ones((128, 128), f32)).T
    tri_loc[:, 128] = 1.0
    tri_off = np.triu(np.ones((32, 32), f32), k=1)
    NEGB = np.float32(-1e9)
    iu = np.arange(896)[None, :]
    ip = np.arange(128)[:, None]
    mask1m = np.where(iu > 384 + ip, NEGB, 0).astype(f32)
    mask2m = np.where(ip > iu - 384, NEGB, 0).astype(f32)

    in_maps = []
    for c in range(8):
        b, r = c // 2, c % 2
        ysl = slice(8192 + 8 * r, 8192 + 8 * r + 8)
        win_own = np.concatenate(
            [WinT[:, blk * 2048 + 1024 * r: blk * 2048 + 1024 * r + 1024]
             for blk in range(4)] + [WinT[:, ysl]], axis=1)
        bqkp = np.stack([
            b_in[0 * 2048 + 1024 * r: 0 * 2048 + 1024 * r + 1024].reshape(8, 128),
            b_in[1 * 2048 + 1024 * r: 1 * 2048 + 1024 * r + 1024].reshape(8, 128),
            b_in[3 * 2048 + 1024 * r: 3 * 2048 + 1024 * r + 1024].reshape(8, 128),
        ], axis=0)
        bqkp = ac(bqkp.reshape(24, 128).T)
        m = dict(
            xT=ac(x_full[b, r * TOWN:(r + 1) * TOWN, :].T.reshape(KC, 128, TOWN)),
            win=ac(win_own.reshape(KC, 128, 4104)),
            bqkp=bqkp,
            bvrow=ac(b_in[2 * 2048 + 1024 * r: 2 * 2048 + 1024 * r + 1024].reshape(8, 128)),
            byrow=ac(b_in[ysl][None, :]),
            lnw=ac(in_ln_w.reshape(KC, 128).T), lnb=ac(in_ln_b.reshape(KC, 128).T),
            olnw=ac(out_ln_w.reshape(KC, 128).T), olnb=ac(out_ln_b.reshape(KC, 128).T),
            woutT=ac(W_out.T[1024 * r:1024 * r + 1024].reshape(KC, 128, D)),
            smear=ac(smear[8 * r:8 * r + 8][None, :]),
            lscale=ac(lscale[8 * r:8 * r + 8][None, :]),
            wqT=ac(Wq.T[:, 512 * r:512 * r + 512].reshape(KC, 128, 512)),
            wkT=ac(Wk.T[:, 512 * r:512 * r + 512].reshape(KC, 128, 512)),
            wvT=ac(Wv.T[:, 512 * r:512 * r + 512].reshape(KC, 128, 512)),
            woT=ac(Wo.T[512 * r:512 * r + 512].reshape(4, 128, D)),
            bo=ac(bo.reshape(KC, 128).T),
            projT=ac(projw[8 * r:8 * r + 8].transpose(0, 2, 1)),
            gates=gates, ups=ups, downs=downs,
            wr=ac(Wr.reshape(KC, 128, 7)), rbias=ac(rbias[None, :]),
            ident=ident, ones_col16=ones_col16, ones_row16=ones_row16,
            ones16m=ones16m,
            ones_col64_16=ones_col64_16, ones_row64=ones_row64,
            ones_col32=ones_col32, tri_loc=tri_loc, tri_off=tri_off,
            mask1m=mask1m, mask2m=mask2m,
        )
        in_maps.append(m)

    nc = build_program()
    res = run_bass_kernel_spmd(nc, in_maps, core_ids=list(range(8)))
    out = np.empty((4, T, D), dtype=np.float32)
    for c in range(8):
        b, r = c // 2, c % 2
        o = res.results[c]["out_fm"]
        out[b, r * TOWN:(r + 1) * TOWN, :] = o.transpose(2, 0, 1).reshape(TOWN, D)
    return out


if __name__ == "__main__":
    import time
    t0 = time.time()
    nc = build_program()
    print(f"build {time.time()-t0:.1f}s n_inst={nc.n_instructions()}")


# version 03
# speedup vs baseline: 1.3424x; 1.0865x over previous
import numpy as np
import concourse.bass as bass
import concourse.mybir as mybir
from concourse.bass_utils import run_bass_kernel_spmd

F32 = mybir.dt.float32
BF16 = mybir.dt.bfloat16
AF = mybir.ActivationFunctionType
ALU = mybir.AluOpType
AX = mybir.AxisListType.X

D = 1024
KC = 8
T = 4096
TOWN = 2048
HN = 8
DK = 64
MF = 256
HID = 2816
JCN = 22
NEXP = 8
LNEPS = 1e-5
LN16 = float(np.log(16.0))
LNS128 = 0.5 * float(np.log(128.0))
EPSB = 1e-6
KB = 1024

DEBUG_STOP = None


def APbc(handle, off, p, n):
    return bass.AP(handle.ap().tensor, off, [[0, p], [1, n]])


def APflat(handle, off, n):
    return bass.AP(handle.ap().tensor, off, [[0, 1], [1, n]])


class Ctx:

    def __init__(self, nc, eng, name, sems):
        self.nc, self.e, self.name, self.sems = nc, eng, name, sems
        self.cnt = {k: 0 for k in sems}
        self._scope = None
        self._sid = None

    def scope(self, name):
        if self._scope is not None:
            self.nc.leave_named_scope(self._scope, self._sid, False)
            self._scope = None
        if name is not None:
            self._sid, _ = self.nc.enter_named_scope(name, False)
            self._scope = name

    def c(self, s):
        return self.cnt[s]

    def op(self, eng, fn, wait=(), inc=()):
        for s, n in inc:
            self.cnt[s] += n
        if eng != self.name:
            return
        for s, v in wait:
            if v > 0:
                self.e.wait_ge(self.sems[s], v)
        if fn is None:
            return
        ins = fn(self.e)
        for s, n in inc:
            ins.then_inc(self.sems[s], n)

    def dma(self, q, out, in_, wait=(), sem=None, **kw):
        assert sem is not None
        self.op(q, lambda e: e.dma_start(out=out, in_=in_, **kw),
                wait=wait, inc=((sem, 16),))
        return self.cnt[sem]

    def pe(self, fn, wait=(), inc_n=0):
        self.op("tensor", fn, wait=wait, inc=((("pe", inc_n),) if inc_n else ()))
        return self.cnt["pe"]

    def dve(self, fn, wait=(), inc_n=0):
        w = tuple(wait) + (("dve", self.cnt["dve"]),)
        self.op("vector", fn, wait=w, inc=(("dve", 1),))
        return self.cnt["dve"]

    def act(self, fn, wait=(), inc_n=1):
        w = tuple(wait) + (("act", self.cnt["act"]),)
        self.op("scalar", fn, wait=w, inc=(("act", 1),))
        return self.cnt["act"]

    def pool(self, fn, wait=(), inc_n=1):
        w = tuple(wait) + (("pool", self.cnt["pool"]),)
        self.op("gpsimd", fn, wait=w, inc=(("pool", 1),))
        return self.cnt["pool"]


def build_program():
    nc = bass.Bass("TRN2", target_bir_lowering=False)

    def din(name, shape, dt=F32):
        return nc.dram_tensor(name, list(shape), dt, kind="ExternalInput")

    h = {}
    h["xT"] = din("xT", [KC, 128, TOWN])
    h["win"] = din("win", [KC, 128, 4104])
    h["bqkp"] = din("bqkp", [128, 3 * HN])
    h["bvrow"] = din("bvrow", [HN, 128])
    h["byrow"] = din("byrow", [1, HN])
    h["lnw"] = din("lnw", [128, KC]); h["lnb"] = din("lnb", [128, KC])
    h["olnw"] = din("olnw", [128, KC]); h["olnb"] = din("olnb", [128, KC])
    h["woutT"] = din("woutT", [KC, 128, D])
    h["smear"] = din("smear", [1, HN])
    h["lscale"] = din("lscale", [1, HN])
    h["wqT"] = din("wqT", [KC, 128, 512])
    h["wkT"] = din("wkT", [KC, 128, 512])
    h["wvT"] = din("wvT", [KC, 128, 512])
    h["woT"] = din("woT", [4, 128, D])
    h["bo"] = din("bo", [128, KC])
    h["projT"] = din("projT", [HN, DK, MF])
    h["gates"] = din("gates", [NEXP, KC, 128, HID])
    h["ups"] = din("ups", [NEXP, KC, 128, HID])
    h["downs"] = din("downs", [NEXP, JCN, 128, D])
    h["wr"] = din("wr", [KC, 128, 7])
    h["rbias"] = din("rbias", [1, 7])
    h["ident"] = din("ident", [128, 128])
    h["ones_col16"] = din("ones_col16", [128, 1], BF16)
    h["ones_row16"] = din("ones_row16", [1, 128], BF16)
    h["ones16m"] = din("ones16m", [128, 128], BF16)
    h["ones_col64_16"] = din("ones_col64_16", [64, 1], BF16)
    h["ones_row64"] = din("ones_row64", [1, 64])
    h["ones_col32"] = din("ones_col32", [128, 1])
    h["tri_loc"] = din("tri_loc", [128, 129])
    h["tri_off"] = din("tri_off", [32, 32])
    h["mask1m"] = din("mask1m", [128, 896])
    h["mask2m"] = din("mask2m", [128, 896])

    out_fm = nc.dram_tensor("out_fm", [KC, 128, TOWN], F32, kind="ExternalOutput")

    def dint(name, shape, dt):
        return nc.dram_tensor(name, list(shape), dt, kind="Internal")

    wst = {k: dint("wst_" + k, shp, BF16) for k, shp in [
        ("win", [KC, 128, 4104]), ("woutT", [KC, 128, D]),
        ("wqT", [KC, 128, 512]), ("wkT", [KC, 128, 512]),
        ("wvT", [KC, 128, 512]), ("woT", [4, 128, D]),
        ("projT", [HN, DK, MF]), ("gates", [NEXP, KC, 128, HID]),
        ("ups", [NEXP, KC, 128, HID]), ("downs", [NEXP, JCN, 128, D])]}
    M = dict(nc=nc, h=h, wst=wst, out_fm=out_fm)
    M["ag_in"] = dint("ag_in", [KC, 128, TOWN], BF16)
    M["ag_out"] = dint("ag_out", [2 * KC, 128, TOWN], BF16)
    M["rs_in"] = dint("rs_in", [2, KC, 128, TOWN], F32)
    M["rs_out"] = dint("rs_out", [KC, 128, TOWN], F32)
    M["qst"] = dint("qst", [HN, 128, T], BF16)
    M["kst"] = dint("kst", [HN, 128, T], BF16)
    M["vst"] = dint("vst", [HN, 128, T], BF16)
    M["spst"] = dint("spst", [HN, 128, T], BF16)
    M["gst"] = dint("gst", [HN, 128, T], BF16)
    M["mgst"] = dint("mgst", [4, 128, T], BF16)
    M["rowd"] = dint("rowd", [2, TOWN], F32)
    M["lnrows"] = dint("lnrows", [3, TOWN], F32)
    M["poshd"] = dint("poshd", [HN, T], BF16)
    M["posld"] = dint("posld", [HN, T], BF16)
    M["mrowd"] = dint("mrowd", [32, 128], BF16)
    M["brql"] = dint("brql", [1, T], BF16)
    M["linvd"] = dint("linvd", [HN, T], F32)
    M["wrowd"] = dint("wrowd", [7, TOWN], F32)
    M["prmd"] = dint("prmd", [1, 3 * HN], F32)
    M["groups"] = [[0, 1], [2, 3], [4, 5], [6, 7]]

    sb = {}

    def asb(name, shape, dt=F32):
        sb[name] = nc.alloc_sbuf_tensor("s_" + name, list(shape), dt)
        return sb[name]

    asb("x", [128, KC * TOWN])
    asb("ident", [128, 128])
    asb("ones_col16", [128, 1], BF16)
    asb("ones_row16", [1, 128], BF16)
    asb("ones16m", [128, 128], BF16)
    asb("ones_col64_16", [64, 1], BF16)
    asb("ones_row64", [1, 64])
    asb("ones_col32", [128, 1])
    asb("tri_loc", [128, 129])
    asb("tri_off", [32, 32])
    asb("lnw", [128, KC]); asb("lnb", [128, KC])
    asb("olnw", [128, KC]); asb("olnb", [128, KC])
    asb("bo", [128, KC])
    asb("bqkp", [128, 3 * HN])
    asb("pcol", [128, 3 * HN])
    asb("prows", [1, 64])
    asb("ywin", [128, KC * HN], BF16)
    asb("ybb", [128, HN])
    asb("vbb", [128, 128])
    asb("sigy", [128, 32 * HN])
    asb("pos_t", [128, 32 * HN])
    asb("negm32", [128, 32])
    asb("linv", [128, 32])
    asb("rowpf", [128, 2048])
    asb("lncol", [128, 128])
    asb("rowp16", [128, T], BF16)
    asb("rcols", [128, 16])
    asb("rbb", [128, 7])
    asb("colscr", [128, 24])
    asb("colscr3", [128, 8], BF16)
    asb("wt7", [7, 16 * 128])

    ARENA_B = 112 * KB
    arena = nc.alloc_sbuf_tensor("arena", [128, ARENA_B], mybir.dt.uint8)
    abase = nc.lookup_mloc(arena).addr
    ar = {}

    def av(name, off, shape, dt=F32):
        t = nc.alloc_sbuf_tensor_at("a_" + name, list(shape), dt, offset=abase + off)
        nb = int(np.prod(shape[1:])) * mybir.dt.size(dt)
        assert off + nb <= ARENA_B, (name, off, nb)
        ar[name] = t
        return t

    av("xnf", 0, [128, KC * T], BF16)
    av("xno", 0, [128, KC * TOWN], BF16)
    av("lncb", 64 * KB, [128, 2 * TOWN], BF16)
    av("lncq32", 64 * KB, [128, TOWN])
    av("lncq", 72 * KB, [128, 2 * TOWN], BF16)
    av("Rb", 80 * KB, [128, TOWN])
    av("Bb", 88 * KB, [128, TOWN])
    av("lnes", 96 * KB, [128, TOWN])
    av("lnes2", 64 * KB, [128, TOWN])
    av("lnrt", 104 * KB, [48, 128])
    av("posfm", 64 * KB, [8, T])
    av("poshi16", 80 * KB, [8, T], BF16)
    av("poslo16", 88 * KB, [8, T], BF16)
    av("hi32h", 96 * KB, [8, TOWN])
    av("winblk", 64 * KB, [128, 2 * KC * 128], BF16)
    av("stg16", 68 * KB, [128, T], BF16)
    av("ksc", 76 * KB, [128, T], BF16)
    av("qh", 0, [128, 2 * T], BF16)
    av("kh", 16 * KB, [128, 2 * T], BF16)
    av("vh", 32 * KB, [128, 2 * T], BF16)
    av("sph", 48 * KB, [128, 2 * T], BF16)
    av("pts", 64 * KB, [128, 2 * 512], BF16)
    av("pthrow", 66 * KB, [128, 2048], BF16)
    av("esmall", 70 * KB, [128, 512])
    av("gtile", 72 * KB, [128, 512], BF16)
    av("invlb", 73 * KB, [128, 512])
    av("nm32t", 75 * KB, [32, 128])
    av("nm16t", 75 * KB + 512, [32, 128], BF16)
    av("masks1", 76 * KB, [128, 896])
    av("masks2", 76 * KB + 3584, [128, 896])
    av("wout_s", 0, [128, KC * D], BF16)
    av("gld", 16 * KB, [128, HN * 512], BF16)
    av("aotile", 24 * KB, [128, KC * 512])
    av("hbuf", 32 * KB, [128, JCN * 1024], BF16)
    av("dquart", 76 * KB, [128, JCN * 256], BF16)
    av("gring", 87 * KB, [128, 4 * 1024], BF16)
    av("uring", 95 * KB, [128, 4 * 1024], BF16)
    av("wbb", 103 * KB, [128, 1024])
    av("sg", 107 * KB, [128, 1024])
    av("fes", 107 * KB, [128, 512])
    av("ltile", 111 * KB, [128, 8])
    av("ptile", 111 * KB + 64, [128, 8])
    av("wr32", 111 * KB + 128, [128, KC * 7])
    av("wqkv3", 64 * KB, [128, 3 * 512], BF16)
    av("wproj", 67 * KB, [128, HN * MF], BF16)
    av("qkh", 71 * KB, [128, T], BF16)
    av("vb_h", 79 * KB, [128, 32 * 65], BF16)
    av("phk16", 83 * KB + 512, [128, MF], BF16)
    av("sqs", 84 * KB, [64, 128])
    av("bes2", 84 * KB + 512, [128, MF])
    av("kvh", 85 * KB + 512, [128, 2 * 65])
    av("kvh16", 86 * KB + 64, [128, 2 * 65], BF16)
    av("phq", 86 * KB + 512, [128, 2 * 512], BF16)
    av("invb", 88 * KB + 512, [64, 512])
    av("mgt", 90 * KB + 512, [128, 512], BF16)
    M["sb"], M["ar"] = sb, ar

    M["psA"] = nc.alloc_psum_tensor("psA", [128, 2048], F32)
    M["psB"] = nc.alloc_psum_tensor("psB", [128, 2048], F32)

    semnames = ["q_p0", "q_prm", "q_ywin", "q_xnf", "q_win0", "q_win1",
                "q_mask", "q_hl0", "q_hl1", "q_rp", "q_nm", "q_gld", "q_wout",
                "q_wo2", "q_wr", "q_wqkv", "q_wproj", "q_mg", "q_dh",
                "q_g0", "q_g1", "q_g2", "q_g3", "q_u0", "q_u1", "q_u2", "q_u3",
                "q_cvi", "q_cvoA", "q_cvoF",
                "q_pcol", "q_ybb", "q_vbb", "q_RB", "q_inv", "q_wbb", "q_rbb",
                "st_prm", "st_lnr", "st_pos", "st_stg", "st_nm", "st_g",
                "st_ao", "st_wrow", "st_mg", "st_out", "st_ag", "q_ao3",
                "pe", "dve", "act", "pool", "cc"]
    sems = {s: nc.alloc_semaphore(name=s) for s in semnames}
    M["sems"] = sems

    with nc.Block() as block:
        @block.sync
        def _(e):
            _C = Ctx(nc, e, "sync", sems)
            emit_program(_C, M)
            _C.scope(None)

        @block.tensor
        def _(e):
            _C = Ctx(nc, e, "tensor", sems)
            emit_program(_C, M)
            _C.scope(None)

        @block.vector
        def _(e):
            _C = Ctx(nc, e, "vector", sems)
            emit_program(_C, M)
            _C.scope(None)

        @block.scalar
        def _(e):
            _C = Ctx(nc, e, "scalar", sems)
            emit_program(_C, M)
            _C.scope(None)

        @block.gpsimd
        def _(e):
            _C = Ctx(nc, e, "gpsimd", sems)
            emit_program(_C, M)
            _C.scope(None)

    return nc


def emit_program(C, M):
    h, wst, sb, ar = M["h"], M["wst"], M["sb"], M["ar"]
    psA, psB = M["psA"], M["psB"]
    x = sb["x"]
    xnf = ar["xnf"]

    def phase_sync():
        w = tuple((s, C.c(s)) for s in ("pe", "dve", "act"))
        for en in ("sync", "tensor", "vector", "scalar", "gpsimd"):
            C.op(en, None, wait=w)

    C.scope("P0_load")
    for kc in range(KC):
        C.dma("sync", x[:, kc * TOWN:(kc + 1) * TOWN], h["xT"].ap()[kc], sem="q_p0")
    for name in ["ident", "ones_col16", "ones_row16", "ones16m", "ones_col64_16",
                 "ones_row64", "ones_col32", "tri_loc", "tri_off",
                 "lnw", "lnb", "olnw", "olnb", "bo", "bqkp"]:
        C.dma("sync", sb[name][:, :], h[name].ap(), sem="q_p0")
    x_ld = C.c("q_p0")

    conv_jobs = [(k, None, "q_cvoA") for k in
                 ["win", "woutT", "wqT", "wkT", "wvT", "woT", "projT"]]
    N_ATTN_JOBS = len(conv_jobs)
    for k in ["gates", "ups", "downs"]:
        for o in range(NEXP):
            conv_jobs.append((k, o, "q_cvoF"))
    cs = dict(i=0)

    def conv_pump(n):
        for _ in range(n):
            i = cs["i"]
            if i >= len(conv_jobs):
                return
            cs["i"] = i + 1
            dkey, o, osem = conv_jobs[i]
            sv, dv = h[dkey].ap(), wst[dkey].ap()
            if o is None:
                C.dma("gpsimd", dv, sv, sem=osem)
            else:
                C.dma("gpsimd", dv[o], sv[o], sem=osem)

    conv_pump(N_ATTN_JOBS)
    attn_w = ("q_cvoA", C.c("q_cvoA"))
    if DEBUG_STOP == 11:
        emit_output(C, M)
        return

    C.dma("sync", sb["prows"][0:1, 0:HN], h["lscale"].ap(), sem="q_prm")
    C.dma("sync", sb["prows"][0:1, 8:16], h["smear"].ap(), sem="q_prm")
    ldp = C.c("q_prm")
    C.dve(lambda e: e.tensor_scalar(sb["prows"][0:1, 48:56], sb["prows"][0:1, 0:8],
                                    -2.0, -LNS128, ALU.mult, ALU.add),
          wait=(("q_prm", ldp),), inc_n=1)
    C.act(lambda e: e.activation(sb["prows"][0:1, 16:24], sb["prows"][0:1, 48:56],
                                 AF.Exp, bias=0.0, scale=1.0),
          wait=(("dve", C.c("dve")),))
    C.act(lambda e: e.activation(sb["prows"][0:1, 24:32], sb["prows"][0:1, 8:16],
                                 AF.Sigmoid, bias=0.0, scale=1.0))
    pa = C.c("act")
    C.dve(lambda e: e.tensor_scalar(sb["prows"][0:1, 32:40], sb["prows"][0:1, 24:32],
                                    -1.0, 1.0, ALU.mult, ALU.add),
          wait=(("act", pa),), inc_n=1)
    pd = C.c("dve")
    C.dma("gpsimd", M["prmd"].ap()[0:1, :], sb["prows"][0:1, 16:40],
          wait=(("dve", pd),), sem="st_prm")
    g0 = C.c("st_prm")
    C.dma("scalar", sb["pcol"][:, :], APbc(M["prmd"], 0, 128, 24),
          wait=(("st_prm", g0),), sem="q_pcol")
    pcol_ld = C.c("q_pcol")

    def ccol(hh):
        return sb["pcol"][:, hh:hh + 1]

    def smcol(hh):
        return sb["pcol"][:, 8 + hh:9 + hh]

    def sm1col(hh):
        return sb["pcol"][:, 16 + hh:17 + hh]

    def emit_ln(src, out16, affine=False, fused_double=False, wcol=None,
                bcol=None, router_rows=False, src_wait=()):
        cq = ar["lncq32"]
        Rb, Bb = ar["Rb"], ar["Bb"]
        mucol = sb["lncol"][:, 0:16]
        vcol = sb["lncol"][:, 16:32]
        Rcol = sb["lncol"][:, 32:48]
        Bcol = sb["lncol"][:, 48:64]
        acc_s = sb["lncol"][:, 96:112]
        acc_q = sb["lncol"][:, 112:128]
        mm_marks = []
        for kc in range(KC):
            sl = slice(kc * TOWN, (kc + 1) * TOWN)
            w = list(src_wait) if kc == 0 else []
            if kc == 0:
                w.append(("pe", C.c("pe")))
                w.append(("dve", C.c("dve")))
            if kc >= 1:
                w.append(("pe", mm_marks[kc - 1][1]))
            C.act(lambda e, sl=sl: e.activation(cq[:, :], src[:, sl], AF.Square,
                                                bias=0.0, scale=1.0),
                  wait=tuple(w))
            ca = C.c("act")
            for tg in range(16):
                C.pe(lambda e, tg=tg, sl=sl: e.matmul(
                    psA[:, 1536 + tg:1536 + tg + 1],
                    src[:, sl.start + tg * 128: sl.start + tg * 128 + 128],
                    sb["ones_col32"][:, :], start=True, stop=True),
                    wait=((("act", ca),) +
                          ((("dve", mm_marks[kc - 1][0]),) if kc >= 1 else ())
                          if tg == 0 else ()))
                C.pe(lambda e, tg=tg: e.matmul(
                    psB[:, 1536 + tg:1536 + tg + 1],
                    cq[:, tg * 128:(tg + 1) * 128],
                    sb["ones_col32"][:, :], start=True, stop=True),
                    inc_n=1 if tg == 15 else 0)
            mmk = C.c("pe")
            if kc == 0:
                C.dve(lambda e: e.tensor_copy(acc_s, psA[:, 1536:1552]),
                      wait=(("pe", mmk),))
                C.dve(lambda e: e.tensor_copy(acc_q, psB[:, 1536:1552]), inc_n=1)
            else:
                C.dve(lambda e: e.tensor_add(acc_s, acc_s, psA[:, 1536:1552]),
                      wait=(("pe", mmk),))
                C.dve(lambda e: e.tensor_add(acc_q, acc_q, psB[:, 1536:1552]),
                      inc_n=1)
            mm_marks.append((C.c("dve"), mmk))
        C.dve(lambda e: e.tensor_scalar_mul(mucol, acc_s, 1.0 / D))
        C.dve(lambda e: e.tensor_scalar_mul(vcol, acc_q, 1.0 / D))
        C.dve(lambda e: e.tensor_mul(Bcol, mucol, mucol))
        C.dve(lambda e: e.tensor_sub(vcol, vcol, Bcol))
        eb, sc = (LNEPS * LNEPS, 1.0 + LNEPS) if fused_double else (LNEPS, 1.0)
        C.dve(lambda e, eb=eb, sc=sc: e.tensor_scalar(vcol, vcol, sc, eb,
                                                      ALU.mult, ALU.add), inc_n=1)
        var_d = C.c("dve")
        C.act(lambda e: e.activation(Bcol, vcol, AF.Ln, bias=0.0, scale=1.0),
              wait=(("dve", var_d),))
        C.act(lambda e: e.activation(Rcol, Bcol, AF.Exp, bias=0.0, scale=-0.5))
        r_a = C.c("act")
        C.dve(lambda e: e.tensor_mul(Bcol, mucol, Rcol), wait=(("act", r_a),))
        C.dve(lambda e: e.tensor_scalar_mul(Bcol, Bcol, -1.0))
        C.dve(lambda e: e.tensor_scalar_mul(sb["lncol"][:, 64:80], mucol, -1.0))
        if router_rows:
            C.dve(lambda e: e.tensor_copy(sb["rcols"][:, :], Rcol))
        rb_d = C.c("dve")
        C.pe(lambda e: e.transpose(psA[0:48, 0:128], sb["lncol"][:, 32 if not router_rows else 32:32], sb["ident"][:, :])
             if False else e.transpose(psA[0:48, 0:128], sb["lncol"][:, 32:80],
                                       sb["ident"][:, :]),
             wait=(("dve", rb_d),), inc_n=1)
        tr_pe = C.c("pe")
        C.dve(lambda e: e.tensor_copy(ar["lnrt"][0:48, :], psA[0:48, 0:128]),
              wait=(("pe", tr_pe),), inc_n=1)
        ev_d = C.c("dve")
        C.dma("gpsimd", M["lnrows"].ap(), ar["lnrt"][0:48, :],
              wait=(("dve", ev_d),), sem="st_lnr")
        g1 = C.c("st_lnr")
        C.dma("scalar", Rb[:, :], APbc(M["lnrows"], 0, 128, TOWN),
              wait=(("st_lnr", g1), ("dve", C.c("dve"))), sem="q_RB")
        C.dma("scalar", Bb[:, :], APbc(M["lnrows"], TOWN, 128, TOWN), sem="q_RB")
        if router_rows:
            C.dma("scalar", sb["rowpf"][0:1, :], APflat(M["lnrows"], 2 * TOWN, TOWN),
                  sem="q_RB")
        bc_v = C.c("q_RB")
        esf = ar["lnes"]
        for kc in range(KC):
            sl = slice(kc * TOWN, (kc + 1) * TOWN)
            C.dve(lambda e, sl=sl: e.tensor_mul(esf[:, :], src[:, sl], Rb[:, :]),
                  wait=(("q_RB", bc_v),) if kc == 0 else ())
            if affine:
                C.dve(lambda e: e.tensor_add(esf[:, :], esf[:, :], Bb[:, :]))
                C.dve(lambda e, sl=sl, kc=kc: e.tensor_scalar(
                    out16[:, sl], esf[:, :], wcol[:, kc:kc + 1],
                    bcol[:, kc:kc + 1], ALU.mult, ALU.add), inc_n=1)
            else:
                C.dve(lambda e, sl=sl: e.tensor_add(out16[:, sl], esf[:, :],
                                                    Bb[:, :]), inc_n=1)
        return C.c("dve")

    def emit_ag(out16own, xn_done):
        for kc in range(KC):
            C.dma("gpsimd", M["ag_in"].ap()[kc],
                  out16own[:, kc * TOWN:(kc + 1) * TOWN],
                  wait=(("dve", xn_done),) if kc == 0 else (), sem="st_ag")
        agw = C.c("st_ag")
        C.op("gpsimd", lambda e: e.collective_compute(
            "AllGather", ALU.bypass, replica_groups=M["groups"],
            ins=[M["ag_in"].ap().opt()], outs=[M["ag_out"].ap().opt()]),
            wait=(("st_ag", agw),), inc=(("cc", 1),))
        ccv = C.c("cc")
        for kc in range(KC):
            for rk in range(2):
                C.dma("sync", xnf[:, kc * T + rk * TOWN: kc * T + (rk + 1) * TOWN],
                      M["ag_out"].ap()[rk * KC + kc],
                      wait=(("cc", ccv), ("pe", C.c("pe"))) if (kc == 0 and rk == 0)
                      else (), sem="q_xnf")
        return C.c("q_xnf")

    C.scope("P1_ln")
    xn_done = emit_ln(x, ar["xno"], affine=True, fused_double=True,
                      wcol=sb["lnw"], bcol=sb["lnb"],
                      src_wait=(("q_p0", x_ld),))
    xnf_ld = emit_ag(ar["xno"], xn_done)
    if DEBUG_STOP == 12:
        emit_output(C, M)
        return

    phase_sync()
    C.scope("P2_pos")
    C.dma("sync", sb["ywin"][:, :].rearrange("p (k c) -> p k c", k=KC),
          wst["win"].ap().rearrange("k p c -> p k c")[:, :, 4096:4104],
          wait=(attn_w,), sem="q_ywin")
    yld = C.c("q_ywin")
    C.dma("scalar", sb["ybb"][:, :], APbc(h["byrow"], 0, 128, HN), sem="q_ybb")
    ybld = C.c("q_ybb")
    y_marks = []
    for tc in range(32):
        ps = psA[:, 0:HN] if tc % 2 == 0 else psA[:, 512:512 + HN]
        if tc == 0:
            w = [("q_ywin", yld), ("q_xnf", xnf_ld), ("q_ybb", ybld)]
        elif tc >= 2:
            w = [("act", y_marks[tc - 2])]
        else:
            w = []
        for kc in range(KC):
            C.pe(lambda e, tc=tc, kc=kc, ps=ps: e.matmul(
                ps, xnf[:, kc * T + tc * 128: kc * T + tc * 128 + 128],
                sb["ywin"][:, kc * HN:(kc + 1) * HN],
                start=(kc == 0), stop=(kc == KC - 1)),
                wait=tuple(w) if kc == 0 else (), inc_n=1 if kc == KC - 1 else 0)
        ypet = C.c("pe")
        C.dve(lambda e, ps=ps: e.tensor_add(ar["lnes"][:, 0:HN], ps, sb["ybb"][:, :]),
              wait=(("pe", ypet), ("act", C.c("act"))), inc_n=1)
        ydv = C.c("dve")
        C.act(lambda e, tc=tc: e.activation(sb["sigy"][:, tc * HN:(tc + 1) * HN],
                                            ar["lnes"][:, 0:HN], AF.Sigmoid,
                                            bias=0.0, scale=1.0),
              wait=(("dve", ydv),))
        y_marks.append(C.c("act"))
    posfm = ar["posfm"]
    tot_marks = []
    for tc in range(32):
        w = [("act", y_marks[-1])] if tc == 0 else []
        if tc >= 4:
            w.append(("dve", tot_marks[tc - 4]))
        C.pe(lambda e, tc=tc: e.matmul(
            psB[0:HN, (tc % 4) * 512:(tc % 4) * 512 + 129],
            sb["sigy"][:, tc * HN:(tc + 1) * HN], sb["tri_loc"][:, :],
            start=True, stop=True), wait=tuple(w), inc_n=1)
        tpe = C.c("pe")
        C.dve(lambda e, tc=tc: e.tensor_copy(
            posfm[:, tc * 128:(tc + 1) * 128],
            psB[0:HN, (tc % 4) * 512:(tc % 4) * 512 + 128]),
            wait=(("pe", tpe),))
        C.dve(lambda e, tc=tc: e.tensor_copy(
            sb["pos_t"][0:HN, tc:tc + 1],
            psB[0:HN, (tc % 4) * 512 + 128:(tc % 4) * 512 + 129]), inc_n=1)
        tot_marks.append(C.c("dve"))
    C.pe(lambda e: e.transpose(psA[0:32, 0:HN], sb["pos_t"][0:HN, 0:32],
                               sb["ident"][0:HN, 0:HN]),
         wait=(("dve", tot_marks[-1]),), inc_n=1)
    C.dve(lambda e: e.tensor_copy(ar["lnes"][0:32, 0:HN], psA[0:32, 0:HN]),
          wait=(("pe", C.c("pe")),), inc_n=1)
    C.pe(lambda e: e.matmul(psA[0:HN, 512:512 + 32], ar["lnes"][0:32, 0:HN],
                            sb["tri_off"][:, :], start=True, stop=True),
         wait=(("dve", C.c("dve")),), inc_n=1)
    C.dve(lambda e: e.tensor_copy(ar["lnes"][0:HN, 0:32], psA[0:HN, 512:512 + 32]),
          wait=(("pe", C.c("pe")),), inc_n=1)
    for tc in range(32):
        C.dve(lambda e, tc=tc: e.tensor_scalar_add(
            posfm[:, tc * 128:(tc + 1) * 128], posfm[:, tc * 128:(tc + 1) * 128],
            ar["lnes"][0:HN, tc:tc + 1]), inc_n=(1 if tc == 31 else 0))
    hi16, lo16, hi32 = ar["poshi16"], ar["poslo16"], ar["hi32h"]
    for hf in range(2):
        sl = slice(hf * TOWN, (hf + 1) * TOWN)
        C.dve(lambda e, sl=sl: e.tensor_copy(hi16[:, sl], posfm[:, sl]))
        C.dve(lambda e, sl=sl: e.tensor_copy(hi32[:, :], hi16[:, sl]))
        C.dve(lambda e, sl=sl: e.tensor_sub(posfm[:, sl], posfm[:, sl], hi32[:, :]))
        C.dve(lambda e, sl=sl: e.tensor_copy(lo16[:, sl], posfm[:, sl]))
        C.dve(lambda e, sl=sl: e.tensor_copy(posfm[:, sl], lo16[:, sl]))
        C.dve(lambda e, sl=sl: e.tensor_add(posfm[:, sl], posfm[:, sl], hi32[:, :]),
              inc_n=1)
    pos_d = C.c("dve")
    C.dma("gpsimd", M["poshd"].ap(), hi16[0:HN, :], wait=(("dve", pos_d),), sem="st_pos")
    C.dma("gpsimd", M["posld"].ap(), lo16[0:HN, :], sem="st_pos")
    pos_st = C.c("st_pos")
    for jc in range(32):
        ps = psB[:, 0:HN] if jc % 2 == 0 else psB[:, 512:512 + HN]
        C.pe(lambda e, jc=jc, ps=ps: e.transpose(
            ps, posfm[0:HN, jc * 128:(jc + 1) * 128], sb["ident"][0:HN, 0:HN]),
            wait=(("dve", pos_d if jc < 2 else C.c("dve")),), inc_n=1)
        tp = C.c("pe")
        C.dve(lambda e, jc=jc, ps=ps: e.tensor_copy(
            sb["pos_t"][:, jc * HN:(jc + 1) * HN], ps),
            wait=(("pe", tp),), inc_n=1)

    if DEBUG_STOP == 13:
        emit_output(C, M)
        return

    phase_sync()
    C.scope("P3_proj")

    def win_load(job, bi):
        hh, blk = job // 4, job % 4
        wsem = "q_win0" if bi == 0 else "q_win1"
        C.dma("sync", ar["winblk"][:, bi * 1024:(bi + 1) * 1024].rearrange(
            "p (k c) -> p k c", k=KC),
            wst["win"].ap().rearrange("k p c -> p k c")[
                :, :, blk * 1024 + hh * 128: blk * 1024 + hh * 128 + 128],
            wait=(attn_w, ("pe", C.c("pe"))), sem=wsem)
        return (wsem, C.c(wsem))

    wl = {0: win_load(0, 0)}
    stg_free = 0
    for job in range(HN * 4):
        hh, blk = job // 4, job % 4
        bi = job % 2
        if job + 1 < HN * 4:
            wl[job + 1] = win_load(job + 1, (job + 1) % 2)
        wld = wl[job]
        evac_marks = []
        if blk == 2:
            C.dma("scalar", sb["vbb"][:, :], APbc(h["bvrow"], hh * 128, 128, 128),
                  wait=(("dve", C.c("dve")),), sem="q_vbb")
            vbld = C.c("q_vbb")
            for tc in range(32):
                ps = psB[:, (tc % 4) * 512:(tc % 4) * 512 + 128]
                w = []
                if tc == 0:
                    w = [wld, ("dve", C.c("dve")), ("act", C.c("act"))]
                elif tc >= 4:
                    w = [("dve", evac_marks[tc - 4])]
                for kc in range(KC):
                    C.pe(lambda e, tc=tc, kc=kc, ps=ps, bi=bi: e.matmul(
                        ps, xnf[:, kc * T + tc * 128: kc * T + tc * 128 + 128],
                        ar["winblk"][:, bi * 1024 + kc * 128: bi * 1024 + kc * 128 + 128],
                        start=(kc == 0), stop=(kc == KC - 1)),
                        wait=tuple(w) if kc == 0 else (),
                        inc_n=1 if kc == KC - 1 else 0)
                vpe = C.c("pe")
                C.dve(lambda e, tc=tc, ps=ps: e.tensor_add(
                    ar["stg16"][:, tc * 128:(tc + 1) * 128], ps, sb["vbb"][:, :]),
                    wait=(("pe", vpe), ("q_vbb", vbld)) +
                    ((("st_stg", stg_free),) if tc == 0 else ()), inc_n=1)
                evac_marks.append(C.c("dve"))
            C.dma("gpsimd", M["vst"].ap()[hh], ar["stg16"][:, :],
                  wait=(("dve", C.c("dve")),), sem="st_stg")
            stg_free = C.c("st_stg")
        else:
            for nt in range(8):
                ps = psB[:, (nt % 4) * 512:(nt % 4) * 512 + 512]
                w = []
                if nt == 0:
                    w = [wld, ("q_pcol", pcol_ld), ("dve", C.c("dve")),
                         ("act", C.c("act"))]
                elif nt >= 4:
                    w = [("dve" if blk != 3 else "act", evac_marks[nt - 4])]
                for kc in range(KC):
                    C.pe(lambda e, nt=nt, kc=kc, ps=ps, bi=bi: e.matmul(
                        ps, ar["winblk"][:, bi * 1024 + kc * 128: bi * 1024 + kc * 128 + 128],
                        xnf[:, kc * T + nt * 512: kc * T + nt * 512 + 512],
                        start=(kc == 0), stop=(kc == KC - 1)),
                        wait=tuple(w) if kc == 0 else (),
                        inc_n=1 if kc == KC - 1 else 0)
                ppe = C.c("pe")
                bix = (0 if blk == 0 else (1 if blk == 1 else 2)) * HN + hh
                bqs = sb["bqkp"][:, bix:bix + 1]
                osl = slice(nt * 512, (nt + 1) * 512)
                ww = (("pe", ppe),) + ((("st_stg", stg_free),) if nt == 0 else ())
                if blk == 0:
                    C.dve(lambda e, ps=ps, bqs=bqs, osl=osl, hh=hh: e.tensor_scalar(
                        ar["stg16"][:, osl], ps, bqs, ccol(hh), ALU.add, ALU.mult),
                        wait=ww, inc_n=1)
                    evac_marks.append(C.c("dve"))
                elif blk == 1:
                    C.dve(lambda e, ps=ps, bqs=bqs, osl=osl: e.tensor_scalar_add(
                        ar["stg16"][:, osl], ps, bqs), wait=ww, inc_n=1)
                    evac_marks.append(C.c("dve"))
                else:
                    C.act(lambda e, ps=ps, bqs=bqs, osl=osl: e.activation(
                        ar["stg16"][:, osl], ps, AF.Silu, bias=bqs, scale=1.0),
                        wait=ww)
                    evac_marks.append(C.c("act"))
            if blk == 0:
                C.dma("gpsimd", M["qst"].ap()[hh], ar["stg16"][:, :],
                      wait=(("dve", C.c("dve")),), sem="st_stg")
            elif blk == 1:
                C.dve(lambda e, hh=hh: e.tensor_scalar_mul(
                    ar["ksc"][:, :], ar["stg16"][:, :], smcol(hh)),
                    wait=(("st_pos", pos_st),))
                C.dve(lambda e, hh=hh: e.tensor_scalar_mul(
                    ar["stg16"][:, :], ar["stg16"][:, :], sm1col(hh)))
                C.dve(lambda e: e.tensor_add(ar["stg16"][:, 1:T],
                                             ar["stg16"][:, 1:T],
                                             ar["ksc"][:, 0:T - 1]), inc_n=1)
                C.dma("gpsimd", M["kst"].ap()[hh], ar["stg16"][:, :],
                      wait=(("dve", C.c("dve")),), sem="st_stg")
            else:
                C.dma("gpsimd", M["spst"].ap()[hh], ar["stg16"][:, :],
                      wait=(("act", C.c("act")),), sem="st_stg")
            stg_free = C.c("st_stg")
        pass
    proj_st = C.c("st_stg")

    if DEBUG_STOP == 14:
        emit_output(C, M)
        return

    phase_sync()
    C.scope("P4_attn")
    C.dma("sync", ar["masks2"][:, :], h["mask2m"].ap(),
          wait=(("pe", C.c("pe")), ("dve", C.c("dve"))), sem="q_mask")
    mask_ld = C.c("q_mask")

    def head_load(hh, guard_pe):
        slot = hh % 2
        o = slot * T
        hsem = "q_hl0" if slot == 0 else "q_hl1"
        w = (("st_stg", proj_st), ("pe", guard_pe), ("dve", C.c("dve")),
             ("act", C.c("act")))
        C.dma("sync", ar["qh"][:, o:o + T], M["qst"].ap()[hh], wait=w, sem=hsem)
        C.dma("sync", ar["kh"][:, o:o + T], M["kst"].ap()[hh], sem=hsem)
        C.dma("sync", ar["vh"][:, o:o + T], M["vst"].ap()[hh], sem=hsem)
        C.dma("sync", ar["sph"][:, o:o + T], M["spst"].ap()[hh], sem=hsem)
        return (hsem, C.c(hsem))

    C.dve(lambda e: e.memset(ar["masks1"][0:1, 0:128], 1.0), inc_n=1)
    ones_d = C.c("dve")

    hload = {0: head_load(0, 0)}
    for hh in range(HN):
        conv_pump(3)
        slot = hh % 2
        o = slot * T
        ld_q = hload[hh]
        if hh + 1 < HN:
            hload[hh + 1] = head_load(hh + 1, C.c("pe"))
        bbase = 64 if hh % 2 == 0 else 0
        bnd_r = sb["rowp16"][bbase:bbase + 1, :]
        qh, kh, vh, sph = ar["qh"], ar["kh"], ar["vh"], ar["sph"]
        C.scope("P4_bnd")
        kred = {}
        for c in range(8):
            psrow = (psA if c < 4 else psB)[0:1, (c % 4) * 512:(c % 4) * 512 + 512]
            sq = ar["pthrow"][:, (c % 2) * 512:(c % 2) * 512 + 512]
            w = [("pe", C.c("pe"))]
            if c == 0:
                w.append(ld_q)
            C.act(lambda e, c=c, sq=sq: e.activation(
                sq, kh[:, o + c * 512: o + c * 512 + 512],
                AF.Square, bias=0.0, scale=1.0), wait=tuple(w))
            sq_a = C.c("act")
            C.pe(lambda e, psrow=psrow, sq=sq: e.matmul(
                psrow, sb["ones_col16"][:, 0:1], sq, start=True, stop=True),
                wait=(("act", sq_a), ("dve", C.c("dve"))), inc_n=1)
            mm_c = C.c("pe")
            C.dve(lambda e, c=c, psrow=psrow: e.reduce_max(
                sb["colscr"][0:1, 16 + c:17 + c], psrow, AX),
                wait=(("pe", mm_c),), inc_n=1)
            kred[c] = C.c("dve")
        C.dve(lambda e: e.reduce_max(sb["colscr"][0:1, 8:9],
                                     sb["colscr"][0:1, 16:24], AX), inc_n=1)
        km_d = C.c("dve")
        C.act(lambda e: e.activation(sb["colscr"][0:1, 9:10],
                                     sb["colscr"][0:1, 8:9], AF.Ln,
                                     bias=0.0, scale=1.0),
              wait=(("dve", km_d),))
        lnk_a = C.c("act")
        C.dve(lambda e: e.tensor_scalar(sb["colscr"][0:1, 10:11],
                                        sb["colscr"][0:1, 9:10],
                                        0.5, -0.5 * float(np.log(128.0)),
                                        ALU.mult, ALU.add),
              wait=(("act", lnk_a),), inc_n=1)
        cell_d = C.c("dve")
        C.dma("sync", sb["rowp16"][96:97, :], APflat(M["poshd"], hh * T, T),
              wait=(("st_pos", pos_st), ("pe", C.c("pe"))), sem="q_rp")
        ph_ld = C.c("q_rp")
        for half in range(2):
            for c4 in range(4):
                c = half * 4 + c4
                psrow = (psA if c < 4 else psB)[0:1, (c % 4) * 512:(c % 4) * 512 + 512]
                sq = ar["pthrow"][:, (c % 2) * 512:(c % 2) * 512 + 512]
                C.act(lambda e, c=c, sq=sq: e.activation(
                    sq, qh[:, o + c * 512: o + c * 512 + 512],
                    AF.Square, bias=0.0, scale=1.0),
                    wait=(("pe", C.c("pe")),))
                sq_a = C.c("act")
                C.pe(lambda e, psrow=psrow, sq=sq: e.matmul(
                    psrow, sb["ones_col16"][:, 0:1], sq, start=True, stop=True),
                    wait=(("act", sq_a), ("dve", kred[c])), inc_n=1)
                mm_c = C.c("pe")
                C.act(lambda e, psrow=psrow, c4=c4: e.activation(
                    sb["rowpf"][0:1, c4 * 512:(c4 + 1) * 512], psrow, AF.Ln,
                    bias=0.0, scale=1.0), wait=(("pe", mm_c), ("dve", cell_d)))
                ln_a = C.c("act")
                C.dve(lambda e, c4=c4: e.tensor_scalar(
                    sb["rowpf"][0:1, c4 * 512:(c4 + 1) * 512],
                    sb["rowpf"][0:1, c4 * 512:(c4 + 1) * 512], 0.5, 0.0,
                    ALU.mult, ALU.add), wait=(("act", ln_a),))
                C.dve(lambda e, c4=c4: e.tensor_single_scalar(
                    sb["rowpf"][0:1, c4 * 512:(c4 + 1) * 512],
                    sb["rowpf"][0:1, c4 * 512:(c4 + 1) * 512],
                    sb["colscr"][0:1, 10:11], ALU.add), inc_n=1)
                arg_d = C.c("dve")
                C.act(lambda e, c4=c4: e.activation(
                    ar["esmall"][0:1, 0:512], sb["rowpf"][0:1, c4 * 512:(c4 + 1) * 512],
                    AF.Exp, bias=0.0, scale=1.0), wait=(("dve", arg_d),))
                ex_a = C.c("act")
                C.dve(lambda e, half=half, c4=c4: e.tensor_copy(
                    ar["invlb"][0:1, 0:512],
                    sb["rowp16"][96:97, half * 2048 + c4 * 512:
                                 half * 2048 + c4 * 512 + 512]),
                    wait=(("q_rp", ph_ld), ("act", ex_a)))
                C.dve(lambda e: e.tensor_add(ar["esmall"][0:1, 0:512],
                                             ar["esmall"][0:1, 0:512],
                                             ar["invlb"][0:1, 0:512]))
                C.dve(lambda e, half=half, c4=c4, bbase=bbase: e.tensor_scalar_mul(
                    sb["rowp16"][bbase:bbase + 1,
                                 half * 2048 + c4 * 512:half * 2048 + c4 * 512 + 512],
                    ar["esmall"][0:1, 0:512], -1.0), inc_n=1)
        bnd_d = C.c("dve")
        bnd_a = C.c("act")
        C.scope("P4_attn")
        exp_war = [bnd_a, bnd_a]
        inv_war = 0
        den_war = 0
        for s in range(8):
            jcn = 4 * s + 4
            av_done = 0
            for jc in range(jcn):
                pslot = jc % 2
                psT = psA[:, pslot * 512:(pslot + 1) * 512]
                w = [("act", exp_war[pslot])]
                if jc == 0 and s == 0:
                    w += [("q_mask", mask_ld), ("dve", bnd_d)]
                if jc >= 2:
                    w.append(("pe", av_done))
                C.pe(lambda e, jc=jc, s=s, psT=psT: e.matmul(
                    psT, kh[:, o + jc * 128: o + jc * 128 + 128],
                    qh[:, o + s * 512: o + s * 512 + 512],
                    start=True, stop=False), wait=tuple(w))
                C.pe(lambda e, s=s, psT=psT, bbase=bbase: e.matmul(
                    psT, sb["ones16m"][bbase:bbase + 1, :],
                    bnd_r[:, s * 512: s * 512 + 512],
                    start=False, stop=True), inc_n=1)
                mmT = C.c("pe")
                dd = jc - 4 * s
                if dd >= 0:
                    C.dve(lambda e, dd=dd, psT=psT: e.tensor_add(
                        psT, psT,
                        ar["masks2"][:, 384 - 128 * dd:384 - 128 * dd + 512]),
                        wait=(("pe", mmT),), inc_n=1)
                    mwd = (("dve", C.c("dve")),)
                else:
                    mwd = (("pe", mmT),)
                C.act(lambda e, jc=jc, hh=hh, psT=psT, pslot=pslot: e.activation(
                    ar["pts"][:, pslot * 512:(pslot + 1) * 512], psT, AF.Exp,
                    bias=sb["pos_t"][:, jc * HN + hh: jc * HN + hh + 1], scale=1.0),
                    wait=mwd)
                exp_war[pslot] = C.c("act")
                C.pe(lambda e, jc=jc, jcn=jcn, pslot=pslot: e.matmul(
                    psB[:, 0:512], vh[:, o + jc * 128: o + jc * 128 + 128],
                    ar["pts"][:, pslot * 512:(pslot + 1) * 512],
                    start=(jc == 0), stop=(jc == jcn - 1)),
                    wait=(("act", exp_war[pslot]),) +
                    ((("dve", C.c("dve")),) if jc == 0 else ()),
                    inc_n=1)
                C.pe(lambda e, jc=jc, jcn=jcn, pslot=pslot: e.matmul(
                    psB[0:1, 512:1024], sb["ones_col16"][:, 0:1],
                    ar["pts"][:, pslot * 512:(pslot + 1) * 512],
                    start=(jc == 0), stop=(jc == jcn - 1)),
                    wait=(("dve", den_war),) if jc == 0 else (),
                    inc_n=1)
                av_done = C.c("pe")
            C.dve(lambda e: e.reciprocal(sb["rowpf"][0:1, 0:512], psB[0:1, 512:1024]),
                  wait=(("pe", av_done),), inc_n=1)
            den_war = C.c("dve")
            C.pe(lambda e: e.matmul(psA[:, 1024:1536], ar["masks1"][0:1, 0:128],
                                    sb["rowpf"][0:1, 0:512], start=True, stop=True),
                 wait=(("dve", den_war), ("dve", ones_d), ("dve", inv_war)),
                 inc_n=1)
            ib_pe = C.c("pe")
            C.dve(lambda e: e.tensor_copy(ar["invlb"][:, :], psA[:, 1024:1536]),
                  wait=(("pe", ib_pe),))
            C.dve(lambda e: e.tensor_mul(ar["esmall"][:, :], psB[:, 0:512],
                                         ar["invlb"][:, :]), inc_n=1)
            inv_war = C.c("dve")
            C.dve(lambda e, s=s: e.tensor_mul(
                ar["gtile"][:, :], ar["esmall"][:, :],
                sph[:, o + s * 512: o + s * 512 + 512]),
                wait=(("st_g", C.c("st_g")),), inc_n=1)
            g_d = C.c("dve")
            C.dma("gpsimd", M["gst"].ap()[hh][:, s * 512:(s + 1) * 512],
                  ar["gtile"][:, :], wait=(("dve", g_d),), sem="st_g")
    g_store = C.c("st_g")

    if DEBUG_STOP == 15:
        emit_output(C, M)
        return

    phase_sync()
    C.scope("P5_wout")
    C.dma("sync", ar["wout_s"][:, :].rearrange("p (k c) -> p k c", k=KC),
          wst["woutT"].ap().rearrange("k p c -> p k c"),
          wait=(attn_w, ("pe", C.c("pe"))), sem="q_wout")
    wo_ld = C.c("q_wout")
    ev_prev = 0
    for nt in range(8):
        for hh2 in range(HN):
            C.dma("sync", ar["gld"][:, hh2 * 512:(hh2 + 1) * 512],
                  M["gst"].ap()[hh2][:, nt * 512:(nt + 1) * 512],
                  wait=((("st_g", g_store), ("pe", C.c("pe"))) if hh2 == 0
                        else ()), sem="q_gld")
        gl = C.c("q_gld")
        evd = {}
        for dm in range(KC):
            ps = psA[:, (dm % 4) * 512:(dm % 4) * 512 + 512]
            for hh2 in range(HN):
                C.pe(lambda e, dm=dm, hh2=hh2, ps=ps: e.matmul(
                    ps, ar["wout_s"][:, hh2 * D + dm * 128: hh2 * D + dm * 128 + 128],
                    ar["gld"][:, hh2 * 512:(hh2 + 1) * 512],
                    start=(hh2 == 0), stop=(hh2 == HN - 1)),
                    wait=(("q_gld", gl), ("q_wout", wo_ld),
                          ("dve", evd.get(dm - 4, ev_prev))) if hh2 == 0 else (),
                    inc_n=1 if hh2 == HN - 1 else 0)
            wpe = C.c("pe")
            C.dve(lambda e, dm=dm, ps=ps: e.tensor_copy(
                ar["aotile"][:, dm * 512:(dm + 1) * 512], ps),
                wait=(("pe", wpe), ("st_ao", C.c("st_ao"))), inc_n=1)
            evd[dm] = C.c("dve")
        ev_prev = C.c("dve")
        for dm in range(KC):
            C.dma("gpsimd",
                  M["rs_in"].ap()[nt // 4, dm][:, (nt % 4) * 512:(nt % 4) * 512 + 512],
                  ar["aotile"][:, dm * 512:(dm + 1) * 512],
                  wait=(("dve", ev_prev),) if dm == 0 else (), sem="st_ao")
        conv_pump(8)
    rs_w = C.c("st_ao")
    C.op("gpsimd", lambda e: e.collective_compute(
        "ReduceScatter", ALU.add, replica_groups=M["groups"],
        ins=[M["rs_in"].ap().opt()], outs=[M["rs_out"].ap().opt()]),
        wait=(("st_ao", rs_w),), inc=(("cc", 1),))
    rs_cc = C.c("cc")

    def ln_residual_from_dram(rs_cc, use_oln):
        cb, cq = ar["lncb"], ar["lncq"]
        Rb, Bb = ar["Rb"], ar["Bb"]
        mucol = sb["lncol"][:, 0:16]
        vcol = sb["lncol"][:, 16:32]
        Rcol = sb["lncol"][:, 32:48]
        Bcol = sb["lncol"][:, 48:64]
        acc_s = sb["lncol"][:, 96:112]
        acc_q = sb["lncol"][:, 112:128]
        mm_marks = []
        for kc in range(KC):
            pp = (kc % 2) * TOWN
            C.dma("sync", ar["lnes"][:, :], M["rs_out"].ap()[kc],
                  wait=((("cc", rs_cc), ("pe", C.c("pe")), ("dve", C.c("dve")))
                        if kc == 0 else (("dve", C.c("dve")),)), sem="q_ao3")
            al = C.c("q_ao3")
            w2 = [("q_ao3", al)]
            if kc == 0:
                w2.append(("pe", C.c("pe")))
            if kc >= 1:
                w2.append(("dve", mm_marks[kc - 1]))
            C.dve(lambda e, pp=pp: e.tensor_copy(cb[:, pp:pp + TOWN],
                                                 ar["lnes"][:, :]),
                  wait=tuple(w2), inc_n=1)
            cd = C.c("dve")
            C.act(lambda e, pp=pp: e.activation(cq[:, pp:pp + TOWN],
                                                cb[:, pp:pp + TOWN], AF.Square,
                                                bias=0.0, scale=1.0),
                  wait=(("dve", cd),))
            ca = C.c("act")
            for tg in range(16):
                C.pe(lambda e, tg=tg, pp=pp: e.matmul(
                    psA[:, 1536 + tg:1536 + tg + 1],
                    cb[:, pp + tg * 128:pp + tg * 128 + 128],
                    sb["ones_col16"][:, :], start=True, stop=True),
                    wait=((("dve", cd), ("act", ca)) if tg == 0 else ()))
                C.pe(lambda e, tg=tg, pp=pp: e.matmul(
                    psB[:, 1536 + tg:1536 + tg + 1],
                    cq[:, pp + tg * 128:pp + tg * 128 + 128],
                    sb["ones_col16"][:, :], start=True, stop=True),
                    inc_n=1 if tg == 15 else 0)
            mmk = C.c("pe")
            if kc == 0:
                C.dve(lambda e: e.tensor_copy(acc_s, psA[:, 1536:1552]),
                      wait=(("pe", mmk),))
                C.dve(lambda e: e.tensor_copy(acc_q, psB[:, 1536:1552]), inc_n=1)
            else:
                C.dve(lambda e: e.tensor_add(acc_s, acc_s, psA[:, 1536:1552]),
                      wait=(("pe", mmk),))
                C.dve(lambda e: e.tensor_add(acc_q, acc_q, psB[:, 1536:1552]),
                      inc_n=1)
            mm_marks.append(C.c("dve"))
        C.dve(lambda e: e.tensor_scalar_mul(mucol, acc_s, 1.0 / D))
        C.dve(lambda e: e.tensor_scalar_mul(vcol, acc_q, 1.0 / D))
        C.dve(lambda e: e.tensor_mul(Bcol, mucol, mucol))
        C.dve(lambda e: e.tensor_sub(vcol, vcol, Bcol))
        C.dve(lambda e: e.tensor_scalar(vcol, vcol, 1.0, LNEPS, ALU.mult, ALU.add),
              inc_n=1)
        var_d = C.c("dve")
        C.act(lambda e: e.activation(Bcol, vcol, AF.Ln, bias=0.0, scale=1.0),
              wait=(("dve", var_d),))
        C.act(lambda e: e.activation(Rcol, Bcol, AF.Exp, bias=0.0, scale=-0.5))
        r_a = C.c("act")
        C.dve(lambda e: e.tensor_mul(Bcol, mucol, Rcol), wait=(("act", r_a),))
        C.dve(lambda e: e.tensor_scalar_mul(Bcol, Bcol, -1.0), inc_n=1)
        rb_d = C.c("dve")
        C.pe(lambda e: e.transpose(psA[0:32, 0:128], sb["lncol"][:, 32:64],
                                   sb["ident"][:, :]),
             wait=(("dve", rb_d),), inc_n=1)
        tr_pe = C.c("pe")
        C.dve(lambda e: e.tensor_copy(ar["lnrt"][0:32, :], psA[0:32, 0:128]),
              wait=(("pe", tr_pe),), inc_n=1)
        ev_d = C.c("dve")
        C.dma("gpsimd", M["lnrows"].ap()[0:2], ar["lnrt"][0:32, :],
              wait=(("dve", ev_d),), sem="st_lnr")
        g1 = C.c("st_lnr")
        C.dma("scalar", Rb[:, :], APbc(M["lnrows"], 0, 128, TOWN),
              wait=(("st_lnr", g1), ("dve", C.c("dve"))), sem="q_RB")
        C.dma("scalar", Bb[:, :], APbc(M["lnrows"], TOWN, 128, TOWN), sem="q_RB")
        bc_v = C.c("q_RB")
        for kc in range(KC):
            sl = slice(kc * TOWN, (kc + 1) * TOWN)
            C.dma("sync", ar["lnes"][:, :], M["rs_out"].ap()[kc],
                  wait=(("dve", C.c("dve")),), sem="q_ao3")
            al = C.c("q_ao3")
            C.dve(lambda e: e.tensor_mul(ar["lnes2"][:, :], ar["lnes"][:, :],
                                         Rb[:, :]),
                  wait=(("q_ao3", al), ("q_RB", bc_v)))
            C.dve(lambda e: e.tensor_add(ar["lnes2"][:, :], ar["lnes2"][:, :],
                                         Bb[:, :]))
            C.dve(lambda e, kc=kc: e.tensor_scalar(
                ar["lnes2"][:, :], ar["lnes2"][:, :], sb["olnw"][:, kc:kc + 1],
                sb["olnb"][:, kc:kc + 1], ALU.mult, ALU.add))
            C.dve(lambda e, sl=sl: e.tensor_add(x[:, sl], x[:, sl],
                                                ar["lnes2"][:, :]), inc_n=1)

    ln_residual_from_dram(rs_cc, True)
    if DEBUG_STOP == 1:
        emit_output(C, M)
        return

    def emit_ffn(call):
        phase_sync()
        C.scope(f"F{call}_ffn")
        conv_pump(len(conv_jobs))
        ffn_w = ("q_cvoF", C.c("q_cvoF"))
        xn_done = emit_ln(x, ar["xno"], affine=False, router_rows=True)
        xno = ar["xno"]
        C.dma("sync", ar["wr32"][:, :].rearrange("p (k c) -> p k c", k=KC),
              h["wr"].ap().rearrange("k p c -> p k c"),
              wait=(("dve", xn_done), ("pe", C.c("pe"))), sem="q_wr")
        wr_ld = C.c("q_wr")
        for kc in range(KC):
            C.pe(lambda e, kc=kc: e.matmul(
                psB[0:1, 1024:1031], sb["ones_col32"][:, :],
                ar["wr32"][:, kc * 7:(kc + 1) * 7],
                start=(kc == 0), stop=(kc == KC - 1)),
                wait=((("q_wr", wr_ld),)) if kc == 0 else (),
                inc_n=1 if kc == KC - 1 else 0)
        swr_pe = C.c("pe")
        C.dve(lambda e: e.tensor_scalar_mul(sb["prows"][0:1, 40:47],
                                            psB[0:1, 1024:1031], -1.0),
              wait=(("pe", swr_pe),), inc_n=1)
        swr_d = C.c("dve")
        C.dma("scalar", sb["rbb"][:, :], APbc(h["rbias"], 0, 128, 7),
              wait=(("dve", C.c("dve")),), sem="q_rbb")
        rb_ld = C.c("q_rbb")
        wt_marks = {}
        for tg in range(16):
            psl = psB[:, 1536 + (tg % 2) * 16: 1536 + (tg % 2) * 16 + 7]
            w = [("dve", xn_done)] if tg == 0 else []
            if tg >= 2:
                w.append(("dve", wt_marks[tg - 2]))
            for kc in range(KC):
                C.pe(lambda e, tg=tg, kc=kc, psl=psl: e.matmul(
                    psl, x[:, kc * TOWN + tg * 128: kc * TOWN + tg * 128 + 128],
                    ar["wr32"][:, kc * 7:(kc + 1) * 7],
                    start=(kc == 0), stop=False),
                    wait=tuple(w) if kc == 0 else ())
            C.pe(lambda e, tg=tg, psl=psl: e.matmul(
                psl, sb["rowpf"][0:1, tg * 128:(tg + 1) * 128],
                sb["prows"][0:1, 40:47], start=False, stop=True),
                wait=(("dve", swr_d),) if tg == 0 else (), inc_n=1)
            rpe = C.c("pe")
            C.dve(lambda e, tg=tg, psl=psl: e.tensor_scalar_mul(
                ar["ltile"][:, 0:7], psl, sb["rcols"][:, tg:tg + 1]),
                wait=(("pe", rpe),))
            C.dve(lambda e: e.tensor_mul(ar["ltile"][:, 0:7], ar["ltile"][:, 0:7],
                                         sb["rbb"][:, :]),
                  wait=(("q_rbb", rb_ld),), inc_n=1)
            lg_d = C.c("dve")
            C.act(lambda e: e.activation(ar["ptile"][:, 0:7], ar["ltile"][:, 0:7],
                                         AF.Sigmoid, bias=0.0, scale=1.0),
                  wait=(("dve", lg_d),))
            pr_a = C.c("act")
            c0 = sb["colscr"][:, 0:1]
            c1 = sb["colscr"][:, 1:2]
            C.dve(lambda e: e.reduce_max(c0, ar["ptile"][:, 0:7], AX),
                  wait=(("act", pr_a),))
            C.dve(lambda e: e.tensor_single_scalar(
                ar["ltile"][:, 0:7], ar["ptile"][:, 0:7], c0, ALU.is_ge))
            C.dve(lambda e: e.tensor_scalar_mul(ar["ltile"][:, 0:7],
                                                ar["ltile"][:, 0:7], 1e30))
            C.dve(lambda e: e.tensor_sub(ar["ltile"][:, 0:7], ar["ptile"][:, 0:7],
                                         ar["ltile"][:, 0:7]))
            C.dve(lambda e: e.reduce_max(c1, ar["ltile"][:, 0:7], AX))
            C.dve(lambda e: e.tensor_single_scalar(
                ar["ltile"][:, 0:7], ar["ptile"][:, 0:7], c1, ALU.is_ge))
            C.dve(lambda e: e.tensor_add(c0, c0, c1))
            C.dve(lambda e: e.reciprocal(c1, c0))
            C.dve(lambda e: e.tensor_mul(ar["ltile"][:, 0:7], ar["ltile"][:, 0:7],
                                         ar["ptile"][:, 0:7]))
            C.dve(lambda e: e.tensor_scalar_mul(ar["ltile"][:, 0:7],
                                                ar["ltile"][:, 0:7], c1), inc_n=1)
            wv_d = C.c("dve")
            C.pe(lambda e, tg=tg: e.transpose(
                psB[0:7, 1280 + (tg % 2) * 128: 1280 + (tg % 2) * 128 + 128],
                ar["ltile"][:, 0:7], sb["ident"][:, :]),
                wait=(("dve", wv_d),), inc_n=1)
            tpe = C.c("pe")
            C.dve(lambda e, tg=tg: e.tensor_copy(
                sb["wt7"][0:7, tg * 128:(tg + 1) * 128],
                psB[0:7, 1280 + (tg % 2) * 128: 1280 + (tg % 2) * 128 + 128]),
                wait=(("pe", tpe),), inc_n=1)
            wt_marks[tg] = C.c("dve")
        C.dma("gpsimd", M["wrowd"].ap(), sb["wt7"][0:7, :],
              wait=(("dve", wt_marks[15]),), sem="st_wrow")
        wrow_st = C.c("st_wrow")

        ln_guard = C.c("dve")
        st = M.setdefault("ffn_state", dict(
            g_act={}, u_dve={}, d_dve={}, mul_last={0: 0, 1: 0},
            down_pe=0, slot_war={}, pending={}))

        def ring_load(eidx, jc, what, first):
            wsrc = wst["gates"] if what == "g" else wst["ups"]
            buf = ar["gring"] if what == "g" else ar["uring"]
            slx = jc % 4
            rsem = f"q_{what}{slx}"
            w = []
            if first:
                w += [ffn_w, ("dve", ln_guard), ("st_wrow", wrow_st)]
            pw = st["slot_war"].get((what, slx), 0)
            if pw:
                w.append(("pe", pw))
            C.dma("sync", buf[:, slx * 1024:(slx + 1) * 1024].rearrange(
                "p (k c) -> p k c", k=KC),
                wsrc.ap()[eidx].rearrange("k p c -> p k c")[:, :, jc * 128:(jc + 1) * 128],
                wait=tuple(w), sem=rsem)
            return (rsem, C.c(rsem))

        exps = [(nt, e) for nt in range(2) for e in range(NEXP)]

        def prefill(key, first=False):
            nt_, e_ = key
            d = {}
            for j0 in range(4):
                d[("g", j0)] = ring_load(e_, j0, "g", first and j0 == 0)
                d[("u", j0)] = ring_load(e_, j0, "u", False)
            st["pending"][key] = d

        prefill(exps[0], first=True)
        for ei, (nt, eidx) in enumerate(exps):
            tb = nt * 1024
            ring_ld = st["pending"].pop((nt, eidx))
            if eidx > 0:
                C.dma("scalar", ar["wbb"][:, 0:1024],
                      APbc(M["wrowd"], (eidx - 1) * TOWN + tb, 128, 1024),
                      wait=(("st_wrow", wrow_st),)
                      + tuple(("dve", v) for v in st["d_dve"].values()),
                      sem="q_wbb")
            wb_ld = C.c("q_wbb")
            for jc in range(JCN):
                gl = ring_ld[("g", jc)]
                ul = ring_ld[("u", jc)]
                gsl = ar["gring"][:, (jc % 4) * 1024:(jc % 4) * 1024 + 1024]
                usl = ar["uring"][:, (jc % 4) * 1024:(jc % 4) * 1024 + 1024]
                gate_pe = {}
                for ns in range(2):
                    pg = psA[:, (jc % 2) * 1024 + ns * 512:
                             (jc % 2) * 1024 + ns * 512 + 512]
                    w = [gl, ("act", st["g_act"].get((jc % 2, ns), 0))]
                    if jc < 2:
                        w.append(("dve", st["d_dve"].get((jc % 2, ns), 0)))
                    for kc in range(KC):
                        C.pe(lambda e, kc=kc, gsl=gsl, pg=pg, ns=ns: e.matmul(
                            pg, gsl[:, kc * 128:(kc + 1) * 128],
                            xno[:, kc * TOWN + tb + ns * 512:
                                 kc * TOWN + tb + ns * 512 + 512],
                            start=(kc == 0), stop=(kc == KC - 1)),
                            wait=tuple(w) if kc == 0 else (),
                            inc_n=1 if kc == KC - 1 else 0)
                    gate_pe[ns] = C.c("pe")
                for ns in range(2):
                    pu = psB[:, (jc % 2) * 1024 + ns * 512:
                             (jc % 2) * 1024 + ns * 512 + 512]
                    w = [ul, ("dve", st["u_dve"].get((jc % 2, ns), 0))]
                    for kc in range(KC):
                        C.pe(lambda e, kc=kc, usl=usl, pu=pu, ns=ns: e.matmul(
                            pu, usl[:, kc * 128:(kc + 1) * 128],
                            xno[:, kc * TOWN + tb + ns * 512:
                                 kc * TOWN + tb + ns * 512 + 512],
                            start=(kc == 0), stop=(kc == KC - 1)),
                            wait=tuple(w) if kc == 0 else (),
                            inc_n=1 if kc == KC - 1 else 0)
                up_pe = C.c("pe")
                st["slot_war"][("g", jc % 4)] = up_pe
                st["slot_war"][("u", jc % 4)] = up_pe
                if jc + 4 < JCN:
                    ring_ld[("g", jc + 4)] = ring_load(eidx, jc + 4, "g", False)
                    ring_ld[("u", jc + 4)] = ring_load(eidx, jc + 4, "u", False)
                for ns in range(2):
                    pg = psA[:, (jc % 2) * 1024 + ns * 512:
                             (jc % 2) * 1024 + ns * 512 + 512]
                    pu = psB[:, (jc % 2) * 1024 + ns * 512:
                             (jc % 2) * 1024 + ns * 512 + 512]
                    sil_w = [("pe", gate_pe[ns]), ("dve", st["mul_last"][ns])]
                    if jc == 0:
                        sil_w += [("dve", v) for v in st["d_dve"].values()]
                    C.act(lambda e, pg=pg, ns=ns: e.activation(
                        ar["sg"][:, ns * 512:(ns + 1) * 512], pg,
                        AF.Silu, bias=0.0, scale=1.0), wait=tuple(sil_w))
                    sg_a = C.c("act")
                    st["g_act"][(jc % 2, ns)] = sg_a
                    C.dve(lambda e, jc=jc, pu=pu, ns=ns: e.tensor_mul(
                        ar["hbuf"][:, jc * 1024 + ns * 512:
                                   jc * 1024 + ns * 512 + 512], pu,
                        ar["sg"][:, ns * 512:(ns + 1) * 512]),
                        wait=(("act", sg_a), ("pe", up_pe),
                              ("pe", st["down_pe"])), inc_n=1)
                    mul_d = C.c("dve")
                    st["u_dve"][(jc % 2, ns)] = mul_d
                    st["mul_last"][ns] = mul_d
            h_done = C.c("dve")
            if ei + 1 < len(exps):
                prefill(exps[ei + 1])
            dq_war = 0
            for q4 in range(4):
                w = [("dve", h_done)] if q4 == 0 else []
                if dq_war:
                    w.append(("pe", dq_war))
                if q4 == 0:
                    w += [ffn_w, ("st_wrow", wrow_st)]
                C.dma("scalar", ar["dquart"][:, :].rearrange(
                    "p (j c) -> p j c", j=JCN),
                    wst["downs"].ap()[eidx].rearrange(
                        "j p c -> p j c")[:, :, q4 * 256:(q4 + 1) * 256],
                    wait=tuple(w), sem="q_dh")
                d_ld = C.c("q_dh")
                for dmi2 in range(2):
                    dm = q4 * 2 + dmi2
                    for ns in range(2):
                        pdp = psA[:, (dm % 2) * 1024 + ns * 512:
                                  (dm % 2) * 1024 + ns * 512 + 512]
                        for jc in range(JCN):
                            C.pe(lambda e, jc=jc, dmi2=dmi2, pdp=pdp, ns=ns: e.matmul(
                                pdp, ar["dquart"][:, jc * 256 + dmi2 * 128:
                                                  jc * 256 + dmi2 * 128 + 128],
                                ar["hbuf"][:, jc * 1024 + ns * 512:
                                           jc * 1024 + ns * 512 + 512],
                                start=(jc == 0), stop=(jc == JCN - 1)),
                                wait=(("q_dh", d_ld), ("dve", h_done),
                                      ("dve", st["d_dve"].get((dm % 2, ns), 0)),
                                      ("act", st["g_act"].get((dm % 2, ns), 0)))
                                if jc == 0 else (),
                                inc_n=1 if jc == JCN - 1 else 0)
                        dpe = C.c("pe")
                        xsl = x[:, dm * TOWN + tb + ns * 512:
                                dm * TOWN + tb + ns * 512 + 512]
                        if eidx == 0:
                            C.dve(lambda e, xsl=xsl, pdp=pdp: e.tensor_add(
                                xsl, xsl, pdp), wait=(("pe", dpe),), inc_n=1)
                        else:
                            C.dve(lambda e, pdp=pdp, ns=ns: e.tensor_mul(
                                ar["fes"][:, 0:512], pdp,
                                ar["wbb"][:, ns * 512:(ns + 1) * 512]),
                                wait=(("pe", dpe), ("q_wbb", wb_ld)))
                            C.dve(lambda e, xsl=xsl: e.tensor_add(
                                xsl, xsl, ar["fes"][:, 0:512]), inc_n=1)
                        st["d_dve"][(dm % 2, ns)] = C.c("dve")
                dq_war = C.c("pe")
            st["down_pe"] = C.c("pe")
        M.pop("ffn_state")
    emit_ffn(0)
    if DEBUG_STOP == 2:
        emit_output(C, M)
        return

    phase_sync()
    C.scope("B0_ln")
    xn_done = emit_ln(x, ar["xno"], affine=False)
    xnf_ld = emit_ag(ar["xno"], xn_done)
    C.dma("sync", ar["wproj"][0:64, :].rearrange("p (hh m) -> p hh m", hh=HN),
          wst["projT"].ap().rearrange("hh p m -> p hh m"),
          wait=(attn_w, ("dve", xn_done), ("pe", C.c("pe"))), sem="q_wproj")
    C.dma("sync", ar["wproj"][64:128, :].rearrange("p (hh m) -> p hh m", hh=HN),
          wst["projT"].ap().rearrange("hh p m -> p hh m"), sem="q_wproj")
    wproj_ld = C.c("q_wproj")

    def wqkv_load(hh, guard_pe):
        for i, wname in enumerate(["wqT", "wkT", "wvT"]):
            C.dma("sync", ar["wqkv3"][:, i * 512:(i + 1) * 512].rearrange(
                "p (k c) -> p k c", k=KC),
                wst[wname].ap().rearrange("k p c -> p k c")[
                    :, :, hh * DK:(hh + 1) * DK],
                wait=(("pe", guard_pe), attn_w) if i == 0 else (), sem="q_wqkv")
        return C.c("q_wqkv")

    mg_prev = 0
    for hh in range(HN):
        C.scope("B1_qkv")
        wq_ld = wqkv_load(hh, C.c("pe"))
        qk_marks = []
        for i in range(2):
            for nt2 in range(8):
                ps = psA[0:DK, (nt2 % 4) * 512:(nt2 % 4) * 512 + 512]
                idx = i * 8 + nt2
                w = []
                if idx == 0:
                    w = [("q_wqkv", wq_ld), ("q_xnf", xnf_ld), ("dve", mg_prev)]
                elif idx >= 4:
                    w = [("dve", qk_marks[idx - 4])]
                for kc in range(KC):
                    C.pe(lambda e, i=i, nt2=nt2, kc=kc, ps=ps: e.matmul(
                        ps, ar["wqkv3"][:, i * 512 + kc * DK: i * 512 + (kc + 1) * DK],
                        xnf[:, kc * T + nt2 * 512: kc * T + nt2 * 512 + 512],
                        start=(kc == 0), stop=(kc == KC - 1)),
                        wait=tuple(w) if kc == 0 else (),
                        inc_n=1 if kc == KC - 1 else 0)
                qpe = C.c("pe")
                dst = (ar["qkh"][0:64, nt2 * 512:(nt2 + 1) * 512] if i == 0
                       else ar["qkh"][64:128, nt2 * 512:(nt2 + 1) * 512])
                C.dve(lambda e, dst=dst, ps=ps: e.tensor_scalar_mul(dst, ps, 0.125),
                      wait=(("pe", qpe),), inc_n=1)
                qk_marks.append(C.c("dve"))
        qk_d = C.c("dve")
        if DEBUG_STOP == 32 and hh == 0:
            emit_output(C, M)
            return
        C.scope("B2_v")
        v_marks = []
        for tc in range(32):
            ps = psB[:, 1024 + (tc % 2) * 512: 1024 + (tc % 2) * 512 + 64]
            w = [("dve", qk_marks[12])] if tc == 0 else (
                [("dve", v_marks[tc - 2])] if tc >= 2 else [])
            for kc in range(KC):
                C.pe(lambda e, tc=tc, kc=kc, ps=ps: e.matmul(
                    ps, xnf[:, kc * T + tc * 128: kc * T + tc * 128 + 128],
                    ar["wqkv3"][:, 2 * 512 + kc * DK: 2 * 512 + (kc + 1) * DK],
                    start=(kc == 0), stop=(kc == KC - 1)),
                    wait=tuple(w) if kc == 0 else (), inc_n=1 if kc == KC - 1 else 0)
            vpe = C.c("pe")
            C.dve(lambda e, tc=tc, ps=ps: e.tensor_copy(
                ar["vb_h"][:, tc * 65: tc * 65 + 64], ps),
                wait=(("pe", vpe),), inc_n=1)
            v_marks.append(C.c("dve"))
        C.dve(lambda e: e.memset(
            ar["vb_h"][:, :].rearrange("p (a b) -> p a b", b=65)[:, :, 64:65], 1.0),
            inc_n=1)
        v_d = C.c("dve")
        if DEBUG_STOP == 33 and hh == 0:
            emit_output(C, M)
            return
        C.scope("B3_phk")
        phk_marks = {}
        for tc in range(32):
            ps = psA[:, 1024 + (tc % 2) * 512: 1024 + (tc % 2) * 512 + MF]
            w = [("dve", qk_d)] if tc == 0 else (
                [("dve", phk_marks[tc - 2])] if tc >= 2 else [])
            C.pe(lambda e, tc=tc, hh=hh, ps=ps: e.matmul(
                ps, ar["qkh"][64:128, tc * 128:(tc + 1) * 128],
                ar["wproj"][64:128, hh * MF:(hh + 1) * MF],
                start=True, stop=True),
                wait=tuple(w) + ((("q_wproj", wproj_ld),) if (hh == 0 and tc == 0) else ()),
                inc_n=1)
            xw_pe = C.c("pe")
            C.dve(lambda e, ps=ps: e.reduce_max(sb["colscr"][:, 0:1], ps, AX),
                  wait=(("pe", xw_pe),), inc_n=1)
            C.act(lambda e, tc=tc: e.activation(
                ar["sqs"][:, :], ar["qkh"][64:128, tc * 128:(tc + 1) * 128],
                AF.Square, bias=0.0, scale=1.0),
                wait=(("dve", qk_d), ("pe", C.c("pe"))))
            sq_a = C.c("act")
            C.pe(lambda e, tc=tc: e.transpose(
                psB[:, (tc % 2) * 512:(tc % 2) * 512 + 64], ar["sqs"][:, :],
                sb["ident"][0:64, 0:64]), wait=(("act", sq_a),), inc_n=1)
            tq_pe = C.c("pe")
            C.dve(lambda e, tc=tc: e.reduce_sum(
                sb["colscr"][:, 1:2], psB[:, (tc % 2) * 512:(tc % 2) * 512 + 64], AX),
                wait=(("pe", tq_pe),))
            C.dve(lambda e: e.tensor_scalar(sb["colscr"][:, 1:2],
                                            sb["colscr"][:, 1:2], -0.5, -LN16,
                                            ALU.mult, ALU.add))
            C.dve(lambda e: e.tensor_sub(sb["colscr"][:, 1:2], sb["colscr"][:, 1:2],
                                         sb["colscr"][:, 0:1]), inc_n=1)
            bc_d = C.c("dve")
            C.act(lambda e, ps=ps: e.activation(
                ar["bes2"][:, :], ps, AF.Exp, bias=sb["colscr"][:, 1:2], scale=1.0),
                wait=(("dve", bc_d),))
            ph_a = C.c("act")
            C.dve(lambda e: e.tensor_scalar_add(ar["phk16"][:, :], ar["bes2"][:, :],
                                                EPSB / 16.0),
                  wait=(("act", ph_a),), inc_n=1)
            phk_d = C.c("dve")
            phk_marks[tc] = phk_d
            for mc in range(2):
                C.pe(lambda e, tc=tc, mc=mc: e.matmul(
                    psB[:, 1024 + mc * 512: 1024 + mc * 512 + 65],
                    ar["phk16"][:, mc * 128:(mc + 1) * 128],
                    ar["vb_h"][:, tc * 65:(tc + 1) * 65],
                    start=(tc == 0), stop=(tc == 31)),
                    wait=(("dve", max(phk_d, v_d)),) if mc == 0 else (),
                    inc_n=1 if (tc == 31 and mc == 1) else 0)
        kv_pe = C.c("pe")
        C.dve(lambda e: e.tensor_copy(ar["kvh"][:, 0:65], psB[:, 1024:1089]),
              wait=(("pe", kv_pe),))
        C.dve(lambda e: e.tensor_copy(ar["kvh"][:, 65:130], psB[:, 1536:1601]))
        C.dve(lambda e: e.tensor_copy(ar["kvh16"][:, :], ar["kvh"][:, :]), inc_n=1)
        kv16_d = C.c("dve")
        if DEBUG_STOP == 34 and hh == 0:
            emit_output(C, M)
            return
        C.scope("B4_phq1")
        q1_marks = {}
        for tc in range(32):
            ps = psA[:, 1024 + (tc % 2) * 512: 1024 + (tc % 2) * 512 + MF]
            w = [("dve", kv16_d)] if tc == 0 else (
                [("dve", q1_marks[tc - 2])] if tc >= 2 else [])
            C.pe(lambda e, tc=tc, hh=hh, ps=ps: e.matmul(
                ps, ar["qkh"][0:64, tc * 128:(tc + 1) * 128],
                ar["wproj"][0:64, hh * MF:(hh + 1) * MF],
                start=True, stop=True), wait=tuple(w), inc_n=1)
            xw_pe = C.c("pe")
            C.dve(lambda e, ps=ps: e.reduce_max(sb["colscr"][:, 0:1], ps, AX),
                  wait=(("pe", xw_pe),), inc_n=1)
            C.act(lambda e, tc=tc: e.activation(
                ar["sqs"][:, :], ar["qkh"][0:64, tc * 128:(tc + 1) * 128],
                AF.Square, bias=0.0, scale=1.0), wait=(("pe", C.c("pe")),))
            sq_a = C.c("act")
            C.pe(lambda e, tc=tc: e.transpose(
                psB[:, (tc % 2) * 512:(tc % 2) * 512 + 64], ar["sqs"][:, :],
                sb["ident"][0:64, 0:64]), wait=(("act", sq_a),), inc_n=1)
            tq_pe = C.c("pe")
            C.dve(lambda e, tc=tc: e.reduce_sum(
                sb["colscr"][:, 1:2], psB[:, (tc % 2) * 512:(tc % 2) * 512 + 64], AX),
                wait=(("pe", tq_pe),))
            C.dve(lambda e: e.tensor_scalar(sb["colscr"][:, 1:2],
                                            sb["colscr"][:, 1:2], -0.5, -LN16,
                                            ALU.mult, ALU.add))
            C.dve(lambda e: e.tensor_sub(sb["colscr"][:, 1:2], sb["colscr"][:, 1:2],
                                         sb["colscr"][:, 0:1]), inc_n=1)
            bq_d = C.c("dve")
            if DEBUG_STOP == 38:
                q1_marks[tc] = C.c("dve")
                continue
            C.pe(lambda e, tc=tc: e.transpose(
                psB[0:1, 1024 + (tc % 2) * 512: 1024 + (tc % 2) * 512 + 128],
                sb["colscr"][:, 1:2], sb["ident"][:, :]),
                wait=(("dve", bq_d),), inc_n=1)
            tb_pe = C.c("pe")
            if DEBUG_STOP == 39:
                q1_marks[tc] = C.c("dve")
                continue
            C.dve(lambda e, tc=tc: e.tensor_copy(
                sb["rowp16"][0:1, tc * 128:(tc + 1) * 128],
                psB[0:1, 1024 + (tc % 2) * 512: 1024 + (tc % 2) * 512 + 128]),
                wait=(("pe", tb_pe),))
            C.dve(lambda e, tc=tc: e.tensor_copy(
                sb["rowpf"][0:1, 0:128],
                sb["rowp16"][0:1, tc * 128:(tc + 1) * 128]))
            if DEBUG_STOP == 40:
                q1_marks[tc] = C.c("dve")
                continue
            C.dve(lambda e, tc=tc: e.tensor_sub(
                sb["rowpf"][0:1, 128:256],
                psB[0:1, 1024 + (tc % 2) * 512: 1024 + (tc % 2) * 512 + 128],
                sb["rowpf"][0:1, 0:128]))
            if DEBUG_STOP == 41:
                q1_marks[tc] = C.c("dve")
                continue
            C.dve(lambda e, tc=tc: e.tensor_copy(
                sb["rowp16"][32:33, tc * 128:(tc + 1) * 128],
                sb["rowpf"][0:1, 128:256]), inc_n=1)
            q1_marks[tc] = C.c("dve")
        brq_d = C.c("dve")
        if DEBUG_STOP in (37, 38, 39, 40, 41, 42, 43) and hh == 0:
            emit_output(C, M)
            return
        C.dma("gpsimd", M["brql"].ap(), sb["rowp16"][32:33, :],
              wait=(("dve", brq_d),), sem="st_nm")
        blq = C.c("st_nm")
        C.dma("sync", sb["rowp16"][1:2, :], APflat(M["brql"], 0, T),
              wait=(("st_nm", blq),), sem="q_nm")
        brq_nm = C.c("q_nm")
        if DEBUG_STOP == 35 and hh == 0:
            emit_output(C, M)
            return
        C.scope("B5_phq2")
        nd_marks = {}
        for nt2 in range(8):
            for mc in range(2):
                ps = psA[0:128, mc * 512: mc * 512 + 512]
                w = [("dve", brq_d), ("q_nm", brq_nm)] if (nt2 == 0 and mc == 0) else []
                if nt2 >= 1:
                    w.append(("dve", nd_marks.get(nt2 - 1, 0)))
                C.pe(lambda e, nt2=nt2, mc=mc, hh=hh, ps=ps: e.matmul(
                    ps, ar["wproj"][0:64, hh * MF + mc * 128: hh * MF + mc * 128 + 128],
                    ar["qkh"][0:64, nt2 * 512:(nt2 + 1) * 512],
                    start=True, stop=False), wait=tuple(w))
                C.pe(lambda e, nt2=nt2, ps=ps: e.matmul(
                    ps, sb["ones16m"][0:2, :],
                    sb["rowp16"][0:2, nt2 * 512:(nt2 + 1) * 512],
                    start=False, stop=True), inc_n=1)
                p2_pe = C.c("pe")
                C.act(lambda e, mc=mc, ps=ps: e.activation(
                    ar["bes2"][:, 0:MF] if False else ar["lnes"][:, mc * 512:(mc + 1) * 512],
                    ps, AF.Exp, bias=0.0, scale=1.0), wait=(("pe", p2_pe),))
                e_a = C.c("act")
                C.dve(lambda e, mc=mc: e.tensor_scalar_add(
                    ar["phq"][:, mc * 512:(mc + 1) * 512],
                    ar["lnes"][:, mc * 512:(mc + 1) * 512], EPSB / 16.0),
                    wait=(("act", e_a),), inc_n=1)
            phq_d = C.c("dve")
            for mc in range(2):
                C.pe(lambda e, mc=mc: e.matmul(
                    psB[0:65, 1024:1536], ar["kvh16"][:, mc * 65:(mc + 1) * 65],
                    ar["phq"][:, mc * 512:(mc + 1) * 512],
                    start=(mc == 0), stop=(mc == 1)),
                    wait=(("dve", phq_d),) if mc == 0 else (),
                    inc_n=1 if mc == 1 else 0)
            nd_pe = C.c("pe")
            C.dve(lambda e: e.tensor_scalar_add(sb["rowpf"][0:1, 1024:1536],
                                                psB[64:65, 1024:1536], EPSB),
                  wait=(("pe", nd_pe),))
            C.dve(lambda e: e.reciprocal(sb["rowpf"][0:1, 1536:2048],
                                         sb["rowpf"][0:1, 1024:1536]), inc_n=1)
            nd_d = C.c("dve")
            C.pe(lambda e: e.matmul(psB[0:64, 0:512], sb["ones_row64"][:, :],
                                    sb["rowpf"][0:1, 1536:2048],
                                    start=True, stop=True),
                 wait=(("dve", nd_d),), inc_n=1)
            ib_pe = C.c("pe")
            C.dve(lambda e: e.tensor_copy(ar["invb"][:, :], psB[0:64, 0:512]),
                  wait=(("pe", ib_pe),))
            C.dve(lambda e: e.tensor_mul(ar["mgt"][0:64, :], psB[0:64, 1024:1536],
                                         ar["invb"][:, :]),
                  wait=(("st_mg", C.c("st_mg")),), inc_n=1)
            mg_d = C.c("dve")
            nd_marks[nt2] = mg_d
            if DEBUG_STOP == 36 and hh == 0 and nt2 == 0:
                emit_output(C, M)
                return
            C.dma("gpsimd",
                  M["mgst"].ap()[hh // 2][(hh % 2) * 64:(hh % 2) * 64 + 64,
                                          nt2 * 512:(nt2 + 1) * 512],
                  ar["mgt"][0:64, :], wait=(("dve", mg_d),), sem="st_mg")
        mg_prev = C.c("dve")
    mg_store = C.c("st_mg")
    phase_sync()
    C.scope("B6_wo")
    C.dma("sync", ar["wout_s"][:, 0:4 * D].rearrange("p (k c) -> p k c", k=4),
          wst["woT"].ap().rearrange("k p c -> p k c"),
          wait=(attn_w, ("pe", C.c("pe"))), sem="q_wo2")
    wo2_ld = C.c("q_wo2")
    ev_prev = 0
    for nt in range(8):
        for kc2 in range(4):
            C.dma("sync", ar["gld"][:, kc2 * 512:(kc2 + 1) * 512],
                  M["mgst"].ap()[kc2][:, nt * 512:(nt + 1) * 512],
                  wait=((("st_mg", mg_store), ("pe", C.c("pe"))) if kc2 == 0
                        else ()), sem="q_gld")
        ml = C.c("q_gld")
        evd = {}
        for dm in range(KC):
            ps = psA[:, (dm % 4) * 512:(dm % 4) * 512 + 512]
            for kc2 in range(4):
                C.pe(lambda e, dm=dm, kc2=kc2, ps=ps: e.matmul(
                    ps, ar["wout_s"][:, kc2 * D + dm * 128: kc2 * D + dm * 128 + 128],
                    ar["gld"][:, kc2 * 512:(kc2 + 1) * 512],
                    start=(kc2 == 0), stop=(kc2 == 3)),
                    wait=(("q_gld", ml), ("q_wo2", wo2_ld),
                          ("dve", evd.get(dm - 4, ev_prev)))
                    if kc2 == 0 else (), inc_n=1 if kc2 == 3 else 0)
            wpe = C.c("pe")
            C.dve(lambda e, dm=dm, ps=ps: e.tensor_copy(
                ar["aotile"][:, dm * 512:(dm + 1) * 512], ps),
                wait=(("pe", wpe), ("st_ao", C.c("st_ao"))), inc_n=1)
            evd[dm] = C.c("dve")
        ev_prev = C.c("dve")
        for dm in range(KC):
            C.dma("gpsimd",
                  M["rs_in"].ap()[nt // 4, dm][:, (nt % 4) * 512:(nt % 4) * 512 + 512],
                  ar["aotile"][:, dm * 512:(dm + 1) * 512],
                  wait=(("dve", ev_prev),) if dm == 0 else (), sem="st_ao")
    rs_w = C.c("st_ao")
    C.op("gpsimd", lambda e: e.collective_compute(
        "ReduceScatter", ALU.add, replica_groups=M["groups"],
        ins=[M["rs_in"].ap().opt()], outs=[M["rs_out"].ap().opt()]),
        wait=(("st_ao", rs_w),), inc=(("cc", 1),))
    rs2_cc = C.c("cc")
    for kc in range(KC):
        sl = slice(kc * TOWN, (kc + 1) * TOWN)
        C.dma("sync", ar["lnes"][:, :], M["rs_out"].ap()[kc],
              wait=((("cc", rs2_cc), ("dve", C.c("dve")))
                    if kc == 0 else (("dve", C.c("dve")),)), sem="q_ao3")
        al = C.c("q_ao3")
        C.dve(lambda e, sl=sl: e.tensor_add(x[:, sl], x[:, sl], ar["lnes"][:, :]),
              wait=(("q_ao3", al),))
        C.dve(lambda e, sl=sl, kc=kc: e.tensor_scalar_add(
            x[:, sl], x[:, sl], sb["bo"][:, kc:kc + 1]), inc_n=1)
    if DEBUG_STOP == 3:
        emit_output(C, M)
        return

    emit_ffn(1)
    emit_output(C, M)


def emit_output(C, M):
    x = M["sb"]["x"]
    w = (("dve", C.c("dve")), ("pe", C.c("pe")), ("act", C.c("act")))
    for kc in range(KC):
        C.dma("gpsimd", M["out_fm"].ap()[kc], x[:, kc * TOWN:(kc + 1) * TOWN],
              wait=w if kc == 0 else (), sem="st_out")
    fin = C.c("st_out")
    C.op("gpsimd", lambda e: e.memset(M["sb"]["colscr"][:, 5:6], 0.0),
         wait=(("st_out", fin),), inc=(("pool", 1),))



def kernel(**inputs):
    f32 = np.float32
    x_full = np.asarray(inputs["L0_attn_a_x"], f32)
    W_in = np.asarray(inputs["L0_attn_a_W_in"], f32)
    b_in = np.asarray(inputs["L0_attn_a_b_in"], f32)
    in_ln_w = np.asarray(inputs["L0_attn_a_in_ln_w"], f32)
    in_ln_b = np.asarray(inputs["L0_attn_a_in_ln_b"], f32)
    W_out = np.asarray(inputs["L0_attn_a_W_out"], f32)
    out_ln_w = np.asarray(inputs["L0_attn_a_out_ln_w"], f32)
    out_ln_b = np.asarray(inputs["L0_attn_a_out_ln_b"], f32)
    smear = np.asarray(inputs["L0_attn_a_smear_factor"], f32)
    lscale = np.asarray(inputs["L0_attn_a_log_scale"], f32)
    Wq = np.asarray(inputs["L0_attn_b_Wq"], f32)
    Wk = np.asarray(inputs["L0_attn_b_Wk"], f32)
    Wv = np.asarray(inputs["L0_attn_b_Wv"], f32)
    Wo = np.asarray(inputs["L0_attn_b_Wo"], f32)
    bo = np.asarray(inputs["L0_attn_b_bo"], f32)
    projw = np.asarray(inputs["L0_attn_b_proj"], f32)
    sg_ = np.asarray(inputs["L0_ffn_shared_gate"], f32)
    su_ = np.asarray(inputs["L0_ffn_shared_up"], f32)
    sd_ = np.asarray(inputs["L0_ffn_shared_down"], f32)
    rg_ = np.asarray(inputs["L0_ffn_routed_gate"], f32)
    ru_ = np.asarray(inputs["L0_ffn_routed_up"], f32)
    rd_ = np.asarray(inputs["L0_ffn_routed_down"], f32)
    Wr = np.asarray(inputs["L0_ffn_Wr"], f32)
    rbias = np.asarray(inputs["L0_ffn_routing_bias"], f32)

    import ml_dtypes
    ac = np.ascontiguousarray
    WinT = W_in.T
    gates = ac(np.concatenate([sg_[None], rg_], 0).reshape(NEXP, KC, 128, HID))
    ups = ac(np.concatenate([su_[None], ru_], 0).reshape(NEXP, KC, 128, HID))
    downs = ac(np.concatenate([sd_[None], rd_], 0).reshape(NEXP, JCN, 128, D))

    ident = np.eye(128, dtype=f32)
    ones_col16 = np.ones((128, 1), dtype=ml_dtypes.bfloat16)
    ones_row16 = np.ones((1, 128), dtype=ml_dtypes.bfloat16)
    ones16m = np.ones((128, 128), dtype=ml_dtypes.bfloat16)
    ones_col64_16 = np.ones((64, 1), dtype=ml_dtypes.bfloat16)
    ones_row64 = np.ones((1, 64), dtype=f32)
    ones_col32 = np.ones((128, 1), dtype=f32)
    tri_loc = np.zeros((128, 129), dtype=f32)
    tri_loc[:, :128] = np.tril(np.ones((128, 128), f32)).T
    tri_loc[:, 128] = 1.0
    tri_off = np.triu(np.ones((32, 32), f32), k=1)
    NEGB = np.float32(-1e9)
    iu = np.arange(896)[None, :]
    ip = np.arange(128)[:, None]
    mask1m = np.where(iu > 384 + ip, NEGB, 0).astype(f32)
    mask2m = np.where(ip > iu - 384, NEGB, 0).astype(f32)

    in_maps = []
    for c in range(8):
        b, r = c // 2, c % 2
        ysl = slice(8192 + 8 * r, 8192 + 8 * r + 8)
        win_own = np.concatenate(
            [WinT[:, blk * 2048 + 1024 * r: blk * 2048 + 1024 * r + 1024]
             for blk in range(4)] + [WinT[:, ysl]], axis=1)
        bqkp = np.stack([
            b_in[0 * 2048 + 1024 * r: 0 * 2048 + 1024 * r + 1024].reshape(8, 128),
            b_in[1 * 2048 + 1024 * r: 1 * 2048 + 1024 * r + 1024].reshape(8, 128),
            b_in[3 * 2048 + 1024 * r: 3 * 2048 + 1024 * r + 1024].reshape(8, 128),
        ], axis=0)
        bqkp = ac(bqkp.reshape(24, 128).T)
        m = dict(
            xT=ac(x_full[b, r * TOWN:(r + 1) * TOWN, :].T.reshape(KC, 128, TOWN)),
            win=ac(win_own.reshape(KC, 128, 4104)),
            bqkp=bqkp,
            bvrow=ac(b_in[2 * 2048 + 1024 * r: 2 * 2048 + 1024 * r + 1024].reshape(8, 128)),
            byrow=ac(b_in[ysl][None, :]),
            lnw=ac(in_ln_w.reshape(KC, 128).T), lnb=ac(in_ln_b.reshape(KC, 128).T),
            olnw=ac(out_ln_w.reshape(KC, 128).T), olnb=ac(out_ln_b.reshape(KC, 128).T),
            woutT=ac(W_out.T[1024 * r:1024 * r + 1024].reshape(KC, 128, D)),
            smear=ac(smear[8 * r:8 * r + 8][None, :]),
            lscale=ac(lscale[8 * r:8 * r + 8][None, :]),
            wqT=ac(Wq.T[:, 512 * r:512 * r + 512].reshape(KC, 128, 512)),
            wkT=ac(Wk.T[:, 512 * r:512 * r + 512].reshape(KC, 128, 512)),
            wvT=ac(Wv.T[:, 512 * r:512 * r + 512].reshape(KC, 128, 512)),
            woT=ac(Wo.T[512 * r:512 * r + 512].reshape(4, 128, D)),
            bo=ac(bo.reshape(KC, 128).T),
            projT=ac(projw[8 * r:8 * r + 8].transpose(0, 2, 1)),
            gates=gates, ups=ups, downs=downs,
            wr=ac(Wr.reshape(KC, 128, 7)), rbias=ac(rbias[None, :]),
            ident=ident, ones_col16=ones_col16, ones_row16=ones_row16,
            ones16m=ones16m,
            ones_col64_16=ones_col64_16, ones_row64=ones_row64,
            ones_col32=ones_col32, tri_loc=tri_loc, tri_off=tri_off,
            mask1m=mask1m, mask2m=mask2m,
        )
        in_maps.append(m)

    nc = build_program()
    res = run_bass_kernel_spmd(nc, in_maps, core_ids=list(range(8)))
    out = np.empty((4, T, D), dtype=np.float32)
    for c in range(8):
        b, r = c // 2, c % 2
        o = res.results[c]["out_fm"]
        out[b, r * TOWN:(r + 1) * TOWN, :] = o.transpose(2, 0, 1).reshape(TOWN, D)
    return out


if __name__ == "__main__":
    import time
    t0 = time.time()
    nc = build_program()
    print(f"build {time.time()-t0:.1f}s n_inst={nc.n_instructions()}")


# version 04
# speedup vs baseline: 1.3772x; 1.0260x over previous
import numpy as np
import concourse.bass as bass
import concourse.mybir as mybir
from concourse.bass_utils import run_bass_kernel_spmd

F32 = mybir.dt.float32
BF16 = mybir.dt.bfloat16
AF = mybir.ActivationFunctionType
ALU = mybir.AluOpType
AX = mybir.AxisListType.X

D = 1024
KC = 8
T = 4096
TOWN = 2048
HN = 8
DK = 64
MF = 256
HID = 2816
JCN = 22
NEXP = 8
LNEPS = 1e-5
LN16 = float(np.log(16.0))
LNS128 = 0.5 * float(np.log(128.0))
EPSB = 1e-6
KB = 1024

DEBUG_STOP = None


def APbc(handle, off, p, n):
    return bass.AP(handle.ap().tensor, off, [[0, p], [1, n]])


def APflat(handle, off, n):
    return bass.AP(handle.ap().tensor, off, [[0, 1], [1, n]])


class Ctx:

    def __init__(self, nc, eng, name, sems):
        self.nc, self.e, self.name, self.sems = nc, eng, name, sems
        self.cnt = {k: 0 for k in sems}
        self._scope = None
        self._sid = None

    def scope(self, name):
        if self._scope is not None:
            self.nc.leave_named_scope(self._scope, self._sid, False)
            self._scope = None
        if name is not None:
            self._sid, _ = self.nc.enter_named_scope(name, False)
            self._scope = name

    def c(self, s):
        return self.cnt[s]

    def op(self, eng, fn, wait=(), inc=()):
        for s, n in inc:
            self.cnt[s] += n
        if eng != self.name:
            return
        for s, v in wait:
            if v > 0:
                self.e.wait_ge(self.sems[s], v)
        if fn is None:
            return
        ins = fn(self.e)
        for s, n in inc:
            ins.then_inc(self.sems[s], n)

    def dma(self, q, out, in_, wait=(), sem=None, **kw):
        assert sem is not None
        self.op(q, lambda e: e.dma_start(out=out, in_=in_, **kw),
                wait=wait, inc=((sem, 16),))
        return self.cnt[sem]

    def pe(self, fn, wait=(), inc_n=0):
        self.op("tensor", fn, wait=wait, inc=((("pe", inc_n),) if inc_n else ()))
        return self.cnt["pe"]

    def dve(self, fn, wait=(), inc_n=0):
        w = tuple(wait) + (("dve", self.cnt["dve"]),)
        self.op("vector", fn, wait=w, inc=(("dve", 1),))
        return self.cnt["dve"]

    def act(self, fn, wait=(), inc_n=1):
        w = tuple(wait) + (("act", self.cnt["act"]),)
        self.op("scalar", fn, wait=w, inc=(("act", 1),))
        return self.cnt["act"]

    def pool(self, fn, wait=(), inc_n=1):
        w = tuple(wait) + (("pool", self.cnt["pool"]),)
        self.op("gpsimd", fn, wait=w, inc=(("pool", 1),))
        return self.cnt["pool"]


def build_program():
    nc = bass.Bass("TRN2", target_bir_lowering=False)

    def din(name, shape, dt=F32):
        return nc.dram_tensor(name, list(shape), dt, kind="ExternalInput")

    h = {}
    h["xT"] = din("xT", [KC, 128, TOWN])
    h["win"] = din("win", [33, 128, 1024])
    h["bqkp"] = din("bqkp", [128, 3 * HN])
    h["bvrow"] = din("bvrow", [HN, 128])
    h["byrow"] = din("byrow", [1, HN])
    h["lnw"] = din("lnw", [128, KC]); h["lnb"] = din("lnb", [128, KC])
    h["olnw"] = din("olnw", [128, KC]); h["olnb"] = din("olnb", [128, KC])
    h["woutT"] = din("woutT", [KC, 128, D])
    h["smear"] = din("smear", [1, HN])
    h["lscale"] = din("lscale", [1, HN])
    h["wqT"] = din("wqT", [HN, 128, 512])
    h["wkT"] = din("wkT", [HN, 128, 512])
    h["wvT"] = din("wvT", [HN, 128, 512])
    h["woT"] = din("woT", [4, 128, D])
    h["bo"] = din("bo", [128, KC])
    h["projT"] = din("projT", [HN, DK, MF])
    h["gates"] = din("gates", [NEXP, JCN, 128, 1024])
    h["ups"] = din("ups", [NEXP, JCN, 128, 1024])
    h["downs"] = din("downs", [NEXP, 4, 128, JCN * 256])
    h["wr"] = din("wr", [KC, 128, 7])
    h["rbias"] = din("rbias", [1, 7])
    h["ident"] = din("ident", [128, 128])
    h["ones_col16"] = din("ones_col16", [128, 1], BF16)
    h["ones_row16"] = din("ones_row16", [1, 128], BF16)
    h["ones16m"] = din("ones16m", [128, 128], BF16)
    h["ones_col64_16"] = din("ones_col64_16", [64, 1], BF16)
    h["ones_row64"] = din("ones_row64", [1, 64])
    h["ones_col32"] = din("ones_col32", [128, 1])
    h["tri_loc"] = din("tri_loc", [128, 129])
    h["tri_off"] = din("tri_off", [32, 32])
    h["mask1m"] = din("mask1m", [128, 896])
    h["mask2m"] = din("mask2m", [128, 896])

    out_fm = nc.dram_tensor("out_fm", [KC, 128, TOWN], F32, kind="ExternalOutput")

    def dint(name, shape, dt):
        return nc.dram_tensor(name, list(shape), dt, kind="Internal")

    wst = {k: dint("wst_" + k, shp, BF16) for k, shp in [
        ("win", [33, 128, 1024]), ("woutT", [KC, 128, D]),
        ("wqT", [HN, 128, 512]), ("wkT", [HN, 128, 512]),
        ("wvT", [HN, 128, 512]), ("woT", [4, 128, D]),
        ("projT", [HN, DK, MF]), ("gates", [NEXP, JCN, 128, 1024]),
        ("ups", [NEXP, JCN, 128, 1024]), ("downs", [NEXP, 4, 128, JCN * 256])]}
    M = dict(nc=nc, h=h, wst=wst, out_fm=out_fm)
    M["ag_in"] = dint("ag_in", [KC, 128, TOWN], BF16)
    M["ag_out"] = dint("ag_out", [2 * KC, 128, TOWN], BF16)
    M["rs_in"] = dint("rs_in", [2, KC, 128, TOWN], F32)
    M["rs_out"] = dint("rs_out", [KC, 128, TOWN], F32)
    M["qst"] = dint("qst", [HN, 128, T], BF16)
    M["kst"] = dint("kst", [HN, 128, T], BF16)
    M["vst"] = dint("vst", [HN, 128, T], BF16)
    M["spst"] = dint("spst", [HN, 128, T], BF16)
    M["gst"] = dint("gst", [HN, 128, T], BF16)
    M["mgst"] = dint("mgst", [4, 128, T], BF16)
    M["rowd"] = dint("rowd", [2, TOWN], F32)
    M["lnrows"] = dint("lnrows", [3, TOWN], F32)
    M["poshd"] = dint("poshd", [HN, T], BF16)
    M["posld"] = dint("posld", [HN, T], BF16)
    M["mrowd"] = dint("mrowd", [32, 128], BF16)
    M["brql"] = dint("brql", [1, T], BF16)
    M["linvd"] = dint("linvd", [HN, T], F32)
    M["wrowd"] = dint("wrowd", [7, TOWN], F32)
    M["prmd"] = dint("prmd", [1, 3 * HN], F32)
    M["groups"] = [[0, 1], [2, 3], [4, 5], [6, 7]]

    sb = {}

    def asb(name, shape, dt=F32):
        sb[name] = nc.alloc_sbuf_tensor("s_" + name, list(shape), dt)
        return sb[name]

    asb("x", [128, KC * TOWN])
    asb("ident", [128, 128])
    asb("ones_col16", [128, 1], BF16)
    asb("ones_row16", [1, 128], BF16)
    asb("ones16m", [128, 128], BF16)
    asb("ones_col64_16", [64, 1], BF16)
    asb("ones_row64", [1, 64])
    asb("ones_col32", [128, 1])
    asb("tri_loc", [128, 129])
    asb("tri_off", [32, 32])
    asb("lnw", [128, KC]); asb("lnb", [128, KC])
    asb("olnw", [128, KC]); asb("olnb", [128, KC])
    asb("bo", [128, KC])
    asb("bqkp", [128, 3 * HN])
    asb("pcol", [128, 3 * HN])
    asb("prows", [1, 64])
    asb("ywin", [128, KC * HN], BF16)
    asb("ybb", [128, HN])
    asb("vbb", [128, 128])
    asb("sigy", [128, 32 * HN])
    asb("pos_t", [128, 32 * HN])
    asb("negm32", [128, 32])
    asb("linv", [128, 32])
    asb("rowpf", [128, 2048])
    asb("lncol", [128, 128])
    asb("rowp16", [128, T], BF16)
    asb("rcols", [128, 16])
    asb("rbb", [128, 7])
    asb("colscr", [128, 24])
    asb("colscr3", [128, 8], BF16)
    asb("wt7", [7, 16 * 128])

    ARENA_B = 112 * KB
    arena = nc.alloc_sbuf_tensor("arena", [128, ARENA_B], mybir.dt.uint8)
    abase = nc.lookup_mloc(arena).addr
    ar = {}

    def av(name, off, shape, dt=F32):
        t = nc.alloc_sbuf_tensor_at("a_" + name, list(shape), dt, offset=abase + off)
        nb = int(np.prod(shape[1:])) * mybir.dt.size(dt)
        assert off + nb <= ARENA_B, (name, off, nb)
        ar[name] = t
        return t

    av("xnf", 0, [128, KC * T], BF16)
    av("xno", 0, [128, KC * TOWN], BF16)
    av("lncb", 64 * KB, [128, 2 * TOWN], BF16)
    av("lncq32", 64 * KB, [128, TOWN])
    av("lncq", 72 * KB, [128, 2 * TOWN], BF16)
    av("Rb", 80 * KB, [128, TOWN])
    av("Bb", 88 * KB, [128, TOWN])
    av("lnes", 96 * KB, [128, TOWN])
    av("lnes2", 64 * KB, [128, TOWN])
    av("lnrt", 104 * KB, [48, 128])
    av("posfm", 64 * KB, [8, T])
    av("poshi16", 80 * KB, [8, T], BF16)
    av("poslo16", 88 * KB, [8, T], BF16)
    av("hi32h", 96 * KB, [8, TOWN])
    av("winblk", 64 * KB, [128, 2 * KC * 128], BF16)
    av("stg16", 68 * KB, [128, T], BF16)
    av("ksc", 76 * KB, [128, T], BF16)
    av("qh", 0, [128, 2 * T], BF16)
    av("kh", 16 * KB, [128, 2 * T], BF16)
    av("vh", 32 * KB, [128, 2 * T], BF16)
    av("sph", 48 * KB, [128, 2 * T], BF16)
    av("pts", 64 * KB, [128, 2 * 512], BF16)
    av("pthrow", 66 * KB, [128, 2048], BF16)
    av("esmall", 70 * KB, [128, 512])
    av("gtile", 72 * KB, [128, 512], BF16)
    av("invlb", 73 * KB, [128, 512])
    av("nm32t", 75 * KB, [32, 128])
    av("nm16t", 75 * KB + 512, [32, 128], BF16)
    av("masks1", 76 * KB, [128, 896])
    av("masks2", 76 * KB + 3584, [128, 896])
    av("wout_s", 0, [128, KC * D], BF16)
    av("gld", 16 * KB, [128, HN * 512], BF16)
    av("aotile", 24 * KB, [128, KC * 512])
    av("hbuf", 32 * KB, [128, JCN * 1024], BF16)
    av("dquart", 76 * KB, [128, JCN * 256], BF16)
    av("gring", 87 * KB, [128, 4 * 1024], BF16)
    av("uring", 95 * KB, [128, 4 * 1024], BF16)
    av("wbb", 103 * KB, [128, 1024])
    av("sg", 107 * KB, [128, 1024])
    av("fes", 107 * KB, [128, 512])
    av("ltile", 111 * KB, [128, 8])
    av("ptile", 111 * KB + 64, [128, 8])
    av("wr32", 111 * KB + 128, [128, KC * 7])
    av("wqkv3", 64 * KB, [128, 3 * 512], BF16)
    av("wproj", 67 * KB, [128, HN * MF], BF16)
    av("qkh", 71 * KB, [128, T], BF16)
    av("vb_h", 79 * KB, [128, 32 * 65], BF16)
    av("phk16", 83 * KB + 512, [128, MF], BF16)
    av("sqs", 84 * KB, [64, 128])
    av("bes2", 84 * KB + 512, [128, MF])
    av("kvh", 85 * KB + 512, [128, 2 * 65])
    av("kvh16", 86 * KB + 64, [128, 2 * 65], BF16)
    av("phq", 86 * KB + 512, [128, 2 * 512], BF16)
    av("invb", 88 * KB + 512, [64, 512])
    av("mgt", 90 * KB + 512, [128, 512], BF16)
    M["sb"], M["ar"] = sb, ar

    M["psA"] = nc.alloc_psum_tensor("psA", [128, 2048], F32)
    M["psB"] = nc.alloc_psum_tensor("psB", [128, 2048], F32)

    semnames = ["q_p0", "q_prm", "q_ywin", "q_xnf", "q_win0", "q_win1",
                "q_mask", "q_hl0", "q_hl1", "q_rp", "q_nm", "q_gld", "q_wout",
                "q_wo2", "q_wr", "q_wqkv", "q_wproj", "q_mg", "q_dh",
                "q_g0", "q_g1", "q_g2", "q_g3", "q_u0", "q_u1", "q_u2", "q_u3",
                "q_cvi", "q_cvoA", "q_cvoF",
                "q_pcol", "q_ybb", "q_vbb", "q_RB", "q_inv", "q_wbb", "q_rbb",
                "st_prm", "st_lnr", "st_pos", "st_stg", "st_nm", "st_g",
                "st_ao", "st_wrow", "st_mg", "st_out", "st_ag", "q_ao3",
                "pe", "dve", "act", "pool", "cc"]
    sems = {s: nc.alloc_semaphore(name=s) for s in semnames}
    M["sems"] = sems

    with nc.Block() as block:
        @block.sync
        def _(e):
            _C = Ctx(nc, e, "sync", sems)
            emit_program(_C, M)
            _C.scope(None)

        @block.tensor
        def _(e):
            _C = Ctx(nc, e, "tensor", sems)
            emit_program(_C, M)
            _C.scope(None)

        @block.vector
        def _(e):
            _C = Ctx(nc, e, "vector", sems)
            emit_program(_C, M)
            _C.scope(None)

        @block.scalar
        def _(e):
            _C = Ctx(nc, e, "scalar", sems)
            emit_program(_C, M)
            _C.scope(None)

        @block.gpsimd
        def _(e):
            _C = Ctx(nc, e, "gpsimd", sems)
            emit_program(_C, M)
            _C.scope(None)

    return nc


def emit_program(C, M):
    h, wst, sb, ar = M["h"], M["wst"], M["sb"], M["ar"]
    psA, psB = M["psA"], M["psB"]
    x = sb["x"]
    xnf = ar["xnf"]

    def phase_sync():
        w = tuple((s, C.c(s)) for s in ("pe", "dve", "act"))
        for en in ("sync", "tensor", "vector", "scalar", "gpsimd"):
            C.op(en, None, wait=w)

    C.scope("P0_load")
    for kc in range(KC):
        C.dma("sync", x[:, kc * TOWN:(kc + 1) * TOWN], h["xT"].ap()[kc], sem="q_p0")
    for name in ["ident", "ones_col16", "ones_row16", "ones16m", "ones_col64_16",
                 "ones_row64", "ones_col32", "tri_loc", "tri_off",
                 "lnw", "lnb", "olnw", "olnb", "bo", "bqkp"]:
        C.dma("sync", sb[name][:, :], h[name].ap(), sem="q_p0")
    x_ld = C.c("q_p0")

    conv_jobs = [(k, None, "q_cvoA") for k in
                 ["win", "woutT", "wqT", "wkT", "wvT", "woT", "projT"]]
    N_ATTN_JOBS = len(conv_jobs)
    for k in ["gates", "ups", "downs"]:
        for o in range(NEXP):
            conv_jobs.append((k, o, "q_cvoF"))
    cs = dict(i=0)

    def conv_pump(n):
        for _ in range(n):
            i = cs["i"]
            if i >= len(conv_jobs):
                return
            cs["i"] = i + 1
            dkey, o, osem = conv_jobs[i]
            sv, dv = h[dkey].ap(), wst[dkey].ap()
            if o is None:
                C.dma("gpsimd", dv, sv, sem=osem)
            else:
                C.dma("gpsimd", dv[o], sv[o], sem=osem)

    conv_pump(N_ATTN_JOBS)
    attn_w = ("q_cvoA", C.c("q_cvoA"))
    if DEBUG_STOP == 11:
        emit_output(C, M)
        return

    C.dma("sync", sb["prows"][0:1, 0:HN], h["lscale"].ap(), sem="q_prm")
    C.dma("sync", sb["prows"][0:1, 8:16], h["smear"].ap(), sem="q_prm")
    ldp = C.c("q_prm")
    C.dve(lambda e: e.tensor_scalar(sb["prows"][0:1, 48:56], sb["prows"][0:1, 0:8],
                                    -2.0, -LNS128, ALU.mult, ALU.add),
          wait=(("q_prm", ldp),), inc_n=1)
    C.act(lambda e: e.activation(sb["prows"][0:1, 16:24], sb["prows"][0:1, 48:56],
                                 AF.Exp, bias=0.0, scale=1.0),
          wait=(("dve", C.c("dve")),))
    C.act(lambda e: e.activation(sb["prows"][0:1, 24:32], sb["prows"][0:1, 8:16],
                                 AF.Sigmoid, bias=0.0, scale=1.0))
    pa = C.c("act")
    C.dve(lambda e: e.tensor_scalar(sb["prows"][0:1, 32:40], sb["prows"][0:1, 24:32],
                                    -1.0, 1.0, ALU.mult, ALU.add),
          wait=(("act", pa),), inc_n=1)
    pd = C.c("dve")
    C.dma("gpsimd", M["prmd"].ap()[0:1, :], sb["prows"][0:1, 16:40],
          wait=(("dve", pd),), sem="st_prm")
    g0 = C.c("st_prm")
    C.dma("scalar", sb["pcol"][:, :], APbc(M["prmd"], 0, 128, 24),
          wait=(("st_prm", g0),), sem="q_pcol")
    pcol_ld = C.c("q_pcol")

    def ccol(hh):
        return sb["pcol"][:, hh:hh + 1]

    def smcol(hh):
        return sb["pcol"][:, 8 + hh:9 + hh]

    def sm1col(hh):
        return sb["pcol"][:, 16 + hh:17 + hh]

    def emit_ln(src, out16, affine=False, fused_double=False, wcol=None,
                bcol=None, router_rows=False, src_wait=()):
        cq = ar["lncq32"]
        Rb, Bb = ar["Rb"], ar["Bb"]
        mucol = sb["lncol"][:, 0:16]
        vcol = sb["lncol"][:, 16:32]
        Rcol = sb["lncol"][:, 32:48]
        Bcol = sb["lncol"][:, 48:64]
        acc_s = sb["lncol"][:, 96:112]
        acc_q = sb["lncol"][:, 112:128]
        mm_marks = []
        for kc in range(KC):
            sl = slice(kc * TOWN, (kc + 1) * TOWN)
            w = list(src_wait) if kc == 0 else []
            if kc == 0:
                w.append(("pe", C.c("pe")))
                w.append(("dve", C.c("dve")))
            if kc >= 1:
                w.append(("pe", mm_marks[kc - 1][1]))
            C.act(lambda e, sl=sl: e.activation(cq[:, :], src[:, sl], AF.Square,
                                                bias=0.0, scale=1.0),
                  wait=tuple(w))
            ca = C.c("act")
            for tg in range(16):
                C.pe(lambda e, tg=tg, sl=sl: e.matmul(
                    psA[:, 1536 + tg:1536 + tg + 1],
                    src[:, sl.start + tg * 128: sl.start + tg * 128 + 128],
                    sb["ones_col32"][:, :], start=True, stop=True),
                    wait=((("act", ca),) +
                          ((("dve", mm_marks[kc - 1][0]),) if kc >= 1 else ())
                          if tg == 0 else ()))
                C.pe(lambda e, tg=tg: e.matmul(
                    psB[:, 1536 + tg:1536 + tg + 1],
                    cq[:, tg * 128:(tg + 1) * 128],
                    sb["ones_col32"][:, :], start=True, stop=True),
                    inc_n=1 if tg == 15 else 0)
            mmk = C.c("pe")
            if kc == 0:
                C.dve(lambda e: e.tensor_copy(acc_s, psA[:, 1536:1552]),
                      wait=(("pe", mmk),))
                C.dve(lambda e: e.tensor_copy(acc_q, psB[:, 1536:1552]), inc_n=1)
            else:
                C.dve(lambda e: e.tensor_add(acc_s, acc_s, psA[:, 1536:1552]),
                      wait=(("pe", mmk),))
                C.dve(lambda e: e.tensor_add(acc_q, acc_q, psB[:, 1536:1552]),
                      inc_n=1)
            mm_marks.append((C.c("dve"), mmk))
        C.dve(lambda e: e.tensor_scalar_mul(mucol, acc_s, 1.0 / D))
        C.dve(lambda e: e.tensor_scalar_mul(vcol, acc_q, 1.0 / D))
        C.dve(lambda e: e.tensor_mul(Bcol, mucol, mucol))
        C.dve(lambda e: e.tensor_sub(vcol, vcol, Bcol))
        eb, sc = (LNEPS * LNEPS, 1.0 + LNEPS) if fused_double else (LNEPS, 1.0)
        C.dve(lambda e, eb=eb, sc=sc: e.tensor_scalar(vcol, vcol, sc, eb,
                                                      ALU.mult, ALU.add), inc_n=1)
        var_d = C.c("dve")
        C.act(lambda e: e.activation(Bcol, vcol, AF.Ln, bias=0.0, scale=1.0),
              wait=(("dve", var_d),))
        C.act(lambda e: e.activation(Rcol, Bcol, AF.Exp, bias=0.0, scale=-0.5))
        r_a = C.c("act")
        C.dve(lambda e: e.tensor_mul(Bcol, mucol, Rcol), wait=(("act", r_a),))
        C.dve(lambda e: e.tensor_scalar_mul(Bcol, Bcol, -1.0))
        C.dve(lambda e: e.tensor_scalar_mul(sb["lncol"][:, 64:80], mucol, -1.0))
        if router_rows:
            C.dve(lambda e: e.tensor_copy(sb["rcols"][:, :], Rcol))
        rb_d = C.c("dve")
        C.pe(lambda e: e.transpose(psA[0:48, 0:128], sb["lncol"][:, 32 if not router_rows else 32:32], sb["ident"][:, :])
             if False else e.transpose(psA[0:48, 0:128], sb["lncol"][:, 32:80],
                                       sb["ident"][:, :]),
             wait=(("dve", rb_d),), inc_n=1)
        tr_pe = C.c("pe")
        C.dve(lambda e: e.tensor_copy(ar["lnrt"][0:48, :], psA[0:48, 0:128]),
              wait=(("pe", tr_pe),), inc_n=1)
        ev_d = C.c("dve")
        C.dma("gpsimd", M["lnrows"].ap(), ar["lnrt"][0:48, :],
              wait=(("dve", ev_d),), sem="st_lnr")
        g1 = C.c("st_lnr")
        C.dma("scalar", Rb[:, :], APbc(M["lnrows"], 0, 128, TOWN),
              wait=(("st_lnr", g1), ("dve", C.c("dve"))), sem="q_RB")
        C.dma("scalar", Bb[:, :], APbc(M["lnrows"], TOWN, 128, TOWN), sem="q_RB")
        if router_rows:
            C.dma("scalar", sb["rowpf"][0:1, :], APflat(M["lnrows"], 2 * TOWN, TOWN),
                  sem="q_RB")
        bc_v = C.c("q_RB")
        esf = ar["lnes"]
        for kc in range(KC):
            sl = slice(kc * TOWN, (kc + 1) * TOWN)
            C.dve(lambda e, sl=sl: e.tensor_mul(esf[:, :], src[:, sl], Rb[:, :]),
                  wait=(("q_RB", bc_v),) if kc == 0 else ())
            if affine:
                C.dve(lambda e: e.tensor_add(esf[:, :], esf[:, :], Bb[:, :]))
                C.dve(lambda e, sl=sl, kc=kc: e.tensor_scalar(
                    out16[:, sl], esf[:, :], wcol[:, kc:kc + 1],
                    bcol[:, kc:kc + 1], ALU.mult, ALU.add), inc_n=1)
            else:
                C.dve(lambda e, sl=sl: e.tensor_add(out16[:, sl], esf[:, :],
                                                    Bb[:, :]), inc_n=1)
        return C.c("dve")

    def emit_ag(out16own, xn_done):
        for kc in range(KC):
            C.dma("gpsimd", M["ag_in"].ap()[kc],
                  out16own[:, kc * TOWN:(kc + 1) * TOWN],
                  wait=(("dve", xn_done),) if kc == 0 else (), sem="st_ag")
        agw = C.c("st_ag")
        C.op("gpsimd", lambda e: e.collective_compute(
            "AllGather", ALU.bypass, replica_groups=M["groups"],
            ins=[M["ag_in"].ap().opt()], outs=[M["ag_out"].ap().opt()]),
            wait=(("st_ag", agw),), inc=(("cc", 1),))
        ccv = C.c("cc")
        for kc in range(KC):
            for rk in range(2):
                C.dma("sync", xnf[:, kc * T + rk * TOWN: kc * T + (rk + 1) * TOWN],
                      M["ag_out"].ap()[rk * KC + kc],
                      wait=(("cc", ccv), ("pe", C.c("pe"))) if (kc == 0 and rk == 0)
                      else (), sem="q_xnf")
        return C.c("q_xnf")

    C.scope("P1_ln")
    xn_done = emit_ln(x, ar["xno"], affine=True, fused_double=True,
                      wcol=sb["lnw"], bcol=sb["lnb"],
                      src_wait=(("q_p0", x_ld),))
    xnf_ld = emit_ag(ar["xno"], xn_done)
    if DEBUG_STOP == 12:
        emit_output(C, M)
        return

    phase_sync()
    C.scope("P2_pos")
    C.dma("sync", sb["ywin"][:, :], wst["win"].ap()[32][:, 0:64],
          wait=(attn_w,), sem="q_ywin")
    yld = C.c("q_ywin")
    C.dma("scalar", sb["ybb"][:, :], APbc(h["byrow"], 0, 128, HN), sem="q_ybb")
    ybld = C.c("q_ybb")
    y_marks = []
    for tc in range(32):
        ps = psA[:, 0:HN] if tc % 2 == 0 else psA[:, 512:512 + HN]
        if tc == 0:
            w = [("q_ywin", yld), ("q_xnf", xnf_ld), ("q_ybb", ybld)]
        elif tc >= 2:
            w = [("act", y_marks[tc - 2])]
        else:
            w = []
        for kc in range(KC):
            C.pe(lambda e, tc=tc, kc=kc, ps=ps: e.matmul(
                ps, xnf[:, kc * T + tc * 128: kc * T + tc * 128 + 128],
                sb["ywin"][:, kc * HN:(kc + 1) * HN],
                start=(kc == 0), stop=(kc == KC - 1)),
                wait=tuple(w) if kc == 0 else (), inc_n=1 if kc == KC - 1 else 0)
        ypet = C.c("pe")
        C.dve(lambda e, ps=ps: e.tensor_add(ar["lnes"][:, 0:HN], ps, sb["ybb"][:, :]),
              wait=(("pe", ypet), ("act", C.c("act"))), inc_n=1)
        ydv = C.c("dve")
        C.act(lambda e, tc=tc: e.activation(sb["sigy"][:, tc * HN:(tc + 1) * HN],
                                            ar["lnes"][:, 0:HN], AF.Sigmoid,
                                            bias=0.0, scale=1.0),
              wait=(("dve", ydv),))
        y_marks.append(C.c("act"))
    posfm = ar["posfm"]
    tot_marks = []
    for tc in range(32):
        w = [("act", y_marks[-1])] if tc == 0 else []
        if tc >= 4:
            w.append(("dve", tot_marks[tc - 4]))
        C.pe(lambda e, tc=tc: e.matmul(
            psB[0:HN, (tc % 4) * 512:(tc % 4) * 512 + 129],
            sb["sigy"][:, tc * HN:(tc + 1) * HN], sb["tri_loc"][:, :],
            start=True, stop=True), wait=tuple(w), inc_n=1)
        tpe = C.c("pe")
        C.dve(lambda e, tc=tc: e.tensor_copy(
            posfm[:, tc * 128:(tc + 1) * 128],
            psB[0:HN, (tc % 4) * 512:(tc % 4) * 512 + 128]),
            wait=(("pe", tpe),))
        C.dve(lambda e, tc=tc: e.tensor_copy(
            sb["pos_t"][0:HN, tc:tc + 1],
            psB[0:HN, (tc % 4) * 512 + 128:(tc % 4) * 512 + 129]), inc_n=1)
        tot_marks.append(C.c("dve"))
    C.pe(lambda e: e.transpose(psA[0:32, 0:HN], sb["pos_t"][0:HN, 0:32],
                               sb["ident"][0:HN, 0:HN]),
         wait=(("dve", tot_marks[-1]),), inc_n=1)
    C.dve(lambda e: e.tensor_copy(ar["lnes"][0:32, 0:HN], psA[0:32, 0:HN]),
          wait=(("pe", C.c("pe")),), inc_n=1)
    C.pe(lambda e: e.matmul(psA[0:HN, 512:512 + 32], ar["lnes"][0:32, 0:HN],
                            sb["tri_off"][:, :], start=True, stop=True),
         wait=(("dve", C.c("dve")),), inc_n=1)
    C.dve(lambda e: e.tensor_copy(ar["lnes"][0:HN, 0:32], psA[0:HN, 512:512 + 32]),
          wait=(("pe", C.c("pe")),), inc_n=1)
    for tc in range(32):
        C.dve(lambda e, tc=tc: e.tensor_scalar_add(
            posfm[:, tc * 128:(tc + 1) * 128], posfm[:, tc * 128:(tc + 1) * 128],
            ar["lnes"][0:HN, tc:tc + 1]), inc_n=(1 if tc == 31 else 0))
    hi16, lo16, hi32 = ar["poshi16"], ar["poslo16"], ar["hi32h"]
    for hf in range(2):
        sl = slice(hf * TOWN, (hf + 1) * TOWN)
        C.dve(lambda e, sl=sl: e.tensor_copy(hi16[:, sl], posfm[:, sl]))
        C.dve(lambda e, sl=sl: e.tensor_copy(hi32[:, :], hi16[:, sl]))
        C.dve(lambda e, sl=sl: e.tensor_sub(posfm[:, sl], posfm[:, sl], hi32[:, :]))
        C.dve(lambda e, sl=sl: e.tensor_copy(lo16[:, sl], posfm[:, sl]))
        C.dve(lambda e, sl=sl: e.tensor_copy(posfm[:, sl], lo16[:, sl]))
        C.dve(lambda e, sl=sl: e.tensor_add(posfm[:, sl], posfm[:, sl], hi32[:, :]),
              inc_n=1)
    pos_d = C.c("dve")
    C.dma("gpsimd", M["poshd"].ap(), hi16[0:HN, :], wait=(("dve", pos_d),), sem="st_pos")
    C.dma("gpsimd", M["posld"].ap(), lo16[0:HN, :], sem="st_pos")
    pos_st = C.c("st_pos")
    for jc in range(32):
        ps = psB[:, 0:HN] if jc % 2 == 0 else psB[:, 512:512 + HN]
        C.pe(lambda e, jc=jc, ps=ps: e.transpose(
            ps, posfm[0:HN, jc * 128:(jc + 1) * 128], sb["ident"][0:HN, 0:HN]),
            wait=(("dve", pos_d if jc < 2 else C.c("dve")),), inc_n=1)
        tp = C.c("pe")
        C.dve(lambda e, jc=jc, ps=ps: e.tensor_copy(
            sb["pos_t"][:, jc * HN:(jc + 1) * HN], ps),
            wait=(("pe", tp),), inc_n=1)

    if DEBUG_STOP == 13:
        emit_output(C, M)
        return

    phase_sync()
    C.scope("P3_proj")

    def win_load(job, bi):
        wsem = "q_win0" if bi == 0 else "q_win1"
        C.dma("sync", ar["winblk"][:, bi * 1024:(bi + 1) * 1024],
              wst["win"].ap()[job],
              wait=(attn_w, ("pe", C.c("pe"))), sem=wsem)
        return (wsem, C.c(wsem))

    wl = {0: win_load(0, 0)}
    stg_free = 0
    for job in range(HN * 4):
        hh, blk = job // 4, job % 4
        bi = job % 2
        if job + 1 < HN * 4:
            wl[job + 1] = win_load(job + 1, (job + 1) % 2)
        wld = wl[job]
        evac_marks = []
        if blk == 2:
            C.dma("scalar", sb["vbb"][:, :], APbc(h["bvrow"], hh * 128, 128, 128),
                  wait=(("dve", C.c("dve")),), sem="q_vbb")
            vbld = C.c("q_vbb")
            for tc in range(32):
                ps = psB[:, (tc % 4) * 512:(tc % 4) * 512 + 128]
                w = []
                if tc == 0:
                    w = [wld, ("dve", C.c("dve")), ("act", C.c("act"))]
                elif tc >= 4:
                    w = [("dve", evac_marks[tc - 4])]
                for kc in range(KC):
                    C.pe(lambda e, tc=tc, kc=kc, ps=ps, bi=bi: e.matmul(
                        ps, xnf[:, kc * T + tc * 128: kc * T + tc * 128 + 128],
                        ar["winblk"][:, bi * 1024 + kc * 128: bi * 1024 + kc * 128 + 128],
                        start=(kc == 0), stop=(kc == KC - 1)),
                        wait=tuple(w) if kc == 0 else (),
                        inc_n=1 if kc == KC - 1 else 0)
                vpe = C.c("pe")
                C.dve(lambda e, tc=tc, ps=ps: e.tensor_add(
                    ar["stg16"][:, tc * 128:(tc + 1) * 128], ps, sb["vbb"][:, :]),
                    wait=(("pe", vpe), ("q_vbb", vbld)) +
                    ((("st_stg", stg_free),) if tc == 0 else ()), inc_n=1)
                evac_marks.append(C.c("dve"))
            C.dma("gpsimd", M["vst"].ap()[hh], ar["stg16"][:, :],
                  wait=(("dve", C.c("dve")),), sem="st_stg")
            stg_free = C.c("st_stg")
        else:
            for nt in range(8):
                ps = psB[:, (nt % 4) * 512:(nt % 4) * 512 + 512]
                w = []
                if nt == 0:
                    w = [wld, ("q_pcol", pcol_ld), ("dve", C.c("dve")),
                         ("act", C.c("act"))]
                elif nt >= 4:
                    w = [("dve" if blk != 3 else "act", evac_marks[nt - 4])]
                for kc in range(KC):
                    C.pe(lambda e, nt=nt, kc=kc, ps=ps, bi=bi: e.matmul(
                        ps, ar["winblk"][:, bi * 1024 + kc * 128: bi * 1024 + kc * 128 + 128],
                        xnf[:, kc * T + nt * 512: kc * T + nt * 512 + 512],
                        start=(kc == 0), stop=(kc == KC - 1)),
                        wait=tuple(w) if kc == 0 else (),
                        inc_n=1 if kc == KC - 1 else 0)
                ppe = C.c("pe")
                bix = (0 if blk == 0 else (1 if blk == 1 else 2)) * HN + hh
                bqs = sb["bqkp"][:, bix:bix + 1]
                osl = slice(nt * 512, (nt + 1) * 512)
                ww = (("pe", ppe),) + ((("st_stg", stg_free),) if nt == 0 else ())
                if blk == 0:
                    C.dve(lambda e, ps=ps, bqs=bqs, osl=osl, hh=hh: e.tensor_scalar(
                        ar["stg16"][:, osl], ps, bqs, ccol(hh), ALU.add, ALU.mult),
                        wait=ww, inc_n=1)
                    evac_marks.append(C.c("dve"))
                elif blk == 1:
                    C.dve(lambda e, ps=ps, bqs=bqs, osl=osl: e.tensor_scalar_add(
                        ar["stg16"][:, osl], ps, bqs), wait=ww, inc_n=1)
                    evac_marks.append(C.c("dve"))
                else:
                    C.act(lambda e, ps=ps, bqs=bqs, osl=osl: e.activation(
                        ar["stg16"][:, osl], ps, AF.Silu, bias=bqs, scale=1.0),
                        wait=ww)
                    evac_marks.append(C.c("act"))
            if blk == 0:
                C.dma("gpsimd", M["qst"].ap()[hh], ar["stg16"][:, :],
                      wait=(("dve", C.c("dve")),), sem="st_stg")
            elif blk == 1:
                C.dve(lambda e, hh=hh: e.tensor_scalar_mul(
                    ar["ksc"][:, :], ar["stg16"][:, :], smcol(hh)),
                    wait=(("st_pos", pos_st),))
                C.dve(lambda e, hh=hh: e.tensor_scalar_mul(
                    ar["stg16"][:, :], ar["stg16"][:, :], sm1col(hh)))
                C.dve(lambda e: e.tensor_add(ar["stg16"][:, 1:T],
                                             ar["stg16"][:, 1:T],
                                             ar["ksc"][:, 0:T - 1]), inc_n=1)
                C.dma("gpsimd", M["kst"].ap()[hh], ar["stg16"][:, :],
                      wait=(("dve", C.c("dve")),), sem="st_stg")
            else:
                C.dma("gpsimd", M["spst"].ap()[hh], ar["stg16"][:, :],
                      wait=(("act", C.c("act")),), sem="st_stg")
            stg_free = C.c("st_stg")
        pass
    proj_st = C.c("st_stg")

    if DEBUG_STOP == 14:
        emit_output(C, M)
        return

    phase_sync()
    C.scope("P4_attn")
    C.dma("sync", ar["masks2"][:, :], h["mask2m"].ap(),
          wait=(("pe", C.c("pe")), ("dve", C.c("dve"))), sem="q_mask")
    mask_ld = C.c("q_mask")

    def head_load(hh, guard_pe):
        slot = hh % 2
        o = slot * T
        hsem = "q_hl0" if slot == 0 else "q_hl1"
        w = (("st_stg", proj_st), ("pe", guard_pe), ("dve", C.c("dve")),
             ("act", C.c("act")))
        C.dma("sync", ar["qh"][:, o:o + T], M["qst"].ap()[hh], wait=w, sem=hsem)
        C.dma("sync", ar["kh"][:, o:o + T], M["kst"].ap()[hh], sem=hsem)
        C.dma("sync", ar["vh"][:, o:o + T], M["vst"].ap()[hh], sem=hsem)
        C.dma("sync", ar["sph"][:, o:o + T], M["spst"].ap()[hh], sem=hsem)
        return (hsem, C.c(hsem))

    C.dve(lambda e: e.memset(ar["masks1"][0:1, 0:128], 1.0), inc_n=1)
    ones_d = C.c("dve")

    hload = {0: head_load(0, 0)}
    for hh in range(HN):
        conv_pump(3)
        slot = hh % 2
        o = slot * T
        ld_q = hload[hh]
        if hh + 1 < HN:
            hload[hh + 1] = head_load(hh + 1, C.c("pe"))
        bbase = 64 if hh % 2 == 0 else 0
        bnd_r = sb["rowp16"][bbase:bbase + 1, :]
        qh, kh, vh, sph = ar["qh"], ar["kh"], ar["vh"], ar["sph"]
        C.scope("P4_bnd")
        kred = {}
        for c in range(8):
            psrow = (psA if c < 4 else psB)[0:1, (c % 4) * 512:(c % 4) * 512 + 512]
            sq = ar["pthrow"][:, (c % 2) * 512:(c % 2) * 512 + 512]
            w = [("pe", C.c("pe"))]
            if c == 0:
                w.append(ld_q)
            C.act(lambda e, c=c, sq=sq: e.activation(
                sq, kh[:, o + c * 512: o + c * 512 + 512],
                AF.Square, bias=0.0, scale=1.0), wait=tuple(w))
            sq_a = C.c("act")
            C.pe(lambda e, psrow=psrow, sq=sq: e.matmul(
                psrow, sb["ones_col16"][:, 0:1], sq, start=True, stop=True),
                wait=(("act", sq_a), ("dve", C.c("dve"))), inc_n=1)
            mm_c = C.c("pe")
            C.dve(lambda e, c=c, psrow=psrow: e.reduce_max(
                sb["colscr"][0:1, 16 + c:17 + c], psrow, AX),
                wait=(("pe", mm_c),), inc_n=1)
            kred[c] = C.c("dve")
        C.dve(lambda e: e.reduce_max(sb["colscr"][0:1, 8:9],
                                     sb["colscr"][0:1, 16:24], AX), inc_n=1)
        km_d = C.c("dve")
        C.act(lambda e: e.activation(sb["colscr"][0:1, 9:10],
                                     sb["colscr"][0:1, 8:9], AF.Ln,
                                     bias=0.0, scale=1.0),
              wait=(("dve", km_d),))
        lnk_a = C.c("act")
        C.dve(lambda e: e.tensor_scalar(sb["colscr"][0:1, 10:11],
                                        sb["colscr"][0:1, 9:10],
                                        0.5, -0.5 * float(np.log(128.0)),
                                        ALU.mult, ALU.add),
              wait=(("act", lnk_a),), inc_n=1)
        cell_d = C.c("dve")
        C.dma("sync", sb["rowp16"][96:97, :], APflat(M["poshd"], hh * T, T),
              wait=(("st_pos", pos_st), ("pe", C.c("pe"))), sem="q_rp")
        ph_ld = C.c("q_rp")
        for half in range(2):
            for c4 in range(4):
                c = half * 4 + c4
                psrow = (psA if c < 4 else psB)[0:1, (c % 4) * 512:(c % 4) * 512 + 512]
                sq = ar["pthrow"][:, (c % 2) * 512:(c % 2) * 512 + 512]
                C.act(lambda e, c=c, sq=sq: e.activation(
                    sq, qh[:, o + c * 512: o + c * 512 + 512],
                    AF.Square, bias=0.0, scale=1.0),
                    wait=(("pe", C.c("pe")),))
                sq_a = C.c("act")
                C.pe(lambda e, psrow=psrow, sq=sq: e.matmul(
                    psrow, sb["ones_col16"][:, 0:1], sq, start=True, stop=True),
                    wait=(("act", sq_a), ("dve", kred[c])), inc_n=1)
                mm_c = C.c("pe")
                C.act(lambda e, psrow=psrow, c4=c4: e.activation(
                    sb["rowpf"][0:1, c4 * 512:(c4 + 1) * 512], psrow, AF.Ln,
                    bias=0.0, scale=1.0), wait=(("pe", mm_c), ("dve", cell_d)))
                ln_a = C.c("act")
                C.dve(lambda e, c4=c4: e.tensor_scalar(
                    sb["rowpf"][0:1, c4 * 512:(c4 + 1) * 512],
                    sb["rowpf"][0:1, c4 * 512:(c4 + 1) * 512], 0.5, 0.0,
                    ALU.mult, ALU.add), wait=(("act", ln_a),))
                C.dve(lambda e, c4=c4: e.tensor_single_scalar(
                    sb["rowpf"][0:1, c4 * 512:(c4 + 1) * 512],
                    sb["rowpf"][0:1, c4 * 512:(c4 + 1) * 512],
                    sb["colscr"][0:1, 10:11], ALU.add), inc_n=1)
                arg_d = C.c("dve")
                C.act(lambda e, c4=c4: e.activation(
                    ar["esmall"][0:1, 0:512], sb["rowpf"][0:1, c4 * 512:(c4 + 1) * 512],
                    AF.Exp, bias=0.0, scale=1.0), wait=(("dve", arg_d),))
                ex_a = C.c("act")
                C.dve(lambda e, half=half, c4=c4: e.tensor_copy(
                    ar["invlb"][0:1, 0:512],
                    sb["rowp16"][96:97, half * 2048 + c4 * 512:
                                 half * 2048 + c4 * 512 + 512]),
                    wait=(("q_rp", ph_ld), ("act", ex_a)))
                C.dve(lambda e: e.tensor_add(ar["esmall"][0:1, 0:512],
                                             ar["esmall"][0:1, 0:512],
                                             ar["invlb"][0:1, 0:512]))
                C.dve(lambda e, half=half, c4=c4, bbase=bbase: e.tensor_scalar_mul(
                    sb["rowp16"][bbase:bbase + 1,
                                 half * 2048 + c4 * 512:half * 2048 + c4 * 512 + 512],
                    ar["esmall"][0:1, 0:512], -1.0), inc_n=1)
        bnd_d = C.c("dve")
        bnd_a = C.c("act")
        C.scope("P4_attn")
        exp_war = [bnd_a, bnd_a]
        inv_war = 0
        den_war = 0
        for s in range(8):
            jcn = 4 * s + 4
            av_done = 0
            for jc in range(jcn):
                pslot = jc % 2
                psT = psA[:, pslot * 512:(pslot + 1) * 512]
                w = [("act", exp_war[pslot])]
                if jc == 0 and s == 0:
                    w += [("q_mask", mask_ld), ("dve", bnd_d)]
                if jc >= 2:
                    w.append(("pe", av_done))
                C.pe(lambda e, jc=jc, s=s, psT=psT: e.matmul(
                    psT, kh[:, o + jc * 128: o + jc * 128 + 128],
                    qh[:, o + s * 512: o + s * 512 + 512],
                    start=True, stop=False), wait=tuple(w))
                C.pe(lambda e, s=s, psT=psT, bbase=bbase: e.matmul(
                    psT, sb["ones16m"][bbase:bbase + 1, :],
                    bnd_r[:, s * 512: s * 512 + 512],
                    start=False, stop=True), inc_n=1)
                mmT = C.c("pe")
                dd = jc - 4 * s
                if dd >= 0:
                    C.dve(lambda e, dd=dd, psT=psT: e.tensor_add(
                        psT, psT,
                        ar["masks2"][:, 384 - 128 * dd:384 - 128 * dd + 512]),
                        wait=(("pe", mmT),), inc_n=1)
                    mwd = (("dve", C.c("dve")),)
                else:
                    mwd = (("pe", mmT),)
                C.act(lambda e, jc=jc, hh=hh, psT=psT, pslot=pslot: e.activation(
                    ar["pts"][:, pslot * 512:(pslot + 1) * 512], psT, AF.Exp,
                    bias=sb["pos_t"][:, jc * HN + hh: jc * HN + hh + 1], scale=1.0),
                    wait=mwd)
                exp_war[pslot] = C.c("act")
                C.pe(lambda e, jc=jc, jcn=jcn, pslot=pslot: e.matmul(
                    psB[:, 0:512], vh[:, o + jc * 128: o + jc * 128 + 128],
                    ar["pts"][:, pslot * 512:(pslot + 1) * 512],
                    start=(jc == 0), stop=(jc == jcn - 1)),
                    wait=(("act", exp_war[pslot]),) +
                    ((("dve", C.c("dve")),) if jc == 0 else ()),
                    inc_n=1)
                C.pe(lambda e, jc=jc, jcn=jcn, pslot=pslot: e.matmul(
                    psB[0:1, 512:1024], sb["ones_col16"][:, 0:1],
                    ar["pts"][:, pslot * 512:(pslot + 1) * 512],
                    start=(jc == 0), stop=(jc == jcn - 1)),
                    wait=(("dve", den_war),) if jc == 0 else (),
                    inc_n=1)
                av_done = C.c("pe")
            C.dve(lambda e: e.reciprocal(sb["rowpf"][0:1, 0:512], psB[0:1, 512:1024]),
                  wait=(("pe", av_done),), inc_n=1)
            den_war = C.c("dve")
            C.pe(lambda e: e.matmul(psA[:, 1024:1536], ar["masks1"][0:1, 0:128],
                                    sb["rowpf"][0:1, 0:512], start=True, stop=True),
                 wait=(("dve", den_war), ("dve", ones_d), ("dve", inv_war)),
                 inc_n=1)
            ib_pe = C.c("pe")
            C.dve(lambda e: e.tensor_copy(ar["invlb"][:, :], psA[:, 1024:1536]),
                  wait=(("pe", ib_pe),))
            C.dve(lambda e: e.tensor_mul(ar["esmall"][:, :], psB[:, 0:512],
                                         ar["invlb"][:, :]), inc_n=1)
            inv_war = C.c("dve")
            C.dve(lambda e, s=s: e.tensor_mul(
                ar["gtile"][:, :], ar["esmall"][:, :],
                sph[:, o + s * 512: o + s * 512 + 512]),
                wait=(("st_g", C.c("st_g")),), inc_n=1)
            g_d = C.c("dve")
            C.dma("gpsimd", M["gst"].ap()[hh][:, s * 512:(s + 1) * 512],
                  ar["gtile"][:, :], wait=(("dve", g_d),), sem="st_g")
    g_store = C.c("st_g")

    if DEBUG_STOP == 15:
        emit_output(C, M)
        return

    phase_sync()
    C.scope("P5_wout")
    C.dma("sync", ar["wout_s"][:, :].rearrange("p (k c) -> p k c", k=KC),
          wst["woutT"].ap().rearrange("k p c -> p k c"),
          wait=(attn_w, ("pe", C.c("pe"))), sem="q_wout")
    wo_ld = C.c("q_wout")
    ev_prev = 0
    for nt in range(8):
        for hh2 in range(HN):
            C.dma("sync", ar["gld"][:, hh2 * 512:(hh2 + 1) * 512],
                  M["gst"].ap()[hh2][:, nt * 512:(nt + 1) * 512],
                  wait=((("st_g", g_store), ("pe", C.c("pe"))) if hh2 == 0
                        else ()), sem="q_gld")
        gl = C.c("q_gld")
        evd = {}
        for dm in range(KC):
            ps = psA[:, (dm % 4) * 512:(dm % 4) * 512 + 512]
            for hh2 in range(HN):
                C.pe(lambda e, dm=dm, hh2=hh2, ps=ps: e.matmul(
                    ps, ar["wout_s"][:, hh2 * D + dm * 128: hh2 * D + dm * 128 + 128],
                    ar["gld"][:, hh2 * 512:(hh2 + 1) * 512],
                    start=(hh2 == 0), stop=(hh2 == HN - 1)),
                    wait=(("q_gld", gl), ("q_wout", wo_ld),
                          ("dve", evd.get(dm - 4, ev_prev))) if hh2 == 0 else (),
                    inc_n=1 if hh2 == HN - 1 else 0)
            wpe = C.c("pe")
            C.dve(lambda e, dm=dm, ps=ps: e.tensor_copy(
                ar["aotile"][:, dm * 512:(dm + 1) * 512], ps),
                wait=(("pe", wpe), ("st_ao", C.c("st_ao"))), inc_n=1)
            evd[dm] = C.c("dve")
        ev_prev = C.c("dve")
        for dm in range(KC):
            C.dma("gpsimd",
                  M["rs_in"].ap()[nt // 4, dm][:, (nt % 4) * 512:(nt % 4) * 512 + 512],
                  ar["aotile"][:, dm * 512:(dm + 1) * 512],
                  wait=(("dve", ev_prev),) if dm == 0 else (), sem="st_ao")
        conv_pump(8)
    rs_w = C.c("st_ao")
    C.op("gpsimd", lambda e: e.collective_compute(
        "ReduceScatter", ALU.add, replica_groups=M["groups"],
        ins=[M["rs_in"].ap().opt()], outs=[M["rs_out"].ap().opt()]),
        wait=(("st_ao", rs_w),), inc=(("cc", 1),))
    rs_cc = C.c("cc")

    def ln_residual_from_dram(rs_cc, use_oln):
        cb, cq = ar["lncb"], ar["lncq"]
        Rb, Bb = ar["Rb"], ar["Bb"]
        mucol = sb["lncol"][:, 0:16]
        vcol = sb["lncol"][:, 16:32]
        Rcol = sb["lncol"][:, 32:48]
        Bcol = sb["lncol"][:, 48:64]
        acc_s = sb["lncol"][:, 96:112]
        acc_q = sb["lncol"][:, 112:128]
        mm_marks = []
        for kc in range(KC):
            pp = (kc % 2) * TOWN
            C.dma("sync", ar["lnes"][:, :], M["rs_out"].ap()[kc],
                  wait=((("cc", rs_cc), ("pe", C.c("pe")), ("dve", C.c("dve")))
                        if kc == 0 else (("dve", C.c("dve")),)), sem="q_ao3")
            al = C.c("q_ao3")
            w2 = [("q_ao3", al)]
            if kc == 0:
                w2.append(("pe", C.c("pe")))
            if kc >= 1:
                w2.append(("dve", mm_marks[kc - 1]))
            C.dve(lambda e, pp=pp: e.tensor_copy(cb[:, pp:pp + TOWN],
                                                 ar["lnes"][:, :]),
                  wait=tuple(w2), inc_n=1)
            cd = C.c("dve")
            C.act(lambda e, pp=pp: e.activation(cq[:, pp:pp + TOWN],
                                                cb[:, pp:pp + TOWN], AF.Square,
                                                bias=0.0, scale=1.0),
                  wait=(("dve", cd),))
            ca = C.c("act")
            for tg in range(16):
                C.pe(lambda e, tg=tg, pp=pp: e.matmul(
                    psA[:, 1536 + tg:1536 + tg + 1],
                    cb[:, pp + tg * 128:pp + tg * 128 + 128],
                    sb["ones_col16"][:, :], start=True, stop=True),
                    wait=((("dve", cd), ("act", ca)) if tg == 0 else ()))
                C.pe(lambda e, tg=tg, pp=pp: e.matmul(
                    psB[:, 1536 + tg:1536 + tg + 1],
                    cq[:, pp + tg * 128:pp + tg * 128 + 128],
                    sb["ones_col16"][:, :], start=True, stop=True),
                    inc_n=1 if tg == 15 else 0)
            mmk = C.c("pe")
            if kc == 0:
                C.dve(lambda e: e.tensor_copy(acc_s, psA[:, 1536:1552]),
                      wait=(("pe", mmk),))
                C.dve(lambda e: e.tensor_copy(acc_q, psB[:, 1536:1552]), inc_n=1)
            else:
                C.dve(lambda e: e.tensor_add(acc_s, acc_s, psA[:, 1536:1552]),
                      wait=(("pe", mmk),))
                C.dve(lambda e: e.tensor_add(acc_q, acc_q, psB[:, 1536:1552]),
                      inc_n=1)
            mm_marks.append(C.c("dve"))
        C.dve(lambda e: e.tensor_scalar_mul(mucol, acc_s, 1.0 / D))
        C.dve(lambda e: e.tensor_scalar_mul(vcol, acc_q, 1.0 / D))
        C.dve(lambda e: e.tensor_mul(Bcol, mucol, mucol))
        C.dve(lambda e: e.tensor_sub(vcol, vcol, Bcol))
        C.dve(lambda e: e.tensor_scalar(vcol, vcol, 1.0, LNEPS, ALU.mult, ALU.add),
              inc_n=1)
        var_d = C.c("dve")
        C.act(lambda e: e.activation(Bcol, vcol, AF.Ln, bias=0.0, scale=1.0),
              wait=(("dve", var_d),))
        C.act(lambda e: e.activation(Rcol, Bcol, AF.Exp, bias=0.0, scale=-0.5))
        r_a = C.c("act")
        C.dve(lambda e: e.tensor_mul(Bcol, mucol, Rcol), wait=(("act", r_a),))
        C.dve(lambda e: e.tensor_scalar_mul(Bcol, Bcol, -1.0), inc_n=1)
        rb_d = C.c("dve")
        C.pe(lambda e: e.transpose(psA[0:32, 0:128], sb["lncol"][:, 32:64],
                                   sb["ident"][:, :]),
             wait=(("dve", rb_d),), inc_n=1)
        tr_pe = C.c("pe")
        C.dve(lambda e: e.tensor_copy(ar["lnrt"][0:32, :], psA[0:32, 0:128]),
              wait=(("pe", tr_pe),), inc_n=1)
        ev_d = C.c("dve")
        C.dma("gpsimd", M["lnrows"].ap()[0:2], ar["lnrt"][0:32, :],
              wait=(("dve", ev_d),), sem="st_lnr")
        g1 = C.c("st_lnr")
        C.dma("scalar", Rb[:, :], APbc(M["lnrows"], 0, 128, TOWN),
              wait=(("st_lnr", g1), ("dve", C.c("dve"))), sem="q_RB")
        C.dma("scalar", Bb[:, :], APbc(M["lnrows"], TOWN, 128, TOWN), sem="q_RB")
        bc_v = C.c("q_RB")
        for kc in range(KC):
            sl = slice(kc * TOWN, (kc + 1) * TOWN)
            C.dma("sync", ar["lnes"][:, :], M["rs_out"].ap()[kc],
                  wait=(("dve", C.c("dve")),), sem="q_ao3")
            al = C.c("q_ao3")
            C.dve(lambda e: e.tensor_mul(ar["lnes2"][:, :], ar["lnes"][:, :],
                                         Rb[:, :]),
                  wait=(("q_ao3", al), ("q_RB", bc_v)))
            C.dve(lambda e: e.tensor_add(ar["lnes2"][:, :], ar["lnes2"][:, :],
                                         Bb[:, :]))
            C.dve(lambda e, kc=kc: e.tensor_scalar(
                ar["lnes2"][:, :], ar["lnes2"][:, :], sb["olnw"][:, kc:kc + 1],
                sb["olnb"][:, kc:kc + 1], ALU.mult, ALU.add))
            C.dve(lambda e, sl=sl: e.tensor_add(x[:, sl], x[:, sl],
                                                ar["lnes2"][:, :]), inc_n=1)

    ln_residual_from_dram(rs_cc, True)
    if DEBUG_STOP == 1:
        emit_output(C, M)
        return

    def emit_ffn(call):
        phase_sync()
        C.scope(f"F{call}_ffn")
        conv_pump(len(conv_jobs))
        ffn_w = ("q_cvoF", C.c("q_cvoF"))
        xn_done = emit_ln(x, ar["xno"], affine=False, router_rows=True)
        xno = ar["xno"]
        C.dma("sync", ar["wr32"][:, :].rearrange("p (k c) -> p k c", k=KC),
              h["wr"].ap().rearrange("k p c -> p k c"),
              wait=(("dve", xn_done), ("pe", C.c("pe"))), sem="q_wr")
        wr_ld = C.c("q_wr")
        for kc in range(KC):
            C.pe(lambda e, kc=kc: e.matmul(
                psB[0:1, 1024:1031], sb["ones_col32"][:, :],
                ar["wr32"][:, kc * 7:(kc + 1) * 7],
                start=(kc == 0), stop=(kc == KC - 1)),
                wait=((("q_wr", wr_ld),)) if kc == 0 else (),
                inc_n=1 if kc == KC - 1 else 0)
        swr_pe = C.c("pe")
        C.dve(lambda e: e.tensor_scalar_mul(sb["prows"][0:1, 40:47],
                                            psB[0:1, 1024:1031], -1.0),
              wait=(("pe", swr_pe),), inc_n=1)
        swr_d = C.c("dve")
        C.dma("scalar", sb["rbb"][:, :], APbc(h["rbias"], 0, 128, 7),
              wait=(("dve", C.c("dve")),), sem="q_rbb")
        rb_ld = C.c("q_rbb")
        wt_marks = {}
        for tg in range(16):
            psl = psB[:, 1536 + (tg % 2) * 16: 1536 + (tg % 2) * 16 + 7]
            w = [("dve", xn_done)] if tg == 0 else []
            if tg >= 2:
                w.append(("dve", wt_marks[tg - 2]))
            for kc in range(KC):
                C.pe(lambda e, tg=tg, kc=kc, psl=psl: e.matmul(
                    psl, x[:, kc * TOWN + tg * 128: kc * TOWN + tg * 128 + 128],
                    ar["wr32"][:, kc * 7:(kc + 1) * 7],
                    start=(kc == 0), stop=False),
                    wait=tuple(w) if kc == 0 else ())
            C.pe(lambda e, tg=tg, psl=psl: e.matmul(
                psl, sb["rowpf"][0:1, tg * 128:(tg + 1) * 128],
                sb["prows"][0:1, 40:47], start=False, stop=True),
                wait=(("dve", swr_d),) if tg == 0 else (), inc_n=1)
            rpe = C.c("pe")
            C.dve(lambda e, tg=tg, psl=psl: e.tensor_scalar_mul(
                ar["ltile"][:, 0:7], psl, sb["rcols"][:, tg:tg + 1]),
                wait=(("pe", rpe),))
            C.dve(lambda e: e.tensor_mul(ar["ltile"][:, 0:7], ar["ltile"][:, 0:7],
                                         sb["rbb"][:, :]),
                  wait=(("q_rbb", rb_ld),), inc_n=1)
            lg_d = C.c("dve")
            C.act(lambda e: e.activation(ar["ptile"][:, 0:7], ar["ltile"][:, 0:7],
                                         AF.Sigmoid, bias=0.0, scale=1.0),
                  wait=(("dve", lg_d),))
            pr_a = C.c("act")
            c0 = sb["colscr"][:, 0:1]
            c1 = sb["colscr"][:, 1:2]
            C.dve(lambda e: e.reduce_max(c0, ar["ptile"][:, 0:7], AX),
                  wait=(("act", pr_a),))
            C.dve(lambda e: e.tensor_single_scalar(
                ar["ltile"][:, 0:7], ar["ptile"][:, 0:7], c0, ALU.is_ge))
            C.dve(lambda e: e.tensor_scalar_mul(ar["ltile"][:, 0:7],
                                                ar["ltile"][:, 0:7], 1e30))
            C.dve(lambda e: e.tensor_sub(ar["ltile"][:, 0:7], ar["ptile"][:, 0:7],
                                         ar["ltile"][:, 0:7]))
            C.dve(lambda e: e.reduce_max(c1, ar["ltile"][:, 0:7], AX))
            C.dve(lambda e: e.tensor_single_scalar(
                ar["ltile"][:, 0:7], ar["ptile"][:, 0:7], c1, ALU.is_ge))
            C.dve(lambda e: e.tensor_add(c0, c0, c1))
            C.dve(lambda e: e.reciprocal(c1, c0))
            C.dve(lambda e: e.tensor_mul(ar["ltile"][:, 0:7], ar["ltile"][:, 0:7],
                                         ar["ptile"][:, 0:7]))
            C.dve(lambda e: e.tensor_scalar_mul(ar["ltile"][:, 0:7],
                                                ar["ltile"][:, 0:7], c1), inc_n=1)
            wv_d = C.c("dve")
            C.pe(lambda e, tg=tg: e.transpose(
                psB[0:7, 1280 + (tg % 2) * 128: 1280 + (tg % 2) * 128 + 128],
                ar["ltile"][:, 0:7], sb["ident"][:, :]),
                wait=(("dve", wv_d),), inc_n=1)
            tpe = C.c("pe")
            C.dve(lambda e, tg=tg: e.tensor_copy(
                sb["wt7"][0:7, tg * 128:(tg + 1) * 128],
                psB[0:7, 1280 + (tg % 2) * 128: 1280 + (tg % 2) * 128 + 128]),
                wait=(("pe", tpe),), inc_n=1)
            wt_marks[tg] = C.c("dve")
        C.dma("gpsimd", M["wrowd"].ap(), sb["wt7"][0:7, :],
              wait=(("dve", wt_marks[15]),), sem="st_wrow")
        wrow_st = C.c("st_wrow")

        ln_guard = C.c("dve")
        st = M.setdefault("ffn_state", dict(
            g_act={}, u_dve={}, d_dve={}, mul_last={0: 0, 1: 0},
            down_pe=0, slot_war={}, pending={}))

        def ring_load(eidx, jc, what, first):
            wsrc = wst["gates"] if what == "g" else wst["ups"]
            buf = ar["gring"] if what == "g" else ar["uring"]
            slx = jc % 4
            rsem = f"q_{what}{slx}"
            w = []
            if first:
                w += [ffn_w, ("dve", ln_guard), ("st_wrow", wrow_st)]
            pw = st["slot_war"].get((what, slx), 0)
            if pw:
                w.append(("pe", pw))
            C.dma("sync", buf[:, slx * 1024:(slx + 1) * 1024],
                  wsrc.ap()[eidx, jc], wait=tuple(w), sem=rsem)
            return (rsem, C.c(rsem))

        exps = [(nt, e) for nt in range(2) for e in range(NEXP)]

        def prefill(key, first=False):
            nt_, e_ = key
            d = {}
            for j0 in range(4):
                d[("g", j0)] = ring_load(e_, j0, "g", first and j0 == 0)
                d[("u", j0)] = ring_load(e_, j0, "u", False)
            st["pending"][key] = d

        prefill(exps[0], first=True)
        for ei, (nt, eidx) in enumerate(exps):
            tb = nt * 1024
            ring_ld = st["pending"].pop((nt, eidx))
            if eidx > 0:
                C.dma("scalar", ar["wbb"][:, 0:1024],
                      APbc(M["wrowd"], (eidx - 1) * TOWN + tb, 128, 1024),
                      wait=(("st_wrow", wrow_st),)
                      + tuple(("dve", v) for v in st["d_dve"].values()),
                      sem="q_wbb")
            wb_ld = C.c("q_wbb")
            for jc in range(JCN):
                gl = ring_ld[("g", jc)]
                ul = ring_ld[("u", jc)]
                gsl = ar["gring"][:, (jc % 4) * 1024:(jc % 4) * 1024 + 1024]
                usl = ar["uring"][:, (jc % 4) * 1024:(jc % 4) * 1024 + 1024]
                gate_pe = {}
                for ns in range(2):
                    pg = psA[:, (jc % 2) * 1024 + ns * 512:
                             (jc % 2) * 1024 + ns * 512 + 512]
                    w = [gl, ("act", st["g_act"].get((jc % 2, ns), 0))]
                    if jc < 2:
                        w.append(("dve", st["d_dve"].get((jc % 2, ns), 0)))
                    for kc in range(KC):
                        C.pe(lambda e, kc=kc, gsl=gsl, pg=pg, ns=ns: e.matmul(
                            pg, gsl[:, kc * 128:(kc + 1) * 128],
                            xno[:, kc * TOWN + tb + ns * 512:
                                 kc * TOWN + tb + ns * 512 + 512],
                            start=(kc == 0), stop=(kc == KC - 1)),
                            wait=tuple(w) if kc == 0 else (),
                            inc_n=1 if kc == KC - 1 else 0)
                    gate_pe[ns] = C.c("pe")
                for ns in range(2):
                    pu = psB[:, (jc % 2) * 1024 + ns * 512:
                             (jc % 2) * 1024 + ns * 512 + 512]
                    w = [ul, ("dve", st["u_dve"].get((jc % 2, ns), 0))]
                    for kc in range(KC):
                        C.pe(lambda e, kc=kc, usl=usl, pu=pu, ns=ns: e.matmul(
                            pu, usl[:, kc * 128:(kc + 1) * 128],
                            xno[:, kc * TOWN + tb + ns * 512:
                                 kc * TOWN + tb + ns * 512 + 512],
                            start=(kc == 0), stop=(kc == KC - 1)),
                            wait=tuple(w) if kc == 0 else (),
                            inc_n=1 if kc == KC - 1 else 0)
                up_pe = C.c("pe")
                st["slot_war"][("g", jc % 4)] = up_pe
                st["slot_war"][("u", jc % 4)] = up_pe
                if jc + 4 < JCN:
                    ring_ld[("g", jc + 4)] = ring_load(eidx, jc + 4, "g", False)
                    ring_ld[("u", jc + 4)] = ring_load(eidx, jc + 4, "u", False)
                for ns in range(2):
                    pg = psA[:, (jc % 2) * 1024 + ns * 512:
                             (jc % 2) * 1024 + ns * 512 + 512]
                    pu = psB[:, (jc % 2) * 1024 + ns * 512:
                             (jc % 2) * 1024 + ns * 512 + 512]
                    sil_w = [("pe", gate_pe[ns]), ("dve", st["mul_last"][ns])]
                    if jc == 0:
                        sil_w += [("dve", v) for v in st["d_dve"].values()]
                    C.act(lambda e, pg=pg, ns=ns: e.activation(
                        ar["sg"][:, ns * 512:(ns + 1) * 512], pg,
                        AF.Silu, bias=0.0, scale=1.0), wait=tuple(sil_w))
                    sg_a = C.c("act")
                    st["g_act"][(jc % 2, ns)] = sg_a
                    C.dve(lambda e, jc=jc, pu=pu, ns=ns: e.tensor_mul(
                        ar["hbuf"][:, jc * 1024 + ns * 512:
                                   jc * 1024 + ns * 512 + 512], pu,
                        ar["sg"][:, ns * 512:(ns + 1) * 512]),
                        wait=(("act", sg_a), ("pe", up_pe),
                              ("pe", st["down_pe"])), inc_n=1)
                    mul_d = C.c("dve")
                    st["u_dve"][(jc % 2, ns)] = mul_d
                    st["mul_last"][ns] = mul_d
            h_done = C.c("dve")
            if ei + 1 < len(exps):
                prefill(exps[ei + 1])
            dq_war = 0
            for q4 in range(4):
                w = [("dve", h_done)] if q4 == 0 else []
                if dq_war:
                    w.append(("pe", dq_war))
                if q4 == 0:
                    w += [ffn_w, ("st_wrow", wrow_st)]
                C.dma("scalar", ar["dquart"][:, :],
                      wst["downs"].ap()[eidx, q4],
                      wait=tuple(w), sem="q_dh")
                d_ld = C.c("q_dh")
                for dmi2 in range(2):
                    dm = q4 * 2 + dmi2
                    for ns in range(2):
                        pdp = psA[:, (dm % 2) * 1024 + ns * 512:
                                  (dm % 2) * 1024 + ns * 512 + 512]
                        for jc in range(JCN):
                            C.pe(lambda e, jc=jc, dmi2=dmi2, pdp=pdp, ns=ns: e.matmul(
                                pdp, ar["dquart"][:, jc * 256 + dmi2 * 128:
                                                  jc * 256 + dmi2 * 128 + 128],
                                ar["hbuf"][:, jc * 1024 + ns * 512:
                                           jc * 1024 + ns * 512 + 512],
                                start=(jc == 0), stop=(jc == JCN - 1)),
                                wait=(("q_dh", d_ld), ("dve", h_done),
                                      ("dve", st["d_dve"].get((dm % 2, ns), 0)),
                                      ("act", st["g_act"].get((dm % 2, ns), 0)))
                                if jc == 0 else (),
                                inc_n=1 if jc == JCN - 1 else 0)
                        dpe = C.c("pe")
                        xsl = x[:, dm * TOWN + tb + ns * 512:
                                dm * TOWN + tb + ns * 512 + 512]
                        if eidx == 0:
                            C.dve(lambda e, xsl=xsl, pdp=pdp: e.tensor_add(
                                xsl, xsl, pdp), wait=(("pe", dpe),), inc_n=1)
                        else:
                            C.dve(lambda e, pdp=pdp, ns=ns: e.tensor_mul(
                                ar["fes"][:, 0:512], pdp,
                                ar["wbb"][:, ns * 512:(ns + 1) * 512]),
                                wait=(("pe", dpe), ("q_wbb", wb_ld)))
                            C.dve(lambda e, xsl=xsl: e.tensor_add(
                                xsl, xsl, ar["fes"][:, 0:512]), inc_n=1)
                        st["d_dve"][(dm % 2, ns)] = C.c("dve")
                dq_war = C.c("pe")
            st["down_pe"] = C.c("pe")
        M.pop("ffn_state")
    emit_ffn(0)
    if DEBUG_STOP == 2:
        emit_output(C, M)
        return

    phase_sync()
    C.scope("B0_ln")
    xn_done = emit_ln(x, ar["xno"], affine=False)
    xnf_ld = emit_ag(ar["xno"], xn_done)
    C.dma("sync", ar["wproj"][0:64, :].rearrange("p (hh m) -> p hh m", hh=HN),
          wst["projT"].ap().rearrange("hh p m -> p hh m"),
          wait=(attn_w, ("dve", xn_done), ("pe", C.c("pe"))), sem="q_wproj")
    C.dma("sync", ar["wproj"][64:128, :].rearrange("p (hh m) -> p hh m", hh=HN),
          wst["projT"].ap().rearrange("hh p m -> p hh m"), sem="q_wproj")
    wproj_ld = C.c("q_wproj")

    def wqkv_load(hh, guard_pe):
        for i, wname in enumerate(["wqT", "wkT", "wvT"]):
            C.dma("sync", ar["wqkv3"][:, i * 512:(i + 1) * 512],
                  wst[wname].ap()[hh],
                  wait=(("pe", guard_pe), attn_w) if i == 0 else (), sem="q_wqkv")
        return C.c("q_wqkv")

    mg_prev = 0
    for hh in range(HN):
        C.scope("B1_qkv")
        wq_ld = wqkv_load(hh, C.c("pe"))
        qk_marks = []
        for i in range(2):
            for nt2 in range(8):
                ps = psA[0:DK, (nt2 % 4) * 512:(nt2 % 4) * 512 + 512]
                idx = i * 8 + nt2
                w = []
                if idx == 0:
                    w = [("q_wqkv", wq_ld), ("q_xnf", xnf_ld), ("dve", mg_prev)]
                elif idx >= 4:
                    w = [("dve", qk_marks[idx - 4])]
                for kc in range(KC):
                    C.pe(lambda e, i=i, nt2=nt2, kc=kc, ps=ps: e.matmul(
                        ps, ar["wqkv3"][:, i * 512 + kc * DK: i * 512 + (kc + 1) * DK],
                        xnf[:, kc * T + nt2 * 512: kc * T + nt2 * 512 + 512],
                        start=(kc == 0), stop=(kc == KC - 1)),
                        wait=tuple(w) if kc == 0 else (),
                        inc_n=1 if kc == KC - 1 else 0)
                qpe = C.c("pe")
                dst = (ar["qkh"][0:64, nt2 * 512:(nt2 + 1) * 512] if i == 0
                       else ar["qkh"][64:128, nt2 * 512:(nt2 + 1) * 512])
                C.dve(lambda e, dst=dst, ps=ps: e.tensor_scalar_mul(dst, ps, 0.125),
                      wait=(("pe", qpe),), inc_n=1)
                qk_marks.append(C.c("dve"))
        qk_d = C.c("dve")
        if DEBUG_STOP == 32 and hh == 0:
            emit_output(C, M)
            return
        C.scope("B2_v")
        v_marks = []
        for tc in range(32):
            ps = psB[:, 1024 + (tc % 2) * 512: 1024 + (tc % 2) * 512 + 64]
            w = [("dve", qk_marks[12])] if tc == 0 else (
                [("dve", v_marks[tc - 2])] if tc >= 2 else [])
            for kc in range(KC):
                C.pe(lambda e, tc=tc, kc=kc, ps=ps: e.matmul(
                    ps, xnf[:, kc * T + tc * 128: kc * T + tc * 128 + 128],
                    ar["wqkv3"][:, 2 * 512 + kc * DK: 2 * 512 + (kc + 1) * DK],
                    start=(kc == 0), stop=(kc == KC - 1)),
                    wait=tuple(w) if kc == 0 else (), inc_n=1 if kc == KC - 1 else 0)
            vpe = C.c("pe")
            C.dve(lambda e, tc=tc, ps=ps: e.tensor_copy(
                ar["vb_h"][:, tc * 65: tc * 65 + 64], ps),
                wait=(("pe", vpe),), inc_n=1)
            v_marks.append(C.c("dve"))
        C.dve(lambda e: e.memset(
            ar["vb_h"][:, :].rearrange("p (a b) -> p a b", b=65)[:, :, 64:65], 1.0),
            inc_n=1)
        v_d = C.c("dve")
        if DEBUG_STOP == 33 and hh == 0:
            emit_output(C, M)
            return
        C.scope("B3_phk")
        phk_marks = {}
        for tc in range(32):
            ps = psA[:, 1024 + (tc % 2) * 512: 1024 + (tc % 2) * 512 + MF]
            w = [("dve", qk_d)] if tc == 0 else (
                [("dve", phk_marks[tc - 2])] if tc >= 2 else [])
            C.pe(lambda e, tc=tc, hh=hh, ps=ps: e.matmul(
                ps, ar["qkh"][64:128, tc * 128:(tc + 1) * 128],
                ar["wproj"][64:128, hh * MF:(hh + 1) * MF],
                start=True, stop=True),
                wait=tuple(w) + ((("q_wproj", wproj_ld),) if (hh == 0 and tc == 0) else ()),
                inc_n=1)
            xw_pe = C.c("pe")
            C.dve(lambda e, ps=ps: e.reduce_max(sb["colscr"][:, 0:1], ps, AX),
                  wait=(("pe", xw_pe),), inc_n=1)
            C.act(lambda e, tc=tc: e.activation(
                ar["sqs"][:, :], ar["qkh"][64:128, tc * 128:(tc + 1) * 128],
                AF.Square, bias=0.0, scale=1.0),
                wait=(("dve", qk_d), ("pe", C.c("pe"))))
            sq_a = C.c("act")
            C.pe(lambda e, tc=tc: e.transpose(
                psB[:, (tc % 2) * 512:(tc % 2) * 512 + 64], ar["sqs"][:, :],
                sb["ident"][0:64, 0:64]), wait=(("act", sq_a),), inc_n=1)
            tq_pe = C.c("pe")
            C.dve(lambda e, tc=tc: e.reduce_sum(
                sb["colscr"][:, 1:2], psB[:, (tc % 2) * 512:(tc % 2) * 512 + 64], AX),
                wait=(("pe", tq_pe),))
            C.dve(lambda e: e.tensor_scalar(sb["colscr"][:, 1:2],
                                            sb["colscr"][:, 1:2], -0.5, -LN16,
                                            ALU.mult, ALU.add))
            C.dve(lambda e: e.tensor_sub(sb["colscr"][:, 1:2], sb["colscr"][:, 1:2],
                                         sb["colscr"][:, 0:1]), inc_n=1)
            bc_d = C.c("dve")
            C.act(lambda e, ps=ps: e.activation(
                ar["bes2"][:, :], ps, AF.Exp, bias=sb["colscr"][:, 1:2], scale=1.0),
                wait=(("dve", bc_d),))
            ph_a = C.c("act")
            C.dve(lambda e: e.tensor_scalar_add(ar["phk16"][:, :], ar["bes2"][:, :],
                                                EPSB / 16.0),
                  wait=(("act", ph_a),), inc_n=1)
            phk_d = C.c("dve")
            phk_marks[tc] = phk_d
            for mc in range(2):
                C.pe(lambda e, tc=tc, mc=mc: e.matmul(
                    psB[:, 1024 + mc * 512: 1024 + mc * 512 + 65],
                    ar["phk16"][:, mc * 128:(mc + 1) * 128],
                    ar["vb_h"][:, tc * 65:(tc + 1) * 65],
                    start=(tc == 0), stop=(tc == 31)),
                    wait=(("dve", max(phk_d, v_d)),) if mc == 0 else (),
                    inc_n=1 if (tc == 31 and mc == 1) else 0)
        kv_pe = C.c("pe")
        C.dve(lambda e: e.tensor_copy(ar["kvh"][:, 0:65], psB[:, 1024:1089]),
              wait=(("pe", kv_pe),))
        C.dve(lambda e: e.tensor_copy(ar["kvh"][:, 65:130], psB[:, 1536:1601]))
        C.dve(lambda e: e.tensor_copy(ar["kvh16"][:, :], ar["kvh"][:, :]), inc_n=1)
        kv16_d = C.c("dve")
        if DEBUG_STOP == 34 and hh == 0:
            emit_output(C, M)
            return
        C.scope("B4_phq1")
        q1_marks = {}
        for tc in range(32):
            ps = psA[:, 1024 + (tc % 2) * 512: 1024 + (tc % 2) * 512 + MF]
            w = [("dve", kv16_d)] if tc == 0 else (
                [("dve", q1_marks[tc - 2])] if tc >= 2 else [])
            C.pe(lambda e, tc=tc, hh=hh, ps=ps: e.matmul(
                ps, ar["qkh"][0:64, tc * 128:(tc + 1) * 128],
                ar["wproj"][0:64, hh * MF:(hh + 1) * MF],
                start=True, stop=True), wait=tuple(w), inc_n=1)
            xw_pe = C.c("pe")
            C.dve(lambda e, ps=ps: e.reduce_max(sb["colscr"][:, 0:1], ps, AX),
                  wait=(("pe", xw_pe),), inc_n=1)
            C.act(lambda e, tc=tc: e.activation(
                ar["sqs"][:, :], ar["qkh"][0:64, tc * 128:(tc + 1) * 128],
                AF.Square, bias=0.0, scale=1.0), wait=(("pe", C.c("pe")),))
            sq_a = C.c("act")
            C.pe(lambda e, tc=tc: e.transpose(
                psB[:, (tc % 2) * 512:(tc % 2) * 512 + 64], ar["sqs"][:, :],
                sb["ident"][0:64, 0:64]), wait=(("act", sq_a),), inc_n=1)
            tq_pe = C.c("pe")
            C.dve(lambda e, tc=tc: e.reduce_sum(
                sb["colscr"][:, 1:2], psB[:, (tc % 2) * 512:(tc % 2) * 512 + 64], AX),
                wait=(("pe", tq_pe),))
            C.dve(lambda e: e.tensor_scalar(sb["colscr"][:, 1:2],
                                            sb["colscr"][:, 1:2], -0.5, -LN16,
                                            ALU.mult, ALU.add))
            C.dve(lambda e: e.tensor_sub(sb["colscr"][:, 1:2], sb["colscr"][:, 1:2],
                                         sb["colscr"][:, 0:1]), inc_n=1)
            bq_d = C.c("dve")
            if DEBUG_STOP == 38:
                q1_marks[tc] = C.c("dve")
                continue
            C.pe(lambda e, tc=tc: e.transpose(
                psB[0:1, 1024 + (tc % 2) * 512: 1024 + (tc % 2) * 512 + 128],
                sb["colscr"][:, 1:2], sb["ident"][:, :]),
                wait=(("dve", bq_d),), inc_n=1)
            tb_pe = C.c("pe")
            if DEBUG_STOP == 39:
                q1_marks[tc] = C.c("dve")
                continue
            C.dve(lambda e, tc=tc: e.tensor_copy(
                sb["rowp16"][0:1, tc * 128:(tc + 1) * 128],
                psB[0:1, 1024 + (tc % 2) * 512: 1024 + (tc % 2) * 512 + 128]),
                wait=(("pe", tb_pe),))
            C.dve(lambda e, tc=tc: e.tensor_copy(
                sb["rowpf"][0:1, 0:128],
                sb["rowp16"][0:1, tc * 128:(tc + 1) * 128]))
            if DEBUG_STOP == 40:
                q1_marks[tc] = C.c("dve")
                continue
            C.dve(lambda e, tc=tc: e.tensor_sub(
                sb["rowpf"][0:1, 128:256],
                psB[0:1, 1024 + (tc % 2) * 512: 1024 + (tc % 2) * 512 + 128],
                sb["rowpf"][0:1, 0:128]))
            if DEBUG_STOP == 41:
                q1_marks[tc] = C.c("dve")
                continue
            C.dve(lambda e, tc=tc: e.tensor_copy(
                sb["rowp16"][32:33, tc * 128:(tc + 1) * 128],
                sb["rowpf"][0:1, 128:256]), inc_n=1)
            q1_marks[tc] = C.c("dve")
        brq_d = C.c("dve")
        if DEBUG_STOP in (37, 38, 39, 40, 41, 42, 43) and hh == 0:
            emit_output(C, M)
            return
        C.dma("gpsimd", M["brql"].ap(), sb["rowp16"][32:33, :],
              wait=(("dve", brq_d),), sem="st_nm")
        blq = C.c("st_nm")
        C.dma("sync", sb["rowp16"][1:2, :], APflat(M["brql"], 0, T),
              wait=(("st_nm", blq),), sem="q_nm")
        brq_nm = C.c("q_nm")
        if DEBUG_STOP == 35 and hh == 0:
            emit_output(C, M)
            return
        C.scope("B5_phq2")
        nd_marks = {}
        for nt2 in range(8):
            for mc in range(2):
                ps = psA[0:128, mc * 512: mc * 512 + 512]
                w = [("dve", brq_d), ("q_nm", brq_nm)] if (nt2 == 0 and mc == 0) else []
                if nt2 >= 1:
                    w.append(("dve", nd_marks.get(nt2 - 1, 0)))
                C.pe(lambda e, nt2=nt2, mc=mc, hh=hh, ps=ps: e.matmul(
                    ps, ar["wproj"][0:64, hh * MF + mc * 128: hh * MF + mc * 128 + 128],
                    ar["qkh"][0:64, nt2 * 512:(nt2 + 1) * 512],
                    start=True, stop=False), wait=tuple(w))
                C.pe(lambda e, nt2=nt2, ps=ps: e.matmul(
                    ps, sb["ones16m"][0:2, :],
                    sb["rowp16"][0:2, nt2 * 512:(nt2 + 1) * 512],
                    start=False, stop=True), inc_n=1)
                p2_pe = C.c("pe")
                C.act(lambda e, mc=mc, ps=ps: e.activation(
                    ar["bes2"][:, 0:MF] if False else ar["lnes"][:, mc * 512:(mc + 1) * 512],
                    ps, AF.Exp, bias=0.0, scale=1.0), wait=(("pe", p2_pe),))
                e_a = C.c("act")
                C.dve(lambda e, mc=mc: e.tensor_scalar_add(
                    ar["phq"][:, mc * 512:(mc + 1) * 512],
                    ar["lnes"][:, mc * 512:(mc + 1) * 512], EPSB / 16.0),
                    wait=(("act", e_a),), inc_n=1)
            phq_d = C.c("dve")
            for mc in range(2):
                C.pe(lambda e, mc=mc: e.matmul(
                    psB[0:65, 1024:1536], ar["kvh16"][:, mc * 65:(mc + 1) * 65],
                    ar["phq"][:, mc * 512:(mc + 1) * 512],
                    start=(mc == 0), stop=(mc == 1)),
                    wait=(("dve", phq_d),) if mc == 0 else (),
                    inc_n=1 if mc == 1 else 0)
            nd_pe = C.c("pe")
            C.dve(lambda e: e.tensor_scalar_add(sb["rowpf"][0:1, 1024:1536],
                                                psB[64:65, 1024:1536], EPSB),
                  wait=(("pe", nd_pe),))
            C.dve(lambda e: e.reciprocal(sb["rowpf"][0:1, 1536:2048],
                                         sb["rowpf"][0:1, 1024:1536]), inc_n=1)
            nd_d = C.c("dve")
            C.pe(lambda e: e.matmul(psB[0:64, 0:512], sb["ones_row64"][:, :],
                                    sb["rowpf"][0:1, 1536:2048],
                                    start=True, stop=True),
                 wait=(("dve", nd_d),), inc_n=1)
            ib_pe = C.c("pe")
            C.dve(lambda e: e.tensor_copy(ar["invb"][:, :], psB[0:64, 0:512]),
                  wait=(("pe", ib_pe),))
            C.dve(lambda e: e.tensor_mul(ar["mgt"][0:64, :], psB[0:64, 1024:1536],
                                         ar["invb"][:, :]),
                  wait=(("st_mg", C.c("st_mg")),), inc_n=1)
            mg_d = C.c("dve")
            nd_marks[nt2] = mg_d
            if DEBUG_STOP == 36 and hh == 0 and nt2 == 0:
                emit_output(C, M)
                return
            C.dma("gpsimd",
                  M["mgst"].ap()[hh // 2][(hh % 2) * 64:(hh % 2) * 64 + 64,
                                          nt2 * 512:(nt2 + 1) * 512],
                  ar["mgt"][0:64, :], wait=(("dve", mg_d),), sem="st_mg")
        mg_prev = C.c("dve")
    mg_store = C.c("st_mg")
    phase_sync()
    C.scope("B6_wo")
    C.dma("sync", ar["wout_s"][:, 0:4 * D].rearrange("p (k c) -> p k c", k=4),
          wst["woT"].ap().rearrange("k p c -> p k c"),
          wait=(attn_w, ("pe", C.c("pe"))), sem="q_wo2")
    wo2_ld = C.c("q_wo2")
    ev_prev = 0
    for nt in range(8):
        for kc2 in range(4):
            C.dma("sync", ar["gld"][:, kc2 * 512:(kc2 + 1) * 512],
                  M["mgst"].ap()[kc2][:, nt * 512:(nt + 1) * 512],
                  wait=((("st_mg", mg_store), ("pe", C.c("pe"))) if kc2 == 0
                        else ()), sem="q_gld")
        ml = C.c("q_gld")
        evd = {}
        for dm in range(KC):
            ps = psA[:, (dm % 4) * 512:(dm % 4) * 512 + 512]
            for kc2 in range(4):
                C.pe(lambda e, dm=dm, kc2=kc2, ps=ps: e.matmul(
                    ps, ar["wout_s"][:, kc2 * D + dm * 128: kc2 * D + dm * 128 + 128],
                    ar["gld"][:, kc2 * 512:(kc2 + 1) * 512],
                    start=(kc2 == 0), stop=(kc2 == 3)),
                    wait=(("q_gld", ml), ("q_wo2", wo2_ld),
                          ("dve", evd.get(dm - 4, ev_prev)))
                    if kc2 == 0 else (), inc_n=1 if kc2 == 3 else 0)
            wpe = C.c("pe")
            C.dve(lambda e, dm=dm, ps=ps: e.tensor_copy(
                ar["aotile"][:, dm * 512:(dm + 1) * 512], ps),
                wait=(("pe", wpe), ("st_ao", C.c("st_ao"))), inc_n=1)
            evd[dm] = C.c("dve")
        ev_prev = C.c("dve")
        for dm in range(KC):
            C.dma("gpsimd",
                  M["rs_in"].ap()[nt // 4, dm][:, (nt % 4) * 512:(nt % 4) * 512 + 512],
                  ar["aotile"][:, dm * 512:(dm + 1) * 512],
                  wait=(("dve", ev_prev),) if dm == 0 else (), sem="st_ao")
    rs_w = C.c("st_ao")
    C.op("gpsimd", lambda e: e.collective_compute(
        "ReduceScatter", ALU.add, replica_groups=M["groups"],
        ins=[M["rs_in"].ap().opt()], outs=[M["rs_out"].ap().opt()]),
        wait=(("st_ao", rs_w),), inc=(("cc", 1),))
    rs2_cc = C.c("cc")
    for kc in range(KC):
        sl = slice(kc * TOWN, (kc + 1) * TOWN)
        C.dma("sync", ar["lnes"][:, :], M["rs_out"].ap()[kc],
              wait=((("cc", rs2_cc), ("dve", C.c("dve")))
                    if kc == 0 else (("dve", C.c("dve")),)), sem="q_ao3")
        al = C.c("q_ao3")
        C.dve(lambda e, sl=sl: e.tensor_add(x[:, sl], x[:, sl], ar["lnes"][:, :]),
              wait=(("q_ao3", al),))
        C.dve(lambda e, sl=sl, kc=kc: e.tensor_scalar_add(
            x[:, sl], x[:, sl], sb["bo"][:, kc:kc + 1]), inc_n=1)
    if DEBUG_STOP == 3:
        emit_output(C, M)
        return

    emit_ffn(1)
    emit_output(C, M)


def emit_output(C, M):
    x = M["sb"]["x"]
    w = (("dve", C.c("dve")), ("pe", C.c("pe")), ("act", C.c("act")))
    for kc in range(KC):
        C.dma("gpsimd", M["out_fm"].ap()[kc], x[:, kc * TOWN:(kc + 1) * TOWN],
              wait=w if kc == 0 else (), sem="st_out")
    fin = C.c("st_out")
    C.op("gpsimd", lambda e: e.memset(M["sb"]["colscr"][:, 5:6], 0.0),
         wait=(("st_out", fin),), inc=(("pool", 1),))



def kernel(**inputs):
    f32 = np.float32
    x_full = np.asarray(inputs["L0_attn_a_x"], f32)
    W_in = np.asarray(inputs["L0_attn_a_W_in"], f32)
    b_in = np.asarray(inputs["L0_attn_a_b_in"], f32)
    in_ln_w = np.asarray(inputs["L0_attn_a_in_ln_w"], f32)
    in_ln_b = np.asarray(inputs["L0_attn_a_in_ln_b"], f32)
    W_out = np.asarray(inputs["L0_attn_a_W_out"], f32)
    out_ln_w = np.asarray(inputs["L0_attn_a_out_ln_w"], f32)
    out_ln_b = np.asarray(inputs["L0_attn_a_out_ln_b"], f32)
    smear = np.asarray(inputs["L0_attn_a_smear_factor"], f32)
    lscale = np.asarray(inputs["L0_attn_a_log_scale"], f32)
    Wq = np.asarray(inputs["L0_attn_b_Wq"], f32)
    Wk = np.asarray(inputs["L0_attn_b_Wk"], f32)
    Wv = np.asarray(inputs["L0_attn_b_Wv"], f32)
    Wo = np.asarray(inputs["L0_attn_b_Wo"], f32)
    bo = np.asarray(inputs["L0_attn_b_bo"], f32)
    projw = np.asarray(inputs["L0_attn_b_proj"], f32)
    sg_ = np.asarray(inputs["L0_ffn_shared_gate"], f32)
    su_ = np.asarray(inputs["L0_ffn_shared_up"], f32)
    sd_ = np.asarray(inputs["L0_ffn_shared_down"], f32)
    rg_ = np.asarray(inputs["L0_ffn_routed_gate"], f32)
    ru_ = np.asarray(inputs["L0_ffn_routed_up"], f32)
    rd_ = np.asarray(inputs["L0_ffn_routed_down"], f32)
    Wr = np.asarray(inputs["L0_ffn_Wr"], f32)
    rbias = np.asarray(inputs["L0_ffn_routing_bias"], f32)

    import ml_dtypes
    ac = np.ascontiguousarray
    WinT = W_in.T
    g8 = np.concatenate([sg_[None], rg_], 0)
    u8 = np.concatenate([su_[None], ru_], 0)
    d8 = np.concatenate([sd_[None], rd_], 0)
    gates = ac(g8.reshape(NEXP, KC, 128, JCN, 128).transpose(0, 3, 2, 1, 4)
               .reshape(NEXP, JCN, 128, 1024))
    ups = ac(u8.reshape(NEXP, KC, 128, JCN, 128).transpose(0, 3, 2, 1, 4)
             .reshape(NEXP, JCN, 128, 1024))
    downs = ac(d8.reshape(NEXP, JCN, 128, 4, 256).transpose(0, 3, 2, 1, 4)
               .reshape(NEXP, 4, 128, JCN * 256))

    ident = np.eye(128, dtype=f32)
    ones_col16 = np.ones((128, 1), dtype=ml_dtypes.bfloat16)
    ones_row16 = np.ones((1, 128), dtype=ml_dtypes.bfloat16)
    ones16m = np.ones((128, 128), dtype=ml_dtypes.bfloat16)
    ones_col64_16 = np.ones((64, 1), dtype=ml_dtypes.bfloat16)
    ones_row64 = np.ones((1, 64), dtype=f32)
    ones_col32 = np.ones((128, 1), dtype=f32)
    tri_loc = np.zeros((128, 129), dtype=f32)
    tri_loc[:, :128] = np.tril(np.ones((128, 128), f32)).T
    tri_loc[:, 128] = 1.0
    tri_off = np.triu(np.ones((32, 32), f32), k=1)
    NEGB = np.float32(-1e9)
    iu = np.arange(896)[None, :]
    ip = np.arange(128)[:, None]
    mask1m = np.where(iu > 384 + ip, NEGB, 0).astype(f32)
    mask2m = np.where(ip > iu - 384, NEGB, 0).astype(f32)

    in_maps = []
    for c in range(8):
        b, r = c // 2, c % 2
        ysl = slice(8192 + 8 * r, 8192 + 8 * r + 8)
        wmain = np.stack(
            [WinT[:, blk * 2048 + 1024 * r: blk * 2048 + 1024 * r + 1024]
             for blk in range(4)], axis=0)
        win_t = np.zeros((33, 128, 1024), f32)
        win_t[:32] = (wmain.reshape(4, KC, 128, HN, 128)
                      .transpose(3, 0, 2, 1, 4).reshape(32, 128, 1024))
        win_t[32, :, :64] = (WinT[:, ysl].reshape(KC, 128, HN)
                             .transpose(1, 0, 2).reshape(128, 64))
        bqkp = np.stack([
            b_in[0 * 2048 + 1024 * r: 0 * 2048 + 1024 * r + 1024].reshape(8, 128),
            b_in[1 * 2048 + 1024 * r: 1 * 2048 + 1024 * r + 1024].reshape(8, 128),
            b_in[3 * 2048 + 1024 * r: 3 * 2048 + 1024 * r + 1024].reshape(8, 128),
        ], axis=0)
        bqkp = ac(bqkp.reshape(24, 128).T)
        m = dict(
            xT=ac(x_full[b, r * TOWN:(r + 1) * TOWN, :].T.reshape(KC, 128, TOWN)),
            win=ac(win_t),
            bqkp=bqkp,
            bvrow=ac(b_in[2 * 2048 + 1024 * r: 2 * 2048 + 1024 * r + 1024].reshape(8, 128)),
            byrow=ac(b_in[ysl][None, :]),
            lnw=ac(in_ln_w.reshape(KC, 128).T), lnb=ac(in_ln_b.reshape(KC, 128).T),
            olnw=ac(out_ln_w.reshape(KC, 128).T), olnb=ac(out_ln_b.reshape(KC, 128).T),
            woutT=ac(W_out.T[1024 * r:1024 * r + 1024].reshape(KC, 128, D)),
            smear=ac(smear[8 * r:8 * r + 8][None, :]),
            lscale=ac(lscale[8 * r:8 * r + 8][None, :]),
            wqT=ac(Wq.T[:, 512 * r:512 * r + 512].reshape(KC, 128, HN, DK)
                   .transpose(2, 1, 0, 3).reshape(HN, 128, KC * DK)),
            wkT=ac(Wk.T[:, 512 * r:512 * r + 512].reshape(KC, 128, HN, DK)
                   .transpose(2, 1, 0, 3).reshape(HN, 128, KC * DK)),
            wvT=ac(Wv.T[:, 512 * r:512 * r + 512].reshape(KC, 128, HN, DK)
                   .transpose(2, 1, 0, 3).reshape(HN, 128, KC * DK)),
            woT=ac(Wo.T[512 * r:512 * r + 512].reshape(4, 128, D)),
            bo=ac(bo.reshape(KC, 128).T),
            projT=ac(projw[8 * r:8 * r + 8].transpose(0, 2, 1)),
            gates=gates, ups=ups, downs=downs,
            wr=ac(Wr.reshape(KC, 128, 7)), rbias=ac(rbias[None, :]),
            ident=ident, ones_col16=ones_col16, ones_row16=ones_row16,
            ones16m=ones16m,
            ones_col64_16=ones_col64_16, ones_row64=ones_row64,
            ones_col32=ones_col32, tri_loc=tri_loc, tri_off=tri_off,
            mask1m=mask1m, mask2m=mask2m,
        )
        in_maps.append(m)

    nc = build_program()
    res = run_bass_kernel_spmd(nc, in_maps, core_ids=list(range(8)))
    out = np.empty((4, T, D), dtype=np.float32)
    for c in range(8):
        b, r = c // 2, c % 2
        o = res.results[c]["out_fm"]
        out[b, r * TOWN:(r + 1) * TOWN, :] = o.transpose(2, 0, 1).reshape(TOWN, D)
    return out


if __name__ == "__main__":
    import time
    t0 = time.time()
    nc = build_program()
    print(f"build {time.time()-t0:.1f}s n_inst={nc.n_instructions()}")
